# version 02
# baseline (speedup 1.0000x reference)
import numpy as np
import ml_dtypes
from contextlib import ExitStack

import concourse.bass as bass
import concourse.mybir as mybir

F32 = mybir.dt.float32
BF16 = mybir.dt.bfloat16
AF = mybir.ActivationFunctionType
ALU = mybir.AluOpType
AXX = mybir.AxisListType.X

D, N, NH, DH, NCH, NTOK, FF = 1024, 2048, 16, 64, 8, 16, 4096
HEAD_ROPE = [(h % 4) in (2, 3) for h in range(NH)]
HEAD_DECAY = [(h % 4) in (1, 3) for h in range(NH)]


class Sem:
    def __init__(self, handle):
        self.h = handle
        self.v = 0

    def inc(self, instr, n=1):
        instr.then_inc(self.h, n)
        self.v += n
        return (self, self.v)

    def tok(self):
        return (self, self.v)


def wait(eng, *toks):
    best = {}
    for tk in toks:
        if tk is None:
            continue
        s, v = tk
        if v > best.get(id(s), (s, -1))[1]:
            best[id(s)] = (s, v)
    for s, v in best.values():
        if v > 0:
            eng.wait_ge(s.h, v)


def ap(t, poff, pcnt, fdims, off=0):
    prow = int(t.shape[1])
    return bass.AP(t, poff * prow + off, [[prow, pcnt]] + [list(d) for d in fdims])


class B:
    def __init__(self, bl):
        self.bl = bl

    def t(self, f):
        self.bl.tensor(f)

    def v(self, f):
        self.bl.vector(f)

    def s(self, f):
        self.bl.scalar(f)

    def g(self, f):
        self.bl.gpsimd(f)

    def y(self, f):
        self.bl.sync(f)


class MK:
    def __init__(self, nc):
        self.nc = nc
        self.es = ExitStack()
        self._names = {}

    def name(self, base):
        i = self._names.get(base, 0)
        self._names[base] = i + 1
        return f"{base}_{i}"

    def sem(self, name=None):
        return Sem(self.es.enter_context(self.nc.semaphore(self.name(name or "s"))))

    def sb(self, name, shape, dtype, scope=None):
        return (scope or self.es).enter_context(self.nc.sbuf_tensor(self.name(name), shape, dtype))

    def ps(self, name, shape, dtype=F32):
        return self.es.enter_context(self.nc.psum_tensor(self.name(name), shape, dtype))

    def block(self, name):
        if getattr(self, "_blk", None) is not None:
            self._blk.close()
        self._blk = ExitStack()
        return B(self._blk.enter_context(self.nc.Block(self.name(name), no_gpsimd_drain=True)))

    def end_blocks(self):
        if getattr(self, "_blk", None) is not None:
            self._blk.close()
            self._blk = None



class PsumPool:

    def __init__(self, mk, n=4):
        self.slots = [mk.ps("pool", [128, 512]) for _ in range(n)]
        self.freed = [(i, None) for i in range(n)]
        self.held = {}

    def acquire(self):
        if not self.freed:
            raise RuntimeError("psum pool exhausted")
        i, tok = self.freed.pop(0)
        return i, self.slots[i], tok

    def release(self, i, consumer_tok):
        self.freed.append((i, consumer_tok))


def host_constants():
    c = {}
    c["ones_row"] = np.ones((1, 128), dtype=ml_dtypes.bfloat16)
    c["onescol_f32"] = np.ones((1, 128), dtype=np.float32)
    c["ones_colbf"] = np.ones((128, 1), dtype=ml_dtypes.bfloat16)
    c["identf32"] = np.eye(128, dtype=np.float32)
    q = np.arange(128)
    c["tri"] = (q[None, :] >= q[:, None]).astype(ml_dtypes.bfloat16)
    pos = np.arange(N, dtype=np.float64)
    inv = 10000.0 ** (-(np.arange(0, DH, 2, dtype=np.float64) / DH))
    th = pos[None, :] * inv[:, None]
    cos, sin = np.cos(th), np.sin(th)
    C2 = np.empty((128, N), dtype=np.float64)
    S2 = np.empty((128, N), dtype=np.float64)
    for blk in (0, 64):
        C2[blk + 0:blk + 64:2] = cos
        C2[blk + 1:blk + 64:2] = cos
        S2[blk + 0:blk + 64:2] = -sin
        S2[blk + 1:blk + 64:2] = sin
    c["C2"] = C2.astype(ml_dtypes.bfloat16)
    c["S2"] = S2.astype(ml_dtypes.bfloat16)
    E = np.zeros((128, 128), dtype=np.float32)
    for ch in range(8):
        E[0:64, ch * 16 + 2 * ch] = 1.0
        E[64:128, ch * 16 + 2 * ch + 1] = 1.0
    c["E_pack"] = E.astype(ml_dtypes.bfloat16)
    c["erow"] = (np.arange(NH) % 2).astype(np.float32).reshape(1, NH)
    c["ones128"] = np.ones((128, 128), dtype=ml_dtypes.bfloat16)
    c["identbf"] = np.eye(128, dtype=ml_dtypes.bfloat16)
    return c


def seg_list(k0):
    segs = [(k0, k0 + 128, True)]
    b = k0 + 128
    while b < N:
        e = min(((b // 512) + 1) * 512, N)
        segs.append((b, e, False))
        b = e
    return segs


class Build:
    def __init__(self, upto="l0"):
        self.nc = bass.Bass("TRN2", target_bir_lowering=False,
                            detect_race_conditions=False)
        self.mk = MK(self.nc)
        self.upto = upto

    def inp(self, name, shape, dtype):
        return self.nc.dram_tensor(name, shape, dtype, kind="ExternalInput")

    def reg_dbg(self, name, t):
        if not getattr(self, "want_dbg", False):
            return
        if not hasattr(self, "dbg"):
            self.dbg = {}
        self.dbg[name] = t

    def dump_dbg(self):
        if not getattr(self, "dbg", None):
            return
        b = self.mk.block("dbg")

        def y(sy):
            wait(sy, self.sT.tok(), self.sV.tok(), self.sS.tok(), self.sG.tok())
            last = None
            for nm, t in self.dbg.items():
                shp = [int(s) for s in t.shape]
                dt = t.dtype
                o = self.nc.dram_tensor("dbg_" + nm, shp, dt, kind="ExternalOutput")
                last = self.sO.inc(sy.dma_start(out=o.ap()[:, :], in_=t[:, :]), 16)
            wait(sy, last)
        b.y(y)

    def declare(self):
        mk = self.mk
        self.x_in = self.inp("x", [D, N], F32)
        self.wqkv_in = self.inp("wqkv", [D, 3 * D], F32)
        self.wo_in = self.inp("wo", [D, D], F32)
        self.logxi_in = self.inp("logxi", [1, NH], F32)
        self.wfc_in = self.inp("wfc", [D, FF], F32)
        self.wpr_in = self.inp("wpr", [FF, D], F32)
        self.wq1_in = self.inp("wq1", [D, D], F32)
        self.wk1_in = self.inp("wk1", [D, D], F32)
        self.wv1_in = self.inp("wv1", [D, D], F32)
        self.slotq_in = self.inp("slotq", [D, 16], F32)
        self.dwq_in = self.inp("dwq", [D, D], F32)
        self.dwk_in = self.inp("dwk", [D, D], F32)
        self.dwv_in = self.inp("dwv", [D, D], F32)
        self.dwo_in = self.inp("dwo", [D, D], F32)
        self.dbq_in = self.inp("dbq", [128, 8], F32)
        self.dbk_in = self.inp("dbk", [128, 8], F32)
        self.dbo_in = self.inp("dbo", [128, 8], F32)
        self.dbv_in = self.inp("dbv", [1, D], F32)
        self.selT_in = self.inp("selT", [128, 32], F32)
        self.ew1_in = self.inp("ew1", [2048, FF], F32)
        self.ewg_in = self.inp("ewg", [2048, FF], F32)
        self.ew2_in = self.inp("ew2", [8192, D], F32)
        self.eb1_in = self.inp("eb1", [1, 8192], F32)
        self.ebg_in = self.inp("ebg", [1, 8192], F32)
        self.eb2_in = self.inp("eb2", [1, 2048], F32)
        self.elng_in = self.inp("elng", [128, 16], F32)
        self.elnb_in = self.inp("elnb", [128, 16], F32)
        self.cwq_in = self.inp("cwq", [D, D], F32)
        self.cwk_in = self.inp("cwk", [D, D], F32)
        self.cwv_in = self.inp("cwv", [D, D], F32)
        self.cwo_in = self.inp("cwo", [D, D], F32)
        self.cbq_in = self.inp("cbq", [128, 8], F32)
        self.cbk_in = self.inp("cbk", [128, 8], F32)
        self.cbv_in = self.inp("cbv", [1, D], F32)
        self.cbo_in = self.inp("cbo", [1, D], F32)
        self.cin = {}
        for nm, shp, dt in [("ones_row", [1, 128], BF16),
                            ("ones_colbf", [128, 1], BF16),
                            ("tri", [128, 128], BF16), ("C2", [128, N], BF16),
                            ("S2", [128, N], BF16), ("E_pack", [128, 128], BF16),
                            ("ones128", [128, 128], BF16), ("identbf", [128, 128], BF16),
                            ("erow", [1, NH], F32)]:
            self.cin[nm] = self.inp(nm, shp, dt)
        self.out_t = self.nc.dram_tensor("out", [D, N], F32, kind="ExternalOutput")

        self.X = [mk.sb("X", [128, N], BF16) for _ in range(NCH)]
        self.C = {}
        for nm in self.cin:
            if nm in ("C2", "S2"):
                continue
            dt = {"erow": F32}.get(nm, BF16)
            self.C[nm] = mk.sb("c_" + nm, list(self.cin[nm].shape), dt)
        self.r_scr = mk.sb("r_scr", [128, N], BF16)
        self.stat = mk.sb("stat", [1, 3072], F32)
        self.statP = mk.sb("statP", [16, 16], F32)
        self.statPbf = mk.sb("statPbf", [16, 4], BF16)
        self.rowbf = mk.sb("rowbf", [1, 64], BF16)
        self.rowf = mk.sb("rowf", [1, 128], F32)
        self.rb = [mk.sb("rb", [128, 512], BF16) for _ in range(2)]
        self.xsq = [mk.sb("xsq", [128, 512], BF16) for _ in range(2)]

        self.sDy = mk.sem("sDy")
        self.sDs = mk.sem("sDs")
        self.sDg = mk.sem("sDg")
        self.sT = mk.sem("sT")
        self.sV = mk.sem("sV")
        self.sS = mk.sem("sS")
        self.sG = mk.sem("sG")
        self.sO = mk.sem("sO")
        self.x_ready = [None] * NCH

    def phase_load(self):
        b = self.mk.block("ld")

        def v0(v):
            v.memset(self.stat[:, :], 0.0)
            v.memset(self.statP[:, :], 0.0)
            v.memset(self.rowbf[:, :], 0.0)
            v.memset(self.rowf[:, :], 0.0)
            v.memset(self.r_scr[:, :], 0.0)
            self.tok_hyg = self.sV.inc(v.memset(self.statP[0:1, 4:5], 1e-5))
        b.v(v0)

        def g(g):
            for c in range(NCH):
                self.tok_X = self.sDg.inc(g.dma_start(out=self.X[c][:, :], in_=self.x_in.ap()[128 * c:128 * (c + 1), :]), 16)
        b.g(g)

        def y(sy):
            for nm in self.cin:
                if nm in ("C2", "S2"):
                    continue
                self.sDy.inc(sy.dma_start(out=self.C[nm][:, :], in_=self.cin[nm].ap()[:, :]), 16)
            wait(sy, self.tok_hyg)
            self.tok_const = self.sDy.inc(sy.dma_start(out=self.rowf[:, 0:16], in_=self.logxi_in.ap()[:, :]), 16)
        b.y(y)
        for c in range(NCH):
            self.x_ready[c] = self.tok_X


    def bcast_row(self, b, row_f32, n, dst, deps, tag, final_scale=1.0):
        mk, pool = self.mk, self.pool
        sT, sV, sS = self.sT, self.sV, self.sS
        rb = self.rowbf

        def v(v):
            wait(v, *deps)
            v.tensor_copy(rb[0:1, 0:n], row_f32)
            v.drain()
            v.scalar_tensor_tensor(self.rowf[:, 80:80 + n], rb[0:1, 0:n], -1.0,
                                   row_f32, ALU.mult, ALU.add)
            v.drain()
            i = v.tensor_copy(rb[0:1, 32:32 + n], self.rowf[:, 80:80 + n])
            self._bc_tok = sV.inc(i)
        b.v(v)
        pi, ps, ptok = pool.acquire()

        def t(t):
            wait(t, self._bc_tok, ptok)
            t.matmul(ps[:, 0:n], self.C["ones_row"][:, :], rb[0:1, 0:n], start=True, stop=False)
            i = t.matmul(ps[:, 0:n], self.C["ones_row"][:, :], rb[0:1, 32:32 + n], start=False, stop=True)
            self._bc_mm = sT.inc(i)
        b.t(t)

        def s(v):
            wait(v, self._bc_mm)
            i = v.tensor_scalar_mul(dst[:, 0:n], ps[:, 0:n], final_scale)
            v.drain()
            self._bc_done = sV.inc(i)
        b.v(s)
        pool.release(pi, self._bc_done)
        return self._bc_done

    def l0_attn(self):
        mk = self.mk
        sT, sV, sS = self.sT, self.sV, self.sS
        pool = self.pool
        sc0 = self.l0_scope = ExitStack()
        _sb = mk.sb
        mk_sb = lambda nm, shp, dt: _sb(nm, shp, dt, scope=sc0)
        Dbig = mk_sb("Dbig", [128, N + 128], F32)
        self.C["C2"] = mk_sb("c_C2", [128, N], BF16)
        self.C["S2"] = mk_sb("c_S2", [128, N], BF16)
        c_all = mk_sb("c_all", [128, 16], F32)
        negB = mk_sb("negB", [128, 16], F32)
        v_tm = [mk_sb("v_tm", [128, 1040], BF16) for _ in range(NTOK)]
        wv = None

        wqk = [[mk_sb("wqk", [128, 512], BF16) for _ in range(NCH)] for _ in range(2)]
        q_sb = [mk_sb("q_sb", [128, N], BF16) for _ in range(2)]
        k_sb = [mk_sb("k_sb", [128, N], BF16) for _ in range(2)]
        swp = mk_sb("swp", [128, N], BF16)
        wv_scope = ExitStack()
        _sbwv = lambda nm, shp, dt: wv_scope.enter_context(self.nc.sbuf_tensor(mk.name(nm), shp, dt))
        Q2 = mk_sb("Q2", [16, N], BF16)
        K2 = mk_sb("K2", [16, N], BF16)
        P_T = [mk_sb("P_T", [128, N], BF16) for _ in range(2)]
        wv = [_sbwv("wv", [128, D], BF16) for _ in range(NCH)]

        st = self.stat

        b = mk.block("l0prep")

        def g(g):
            g.iota(Dbig[:, :], [[1, N + 128]], base=-128, channel_multiplier=-1,
                   allow_small_or_imprecise_dtypes=True)
            self.tok_iota = self.sG.inc(g.engine_nop())
            for c in range(NCH):
                self.tok_wv = self.sDg.inc(g.dma_start(out=wv[c][:, :], in_=self.wqkv_in.ap()[128 * c:128 * (c + 1), 2048:3072]), 16)
            for c in range(NCH):
                self.sDg.inc(g.dma_start(out=wqk[0][c][:, 0:256], in_=self.wqkv_in.ap()[128 * c:128 * (c + 1), 0:256]), 16)
                tk = self.sDg.inc(g.dma_start(out=wqk[0][c][:, 256:512], in_=self.wqkv_in.ap()[128 * c:128 * (c + 1), 1024:1280]), 16)
            self.tok_wqk = [tk, None, None, None]
        b.g(g)

        def yld(sy):
            self.sDy.inc(sy.dma_start(out=self.C["C2"][:, :], in_=self.cin["C2"].ap()[:, :]), 16)
            self.tok_cs2 = self.sDy.inc(sy.dma_start(out=self.C["S2"][:, :], in_=self.cin["S2"].ap()[:, :]), 16)
        b.y(yld)

        def s(s):
            wait(s, self.tok_const)
            self.tok_xi = sS.inc(s.activation(out=self.rowf[:, 16:32], in_=self.rowf[:, 0:16], func=AF.Exp))
        b.s(s)

        def v(v):
            wait(v, self.tok_xi)
            v.reciprocal(self.rowf[:, 32:48], self.rowf[:, 16:32])
            v.drain()
            i = v.scalar_tensor_tensor(self.rowf[:, 48:64], self.rowf[:, 32:48], -1.0,
                                       self.C["erow"][:, :], ALU.mult, ALU.mult)
            self.tok_crow = sV.inc(i)
            v.memset(Q2[:, :], 0.0)
            v.memset(P_T[0][:, :], 0.0)
            v.memset(P_T[1][:, :], 0.0)
            self.tok_q2z = sV.inc(v.memset(K2[:, :], 0.0))
        b.v(v)

        self.tok_call = self.bcast_row(b, self.rowf[:, 48:64], 16, c_all, [self.tok_crow, self.tok_const], "call")

        b = mk.block("l0v")
        ev_toks, mm_toks = [], []
        for tt in range(NTOK):
            for half in range(2):
                pi, psv, ptok = pool.acquire()

                def t(t, tt=tt, half=half, psv=psv, ptok=ptok):
                    wait(t, self.tok_wv, self.tok_X, ptok)
                    i = None
                    for c in range(NCH):
                        i = t.matmul(psv[:, :], self.X[c][:, 128 * tt:128 * (tt + 1)],
                                     wv[c][:, 512 * half:512 * (half + 1)],
                                     start=(c == 0), stop=(c == NCH - 1))
                    mm_toks.append(sT.inc(i))
                b.t(t)

                def v(v, tt=tt, half=half, psv=psv):
                    wait(v, mm_toks[-1])
                    src = ap(psv, 0, 128, [[64, 8], [1, 64]])
                    dst = ap(v_tm[tt], 0, 128, [[65, 8], [1, 64]], off=65 * 8 * half)
                    ev_toks.append(sV.inc(v.tensor_copy(dst, src)))
                b.v(v)
                pool.release(pi, ev_toks[-1])

            def v2(v, tt=tt):
                i = v.memset(ap(v_tm[tt], 0, 128, [[65, 16], [1, 1]], off=64), 1.0)
                self.tok_vones = sV.inc(i)
            b.v(v2)

        wv_scope.close()
        attn = [mk_sb("attn", [128, N], BF16) for _ in range(NCH)]
        wo_ch = [mk_sb("wo_ch", [128, D], BF16) for _ in range(NCH)]

        b = mk.block("l0at")
        ps_o = self.ps_big

        evac_q = []
        exp_cnt = [0]
        pv_done = []
        pv_bank_ev = [None] * 4
        last_score = [None]
        wqk_used = [None, None]
        self.tok_wo = None

        for grp in range(4):
            wbuf = grp % 2
            if grp < 3:
                def g(g, grp=grp):
                    if grp >= 1:
                        wait(g, wqk_used[(grp + 1) % 2])
                    for c in range(NCH):
                        self.sDg.inc(g.dma_start(out=wqk[(grp + 1) % 2][c][:, 0:256],
                                        in_=self.wqkv_in.ap()[128 * c:128 * (c + 1), 256 * (grp + 1):256 * (grp + 2)]), 16)
                        tk = self.sDg.inc(g.dma_start(out=wqk[(grp + 1) % 2][c][:, 256:512],
                                        in_=self.wqkv_in.ap()[128 * c:128 * (c + 1), 1024 + 256 * (grp + 1):1280 + 256 * (grp + 1)]), 16)
                    self.tok_wqk[grp + 1] = tk
                b.g(g)
            else:
                def g(g):
                    for c in range(NCH):
                        self.tok_wo = self.sDg.inc(g.dma_start(out=wo_ch[c][:, :], in_=self.wo_in.ap()[128 * c:128 * (c + 1), :]), 16)
                b.g(g)

            if self.upto == 'v':
                continue
            q2i = [None, None]
            for side in range(2):
                dst = q_sb if side == 0 else k_sb
                sq_dst = Q2 if side == 0 else K2
                for ts in range(4):
                    for qc in range(2):
                        pi, psq, ptok = pool.acquire()

                        def t(t, grp=grp, side=side, ts=ts, qc=qc, psq=psq, ptok=ptok, wbuf=wbuf):
                            wait(t, self.tok_wqk[grp], self.tok_X, ptok)
                            i = None
                            for c in range(NCH):
                                i = t.matmul(psq[:, :],
                                             wqk[wbuf][c][:, 256 * side + 128 * qc:256 * side + 128 * qc + 128],
                                             self.X[c][:, 512 * ts:512 * (ts + 1)],
                                             start=(c == 0), stop=(c == NCH - 1))
                            mm_toks.append(sT.inc(i))
                            wqk_used[wbuf] = mm_toks[-1]
                        b.t(t)

                        if qc == 0:
                            q2i[ts % 2] = pool.acquire()

                        def s(v, grp=grp, side=side, ts=ts, qc=qc, psq=psq):
                            wait(v, mm_toks[-1])
                            if grp >= 1:
                                wait(v, last_score[0])
                            scale = 0.125 if side == 0 else 1.0
                            sl = dst[qc][:, 512 * ts:512 * (ts + 1)]
                            v.tensor_scalar_mul(sl, psq[:, :], scale)
                            i = v.tensor_tensor(self.xsq[qc][:, :], sl, sl, ALU.mult)
                            evac_q.append(self.sV.inc(i))
                        b.v(s)

                        def t2(t, grp=grp, ts=ts, qc=qc):
                            wait(t, evac_q[-1])
                            if qc == 0:
                                wait(t, q2i[ts % 2][2])
                            gc = 2 * grp + qc
                            i = t.matmul(q2i[ts % 2][1][0:16, :], self.C["E_pack"][:, 16 * gc:16 * gc + 16],
                                         self.xsq[qc][:, :], start=(qc == 0), stop=(qc == 1))
                            mm_toks.append(sT.inc(i))
                        b.t(t2)
                        pool.release(pi, evac_q[-1])

                    def v(v, side=side, ts=ts, sq_dst=sq_dst):
                        wait(v, mm_toks[-1], self.tok_q2z)
                        i = v.tensor_tensor(sq_dst[:, 512 * ts:512 * (ts + 1)],
                                            sq_dst[:, 512 * ts:512 * (ts + 1)],
                                            q2i[ts % 2][1][0:16, :], ALU.add)
                        self.tok_q2a = sV.inc(i)
                    b.v(v)
                    pool.release(q2i[ts % 2][0], self.tok_q2a)

            def v(v, grp=grp):
                wait(v, evac_q[-1], self.tok_cs2)
                i = None
                for t_sb in (q_sb[1], k_sb[1]):
                    v.stream_shuffle(swp[:, :], t_sb[:, :], [i0 ^ 1 for i0 in range(32)])
                    v.drain()
                    v.tensor_tensor(t_sb[:, :], t_sb[:, :], self.C["C2"][:, :], ALU.mult)
                    v.tensor_tensor(swp[:, :], swp[:, :], self.C["S2"][:, :], ALU.mult)
                    i = v.tensor_tensor(t_sb[:, :], t_sb[:, :], swp[:, :], ALU.add)
                self.tok_rope = sV.inc(i)
            b.v(v)

            def v(v, grp=grp):
                wait(v, self.tok_q2a)
                v.drain()
                v.reduce_max(self.statP[:, 0:1], Q2[:, :], axis=AXX)
                v.reduce_max(self.statP[:, 1:2], K2[:, :], axis=AXX)
                v.drain()
                i = v.tensor_tensor(self.statP[:, 2:3], self.statP[:, 0:1], self.statP[:, 1:2], ALU.mult)
                self.tok_bmax = sV.inc(i)
            b.v(v)

            def s(s, grp=grp):
                wait(s, self.tok_bmax)
                if grp >= 1:
                    wait(s, self.tok_brow)
                self.tok_bsq = sS.inc(s.activation(out=self.statP[:, 3:4], in_=self.statP[:, 2:3], func=AF.Sqrt))
            b.s(s)

            def vbf(v, grp=grp):
                wait(v, self.tok_bsq)
                self.tok_bbf = sV.inc(v.tensor_copy(self.statPbf[:, 0:1], self.statP[:, 3:4]))
            b.v(vbf)
            bi, ps_bx, btok = pool.acquire()

            def tmt(t, grp=grp, ps_bx=ps_bx, btok=btok):
                wait(t, self.tok_bbf, btok)
                i = t.matmul(ps_bx[0:1, 0:16], self.statPbf[:, 0:1],
                             self.C["identbf"][0:16, 0:16], start=True, stop=True)
                self.tok_btr = sT.inc(i)
            b.t(tmt)

            def scp(s, grp=grp, ps_bx=ps_bx):
                wait(s, self.tok_btr)
                self.tok_brow = sS.inc(s.activation(out=self.rowf[:, 64:80], in_=ps_bx[0:1, 0:16],
                                                    func=AF.Copy))
            b.s(scp)
            pool.release(bi, self.tok_brow)
            self.tok_negB = self.bcast_row(b, self.rowf[:, 64:80], 16, negB,
                                           [self.tok_brow], "negB", final_scale=-1.0)

            for hl in (range(4) if self.upto not in ('v', 'qkv') else []):
                h = 4 * grp + hl
                qc, ro = hl // 2, 64 * (hl % 2)
                odd = h % 2 == 1
                rope_dep = self.tok_rope if HEAD_ROPE[h] else None
                orow = 64 if odd else 0
                zrow = 32 if odd else 64

                for ki in range(NTOK):
                    k0 = 128 * ki
                    pp = ki % 2
                    m = len(pv_done)
                    seg_exp = []
                    first_seg = True
                    for (qs, qe, diag) in seg_list(k0):
                        pi, pss, ptok = pool.acquire()

                        def t(t, h=h, qc=qc, ro=ro, k0=k0, qs=qs, qe=qe, pss=pss, ptok=ptok,
                              rope_dep=rope_dep):
                            wait(t, evac_q[-1], rope_dep, self.tok_negB, self.tok_call, ptok)
                            i = t.matmul(pss[:, 0:qe - qs],
                                         k_sb[qc][ro:ro + 64, k0:k0 + 128],
                                         q_sb[qc][ro:ro + 64, qs:qe], start=True, stop=True)
                            mm_toks.append(sT.inc(i))
                            last_score[0] = mm_toks[-1]
                        b.t(t)
                        dec = [None]
                        if HEAD_DECAY[h]:
                            def v(v, h=h, k0=k0, qs=qs, qe=qe, pss=pss, dec=dec):
                                wait(v, mm_toks[-1], self.tok_iota, self.tok_call)
                                i = v.scalar_tensor_tensor(pss[:, 0:qe - qs],
                                                           Dbig[:, qs - k0 + 128:qe - k0 + 128],
                                                           c_all[:, h:h + 1],
                                                           pss[:, 0:qe - qs], ALU.mult, ALU.add)
                                dec[0] = sV.inc(i)
                            b.v(v)

                        def s(s, h=h, qs=qs, qe=qe, pss=pss, pp=pp, dec=dec, m=m, first=first_seg):
                            wait(s, mm_toks[-1], self.tok_negB, dec[0])
                            if first and m >= 2:
                                wait(s, pv_done[m - 2])
                            i = s.activation(out=P_T[pp][:, qs:qe], in_=pss[:, 0:qe - qs],
                                             func=AF.Exp, bias=negB[:, h:h + 1], scale=1.0)
                            tk = sS.inc(i)
                            seg_exp.append(tk)
                        b.s(s)
                        pool.release(pi, seg_exp[-1])
                        first_seg = False

                        if diag:
                            def v(v, qs=qs, qe=qe, pp=pp, seg_exp=seg_exp):
                                wait(v, seg_exp[-1])
                                i = v.tensor_tensor(P_T[pp][:, qs:qe], P_T[pp][:, qs:qe],
                                                    self.C["tri"][:, 0:qe - qs], ALU.mult)
                                seg_exp.append(sV.inc(i))
                            b.v(v)

                    def t(t, h=h, ki=ki, k0=k0, pp=pp, seg_exp=seg_exp, odd=odd, orow=orow, zrow=zrow):
                        wait(t, *seg_exp)
                        if ki == 0:
                            wait(t, self.tok_vones, *[tk for tk in pv_bank_ev if tk])
                            if odd:
                                wait(t, self.tok_recip)
                        i = None
                        for j in range(4):
                            qs2, qe2 = max(k0, 512 * j), 512 * (j + 1)
                            if qs2 >= qe2:
                                continue
                            last = min(NTOK - 1, 4 * j + 3) == ki
                            M = 64 if odd else 65
                            i = t.matmul(ps_o[orow:orow + M, qs2:qe2],
                                         v_tm[ki][:, 65 * h:65 * h + M],
                                         P_T[pp][:, qs2:qe2], start=(ki == 0), stop=last)
                            if odd:
                                i = t.matmul(ps_o[zrow:zrow + 1, qs2:qe2],
                                             self.C["ones_colbf"][:, :],
                                             P_T[pp][:, qs2:qe2], start=(ki == 0), stop=last)
                        pv_done.append(sT.inc(i))
                    b.t(t)

                def v(v, h=h, zrow=zrow):
                    wait(v, pv_done[-1])
                    i = v.reciprocal(self.r_scr[zrow:zrow + 1, :], ps_o[zrow:zrow + 1, :])
                    self.tok_recip = sV.inc(i)
                b.v(v)
                for j in range(4):
                    ri, ps_rx, rtok = pool.acquire()

                    def t(t, h=h, j=j, ps_rx=ps_rx, rtok=rtok, orow=orow, zrow=zrow):
                        wait(t, self.tok_recip, rtok)
                        i = t.matmul(ps_rx[orow:orow + 64, :],
                                     self.C["ones128"][zrow:zrow + 1, 0:64],
                                     self.r_scr[zrow:zrow + 1, 512 * j:512 * (j + 1)],
                                     start=True, stop=True)
                        self.tok_rbmm = sT.inc(i)
                    b.t(t)

                    def s(v, h=h, j=j, ps_rx=ps_rx, orow=orow):
                        wait(v, self.tok_rbmm)
                        i = v.tensor_copy(self.rb[j % 2][orow:orow + 64, :],
                                          ps_rx[orow:orow + 64, :])
                        self.tok_rbcp = self.sV.inc(i)
                    b.v(s)

                    def v(v, h=h, j=j, orow=orow):
                        wait(v, self.tok_rbcp)
                        i = v.tensor_tensor(attn[h // 2][orow:orow + 64, 512 * j:512 * (j + 1)],
                                            ps_o[orow:orow + 64, 512 * j:512 * (j + 1)],
                                            self.rb[j % 2][orow:orow + 64, :], ALU.mult)
                        tk = sV.inc(i)
                        pv_bank_ev[j] = tk
                        self.tok_attn_ev = tk
                    b.v(v)
                    pool.release(ri, self.tok_attn_ev)

        for ii in range(NCH):
            self.reg_dbg(f"attn{ii}", attn[ii])
        self.reg_dbg("negB", negB)
        self.reg_dbg("c_all", c_all)
        self.reg_dbg("Q2", Q2)
        self.reg_dbg("K2", K2)
        self.reg_dbg("q0", q_sb[0]); self.reg_dbg("q1", q_sb[1])
        self.reg_dbg("k0", k_sb[0]); self.reg_dbg("k1", k_sb[1])
        self.reg_dbg("v0", v_tm[0]); self.reg_dbg("v5", v_tm[5])
        self.reg_dbg("P0", P_T[0]); self.reg_dbg("P1", P_T[1])
        self.reg_dbg("rscr", self.r_scr)
        self.reg_dbg("statP", self.statP)
        self.reg_dbg("rowbf", self.rowbf)
        self.reg_dbg("statrow", self.stat)
        self.reg_dbg("rowf", self.rowf)
        self.reg_dbg("erowc", self.C["erow"])
        self.reg_dbg("onesrowc", self.C["ones_row"])

        if self.upto in ('v', 'qkv', 'heads'):
            if self.upto == 'heads':
                pass
            self.layernorm_skip = True
            return
        b = mk.block("l0wo")
        for dc in range(NCH):
            for ts in range(4):
                pi, psy, ptok = pool.acquire()

                def t(t, dc=dc, ts=ts, psy=psy, ptok=ptok):
                    wait(t, self.tok_wo, self.tok_attn_ev, ptok)
                    i = None
                    for r in range(NCH):
                        i = t.matmul(psy[:, :],
                                     wo_ch[r][:, 128 * dc:128 * (dc + 1)],
                                     attn[r][:, 512 * ts:512 * (ts + 1)],
                                     start=(r == 0), stop=(r == NCH - 1))
                    mm_toks.append(sT.inc(i))
                b.t(t)

                def v(v, dc=dc, ts=ts, psy=psy):
                    wait(v, mm_toks[-1])
                    sl = self.X[dc][:, 512 * ts:512 * (ts + 1)]
                    i = v.scalar_tensor_tensor(sl, psy[:, :], 1.0, sl, ALU.mult, ALU.add)
                    tk = sV.inc(i)
                    self.x_ready[dc] = tk
                b.v(v)
                pool.release(pi, self.x_ready[dc])

        if self.upto != 'wo':
            self.layernorm("ln1")
        self.mk.block("l0done")
        self.l0_scope.close()


    def l0_ffn(self):
        mk = self.mk
        sT, sV, sS = self.sT, self.sV, self.sS
        pool = self.pool
        sc = ExitStack()
        wfcb = [[mk.sb("wfcb", [128, 256], BF16, scope=sc) for _ in range(NCH)] for _ in range(2)]
        wpr = [mk.sb("wpr", [128, 1024], BF16, scope=sc) for _ in range(32)]
        h_sb = [mk.sb("h_sb", [128, 1024], BF16, scope=sc) for _ in range(32)]
        ymb = mk.sb("ymb", [128, 1024], F32, scope=sc)
        mem = mk.sb("mem", [128, 1024], F32, scope=sc)
        tmb = mk.sb("tmb", [128, 1024], F32, scope=sc)
        acc = mk.sb("acc", [128, 1024], F32, scope=sc)

        b = mk.block("ffn")
        wfc_tok = []
        relu_toks = []
        spike_toks = []
        ybank_free = None

        def gld(g):
            for f in range(32):
                self.tok_wpr = self.sDg.inc(g.dma_start(out=wpr[f][:, :],
                                                        in_=self.wpr_in.ap()[128 * f:128 * (f + 1), :]), 16)
        b.g(gld)

        for half in range(2):
            t0 = 1024 * half
            for f in range(32):
                par = (f // 2) % 2
                fo = 128 * (f % 2)
                if f % 2 == 0:
                    def g(g, half=half, f=f, par=par):
                        if len(wfc_tok) >= 2:
                            wait(g, wfc_tok[-2][1])
                        for dc in range(NCH):
                            i = g.dma_start(out=wfcb[par][dc][:, :],
                                            in_=self.wfc_in.ap()[128 * dc:128 * (dc + 1),
                                                                 128 * f:128 * f + 256])
                            self.sDg.inc(i, 16)
                        wfc_tok.append((self.sDg.tok(), None))
                    b.g(g)
                ldtok = self.sDg.tok()
                p1, ps1, tk1 = pool.acquire()
                p2, ps2, tk2 = pool.acquire()

                def t(t, half=half, f=f, par=par, fo=fo, ps1=ps1, ps2=ps2, tk1=tk1, tk2=tk2, ldtok=ldtok):
                    wait(t, ldtok, tk1, tk2, self.x_ready[0], self.x_ready[7])
                    for cs in range(2):
                        ps = ps1 if cs == 0 else ps2
                        i = None
                        for dc in range(NCH):
                            i = t.matmul(ps[:, :], wfcb[par][dc][:, fo:fo + 128],
                                         self.X[dc][:, 1024 * half + 512 * cs:1024 * half + 512 * (cs + 1)],
                                         start=(dc == 0), stop=(dc == NCH - 1))
                        self.tok_fc = sT.inc(i)
                    if f % 2 == 1:
                        wfc_tok[-1] = (wfc_tok[-1][0], self.tok_fc)
                b.t(t)

                def s(s, f=f, ps1=ps1, ps2=ps2):
                    wait(s, self.tok_fc)
                    if relu_toks:
                        wait(s, relu_toks[-1])
                    s.activation(out=self.xsq[0][:, :], in_=ps1[:, :], func=AF.Relu)
                    i = s.activation(out=self.xsq[1][:, :], in_=ps2[:, :], func=AF.Relu)
                    self.tok_relu = sS.inc(i)
                b.s(s)

                def v(v, f=f, ps1=ps1, ps2=ps2):
                    wait(v, self.tok_relu)
                    v.tensor_tensor(h_sb[f][:, 0:512], self.xsq[0][:, :], ps1[:, :], ALU.mult)
                    i = v.tensor_tensor(h_sb[f][:, 512:1024], self.xsq[1][:, :], ps2[:, :], ALU.mult)
                    relu_toks.append(sV.inc(i))
                b.v(v)
                pool.release(p1, relu_toks[-1])
                pool.release(p2, relu_toks[-1])

            for d in range(NCH):
                def t(t, half=half, d=d):
                    wait(t, relu_toks[-1], self.tok_wpr)
                    if ybank_free is not None:
                        wait(t, ybank_free)
                    i = None
                    for f in range(32):
                        for cs in range(2):
                            i = t.matmul(self.ps_big[:, 512 * cs:512 * (cs + 1)],
                                         wpr[f][:, 128 * d:128 * (d + 1)],
                                         h_sb[f][:, 512 * cs:512 * (cs + 1)],
                                         start=(f == 0), stop=(f == 31))
                    self.tok_proj = sT.inc(i)
                b.t(t)

                def v(v, half=half, d=d):
                    wait(v, self.tok_proj)
                    y = self.ps_big[:, 0:1024]
                    v.tensor_scalar_add(ymb[:, :], y, -0.9)
                    v.tensor_scalar_add(mem[:, :], y, 1.0)
                    v.tensor_single_scalar(acc[:, :], y, 1.0, ALU.is_gt)
                    for step in range(4):
                        v.scalar_tensor_tensor(tmb[:, :], mem[:, :], 0.9, ymb[:, :], ALU.mult, ALU.add)
                        v.scalar_tensor_tensor(mem[:, :], mem[:, :], 2.0, tmb[:, :], ALU.is_le, ALU.add)
                        v.scalar_tensor_tensor(acc[:, :], mem[:, :], 2.0, acc[:, :], ALU.is_gt, ALU.add)
                    sl = self.X[d][:, 1024 * half:1024 * (half + 1)]
                    i = v.scalar_tensor_tensor(sl, acc[:, :], 0.2, sl, ALU.mult, ALU.add)
                    tk = sV.inc(i)
                    spike_toks.append(tk)
                    self.x_ready[d] = tk
                b.v(v)
                ybank_free = spike_toks[-1]

        self.layernorm("ln2")
        self.mk.block("ffndone")
        sc.close()


    def l1_attn(self):
        mk = self.mk
        sT, sV, sS = self.sT, self.sV, self.sS
        pool = self.pool
        sc = ExitStack()
        q1 = [mk.sb("q1", [128, N], BF16, scope=sc) for _ in range(NCH)]
        k1 = [mk.sb("k1", [128, N], BF16, scope=sc) for _ in range(NCH)]
        v1 = [mk.sb("v1", [128, 1024], BF16, scope=sc) for _ in range(NTOK)]
        q1aug = mk.sb("q1aug", [1, N], BF16, scope=sc)
        P1a = [mk.sb("P1a", [128, 256], BF16, scope=sc) for _ in range(NTOK)]
        rbq = mk.sb("rbq", [128, 256], BF16, scope=sc)
        tmpq = mk.sb("tmpq", [128, 256], BF16, scope=sc)

        scw = ExitStack()
        wq1 = [mk.sb("wq1", [128, 1024], BF16, scope=scw) for _ in range(NCH)]
        wk1 = [mk.sb("wk1", [128, 1024], BF16, scope=scw) for _ in range(NCH)]
        b = mk.block("l1a")

        def gld(g):
            for c in range(NCH):
                self.sDg.inc(g.dma_start(out=wq1[c][:, :], in_=self.wq1_in.ap()[128 * c:128 * (c + 1), :]), 16)
            for c in range(NCH):
                self.sDg.inc(g.dma_start(out=wk1[c][:, :], in_=self.wk1_in.ap()[128 * c:128 * (c + 1), :]), 16)
        b.g(gld)
        wtok = self.sDg.tok()

        evq = []
        mmq = []
        for side in range(2):
            w_sb = wq1 if side == 0 else wk1
            dstq = q1 if side == 0 else k1
            scl = 1.0 / 32.0 if side == 0 else 1.0
            for ts in range(4):
                zp, psz, ztk = pool.acquire()
                for dc in range(NCH):
                    pi, ps, ptok = pool.acquire()

                    def t(t, side=side, ts=ts, dc=dc, ps=ps, ptok=ptok, w_sb=w_sb):
                        wait(t, wtok, ptok, self.x_ready[dc])
                        i = None
                        for c in range(NCH):
                            i = t.matmul(ps[:, :], w_sb[c][:, 128 * dc:128 * (dc + 1)],
                                         self.X[c][:, 512 * ts:512 * (ts + 1)],
                                         start=(c == 0), stop=(c == NCH - 1))
                        mmq.append(sT.inc(i))
                    b.t(t)

                    def s(v, side=side, ts=ts, dc=dc, ps=ps, scl=scl, dstq=dstq):
                        wait(v, mmq[-1])
                        if evq:
                            wait(v, self.tok_sqc)
                        sl = dstq[dc][:, 512 * ts:512 * (ts + 1)]
                        v.tensor_scalar_mul(sl, ps[:, :], scl)
                        i = v.tensor_tensor(self.xsq[dc % 2][:, :], sl, sl, ALU.mult)
                        evq.append(self.sV.inc(i))
                    b.v(s)

                    def t2(t, side=side, ts=ts, dc=dc, psz=psz, ztk=ztk):
                        wait(t, evq[-1])
                        if dc == 0:
                            wait(t, ztk)
                        i = t.matmul(psz[0:1, :], self.C["ones_colbf"][:, :], self.xsq[dc % 2][:, :],
                                     start=(dc == 0), stop=(dc == NCH - 1))
                        self.tok_sqmm = sT.inc(i)
                        self.tok_sqc = self.tok_sqmm
                    b.t(t2)
                    pool.release(pi, evq[-1])

                if side == 0:
                    def v(v, ts=ts, psz=psz):
                        wait(v, self.tok_sqmm)
                        self.tok_q2s = sV.inc(v.tensor_copy(self.stat[:, 512 * ts:512 * (ts + 1)], psz[0:1, :]))
                    b.v(v)
                    pool.release(zp, self.tok_q2s)
                else:
                    def v(v, ts=ts, psz=psz):
                        wait(v, self.tok_sqmm)
                        self.tok_k2m = sV.inc(v.reduce_max(self.rowf[0:1, 100 + ts:101 + ts], psz[0:1, :], axis=AXX))
                    b.v(v)
                    pool.release(zp, self.tok_k2m)

        def v(v):
            wait(v, self.tok_k2m, self.tok_q2s)
            v.drain()
            i = v.reduce_max(self.rowf[0:1, 104:105], self.rowf[0:1, 100:104], axis=AXX)
            self.tok_kmax = sV.inc(i)
        b.v(v)
        for ts in range(4):
            def v(v, ts=ts):
                wait(v, self.tok_kmax)
                v.drain()
                i = v.tensor_scalar_mul(self.stat[:, 512 * ts:512 * (ts + 1)],
                                        self.stat[:, 512 * ts:512 * (ts + 1)], self.rowf[0:1, 104:105])
                self.tok_bm = sV.inc(i)
            b.v(v)

            def s(s, ts=ts):
                wait(s, self.tok_bm)
                i = s.activation(out=self.stat[:, 512 * ts:512 * (ts + 1)],
                                 in_=self.stat[:, 512 * ts:512 * (ts + 1)], func=AF.Sqrt)
                s.drain()
                i = s.activation(out=q1aug[0:1, 512 * ts:512 * (ts + 1)],
                                 in_=self.stat[:, 512 * ts:512 * (ts + 1)], func=AF.Copy, scale=-1.0)
                self.tok_baug = sS.inc(i)
            b.s(s)

        self.mk.block("l1av")
        scw.close()
        scv = ExitStack()
        wv1 = [mk.sb("wv1", [128, 1024], BF16, scope=scv) for _ in range(NCH)]
        b = mk.block("l1av2")

        def gldv(g):
            for c in range(NCH):
                self.tok_wv1 = self.sDg.inc(g.dma_start(out=wv1[c][:, :], in_=self.wv1_in.ap()[128 * c:128 * (c + 1), :]), 16)
        b.g(gldv)
        for tt in range(NTOK):
            for half in range(2):
                pi, ps, ptok = pool.acquire()

                def t(t, tt=tt, half=half, ps=ps, ptok=ptok):
                    wait(t, self.tok_wv1, ptok, self.x_ready[0], self.x_ready[7])
                    i = None
                    for c in range(NCH):
                        i = t.matmul(ps[:, :], self.X[c][:, 128 * tt:128 * (tt + 1)],
                                     wv1[c][:, 512 * half:512 * (half + 1)],
                                     start=(c == 0), stop=(c == NCH - 1))
                    self.tok_v1 = sT.inc(i)
                b.t(t)

                def s(v, tt=tt, half=half, ps=ps):
                    wait(v, self.tok_v1)
                    i = v.tensor_copy(v1[tt][:, 512 * half:512 * (half + 1)], ps[:, :])
                    self.tok_v1e = self.sV.inc(i)
                b.v(s)
                pool.release(pi, self.tok_v1e)

        res3 = []
        for Q in range(8):
            q0 = 256 * Q
            nsl = 2 * Q + 2
            zp, psz, ztk = pool.acquire()
            exp_list = []
            seg_all = []
            for ki in range(nsl):
                k0 = 128 * ki
                pi, ps, ptok = pool.acquire()

                def t(t, Q=Q, q0=q0, ki=ki, k0=k0, ps=ps, ptok=ptok):
                    wait(t, self.tok_baug, evq[-1], ptok)
                    if Q >= 1:
                        wait(t, self.pv1_done[-1] if self.pv1_done else None)
                    i = None
                    for c in range(NCH):
                        i = t.matmul(ps[:, 0:256], k1[c][:, k0:k0 + 128],
                                     q1[c][:, q0:q0 + 256], start=(c == 0), stop=False)
                    i = t.matmul(ps[:, 0:256], self.C["ones128"][0:1, 0:128], q1aug[0:1, q0:q0 + 256],
                                 start=False, stop=True)
                    self.tok_sc1 = sT.inc(i)
                b.t(t)

                def s(s, Q=Q, ki=ki, ps=ps):
                    wait(s, self.tok_sc1)
                    i = s.activation(out=P1a[ki][:, :], in_=ps[:, 0:256], func=AF.Exp)
                    tk = sS.inc(i)
                    exp_list.append(tk)
                    seg_all.append(tk)
                b.s(s)
                pool.release(pi, exp_list[-1])

                if ki >= 2 * Q:
                    off = 0 if ki == 2 * Q else 128

                    def v(v, ki=ki, off=off):
                        wait(v, exp_list[-1])
                        if off == 128:
                            v.memset(P1a[ki][:, 0:128], 0.0)
                        i = v.tensor_tensor(P1a[ki][:, off:off + 128], P1a[ki][:, off:off + 128],
                                            self.C["tri"][:, 0:128], ALU.mult)
                        self.tok_msk1 = sV.inc(i)
                        seg_all.append(self.tok_msk1)
                    b.v(v)

            pv_toks = []
            for half in range(2):
                def t2(t, Q=Q, nsl=nsl, half=half, psz=psz, ztk=ztk, seg_all=seg_all):
                    wait(t, *seg_all)
                    if half == 0:
                        wait(t, ztk, self.tok_v1e, *res3[-8:])
                    else:
                        wait(t, self.pv1_done[-1])
                    i = None
                    for ki in range(nsl):
                        for oc in range(4):
                            i = t.matmul(self.ps_big[:, 512 * oc:512 * oc + 256],
                                         v1[ki][:, 128 * (4 * half + oc):128 * (4 * half + oc + 1)],
                                         P1a[ki][:, :], start=(ki == 0), stop=(ki == nsl - 1))
                        if half == 0:
                            i = t.matmul(psz[0:1, 0:256], self.C["ones_colbf"][:, :], P1a[ki][:, :],
                                         start=(ki == 0), stop=(ki == nsl - 1))
                    self.pv1_done.append(sT.inc(i))
                b.t(t2)

                if half == 0:
                    def v(v, Q=Q, psz=psz):
                        wait(v, self.pv1_done[-1])
                        i = v.reciprocal(self.r_scr[0:1, 0:256], psz[0:1, 0:256])
                        self.tok_r1 = sV.inc(i)
                    b.v(v)
                    pool.release(zp, self.tok_r1)
                    bi, psb, btok = pool.acquire()

                    def t3(t, Q=Q, psb=psb, btok=btok):
                        wait(t, self.tok_r1, btok)
                        i = t.matmul(psb[:, 0:256], self.C["ones_row"][:, :], self.r_scr[0:1, 0:256],
                                     start=True, stop=True)
                        self.tok_rb1 = sT.inc(i)
                    b.t(t3)

                    def s3(v, Q=Q, psb=psb):
                        wait(v, self.tok_rb1)
                        i = v.tensor_copy(rbq[:, :], psb[:, 0:256])
                        self.tok_rbc1 = self.sV.inc(i)
                    b.v(s3)
                    pool.release(bi, self.tok_rbc1)

                evs = []
                for oc4 in range(4):
                    oc = 4 * half + oc4

                    def v2(v, Q=Q, q0=q0, oc=oc, oc4=oc4):
                        wait(v, self.tok_rbc1, self.pv1_done[-1])
                        v.tensor_tensor(tmpq[:, :], self.ps_big[:, 512 * oc4:512 * oc4 + 256],
                                        rbq[:, :], ALU.mult)
                        sl = self.X[oc][:, q0:q0 + 256]
                        i = v.tensor_tensor(sl, sl, tmpq[:, :], ALU.add)
                        tk = sV.inc(i)
                        res3.append(tk)
                        evs.append(tk)
                        self.x_ready[oc] = tk
                    b.v(v2)
                self.pv1_done.append(evs[-1])

        self.mk.block("l1avdone")
        scv.close()
        self.layernorm("ln3")
        self.mk.block("l1adone")
        sc.close()


    def l1_ffn(self):
        mk = self.mk
        sT, sV, sS = self.sT, self.sV, self.sS
        pool = self.pool
        sc = ExitStack()

        o_fm = [mk.sb("o_fm", [128, 16], BF16, scope=sc) for _ in range(NCH)]
        o_tm = mk.sb("o_tm", [16, 1024], F32, scope=sc)
        y_sb = mk.sb("y_sb", [8, 1024], F32, scope=sc)
        dbc = mk.sb("dbc", [128, 48], F32, scope=sc)
        selT = mk.sb("selT", [128, 32], BF16, scope=sc)

        bo_in = self.nc.dram_tensor("bo_in", [16, 1024], F32)
        bo_out = self.nc.dram_tensor("bo_out", [128, 1024], F32)
        by_in = self.nc.dram_tensor("by_in", [16, 1024], F32)
        by_out = self.nc.dram_tensor("by_out", [128, 1024], F32)
        self.bo_in, self.bo_out, self.by_in, self.by_out = bo_in, bo_out, by_in, by_out
        self.o_fm, self.o_tm, self.y_sb, self.dbc, self.selT = o_fm, o_tm, y_sb, dbc, selT
        self.l1f_scope = sc

        scw = ExitStack()
        dwq = [mk.sb("dwq", [128, 1024], BF16, scope=scw) for _ in range(NCH)]
        dwk = [mk.sb("dwk", [128, 1024], BF16, scope=scw) for _ in range(NCH)]
        dwv = [mk.sb("dwv", [128, 1024], BF16, scope=scw) for _ in range(NCH)]
        dwo = [mk.sb("dwo", [128, 1024], BF16, scope=scw) for _ in range(NCH)]
        slotq = [mk.sb("slotq", [128, 16], BF16, scope=scw) for _ in range(NCH)]
        kd = [mk.sb("kd", [128, N], BF16, scope=scw) for _ in range(NCH)]
        vd = [mk.sb("vd", [128, 1040], BF16, scope=scw) for _ in range(NTOK)]
        qd = [mk.sb("qd", [128, 16], BF16, scope=scw) for _ in range(NCH)]
        dbv_row = mk.sb("dbv_row", [1, 1024], BF16, scope=scw)
        P_d = [mk.sb("P_d", [128, 256], BF16, scope=scw) for _ in range(2)]
        M_d = mk.sb("M_d", [16, 16], F32, scope=scw)
        M_dbf = mk.sb("M_dbf", [16, 16], BF16, scope=scw)
        qd_aug = mk.sb("qd_aug", [128, 256], BF16, scope=scw)
        attn_d = mk.sb("attn_d", [128, 128], BF16, scope=scw)
        rb128 = mk.sb("rb128", [128, 128], BF16, scope=scw)

        b = mk.block("dmha")

        def gld(g):
            for wsb, win in [(dwq, self.dwq_in), (dwk, self.dwk_in), (dwv, self.dwv_in),
                             (dwo, self.dwo_in), (slotq, self.slotq_in)]:
                for c in range(NCH):
                    self.sDg.inc(g.dma_start(out=wsb[c][:, :], in_=win.ap()[128 * c:128 * (c + 1), :]), 16)
            self.sDg.inc(g.dma_start(out=dbv_row[:, :], in_=self.dbv_in.ap()[:, :]), 16)
            self.tok_dld = self.sDg.inc(g.dma_start(out=selT[:, :], in_=self.selT_in.ap()[:, :]), 16)
        b.g(gld)
        wtok = self.sDg.tok()

        def yld(sy):
            self.sDy.inc(sy.dma_start(out=dbc[:, 24:32], in_=self.dbq_in.ap()[:, :]), 16)
            self.sDy.inc(sy.dma_start(out=dbc[:, 8:16], in_=self.dbk_in.ap()[:, :]), 16)
            self.tok_dbc = self.sDy.inc(sy.dma_start(out=dbc[:, 16:24], in_=self.dbo_in.ap()[:, :]), 16)
        b.y(yld)

        def v0(v):
            wait(v, self.tok_dbc)
            i = v.tensor_scalar_mul(dbc[:, 0:8], dbc[:, 24:32], 0.125)
            self.tok_dbc8 = sV.inc(i)
        b.v(v0)

        for dc in range(NCH):
            pi, ps, ptok = pool.acquire()

            def t(t, dc=dc, ps=ps, ptok=ptok):
                wait(t, wtok, ptok)
                i = None
                for c in range(NCH):
                    i = t.matmul(ps[:, 0:16], dwq[c][:, 128 * dc:128 * (dc + 1)], slotq[c][:, :],
                                 start=(c == 0), stop=(c == NCH - 1))
                self.tok_qd = sT.inc(i)
            b.t(t)

            def s(v, dc=dc, ps=ps):
                wait(v, self.tok_qd, self.tok_dbc8)
                i = v.tensor_scalar(qd[dc][:, :], ps[:, 0:16], dbc[:, 24 + dc:25 + dc], 0.125,
                                    ALU.add, ALU.mult)
                self.tok_qde = self.sV.inc(i)
            b.v(s)
            pool.release(pi, self.tok_qde)

        evd = []
        for dc in range(NCH):
            for ts in range(4):
                pi, ps, ptok = pool.acquire()

                def t(t, dc=dc, ts=ts, ps=ps, ptok=ptok):
                    wait(t, wtok, ptok, self.x_ready[dc])
                    i = None
                    for c in range(NCH):
                        i = t.matmul(ps[:, :], dwk[c][:, 128 * dc:128 * (dc + 1)],
                                     self.X[c][:, 512 * ts:512 * (ts + 1)],
                                     start=(c == 0), stop=(c == NCH - 1))
                    self.tok_kd = sT.inc(i)
                b.t(t)

                def s(v, dc=dc, ts=ts, ps=ps):
                    wait(v, self.tok_kd, self.tok_dbc)
                    i = v.tensor_scalar(kd[dc][:, 512 * ts:512 * (ts + 1)], ps[:, :],
                                        dbc[:, 8 + dc:9 + dc], 1.0, ALU.add, ALU.mult)
                    evd.append(self.sV.inc(i))
                b.v(s)
                pool.release(pi, evd[-1])

        for tt in range(NTOK):
            for half in range(2):
                pi, ps, ptok = pool.acquire()

                def t(t, tt=tt, half=half, ps=ps, ptok=ptok):
                    wait(t, ptok, self.x_ready[0], self.x_ready[7], wtok)
                    i = None
                    for c in range(NCH):
                        i = t.matmul(ps[:, :], self.X[c][:, 128 * tt:128 * (tt + 1)],
                                     dwv[c][:, 512 * half:512 * (half + 1)],
                                     start=(c == 0), stop=False)
                    i = t.matmul(ps[:, :], self.C["ones128"][0:1, 0:128],
                                 dbv_row[0:1, 512 * half:512 * (half + 1)], start=False, stop=True)
                    self.tok_vd = sT.inc(i)
                b.t(t)

                def v(v, tt=tt, half=half, ps=ps):
                    wait(v, self.tok_vd)
                    src = ap(ps, 0, 128, [[64, 8], [1, 64]])
                    dst = ap(vd[tt], 0, 128, [[65, 8], [1, 64]], off=65 * 8 * half)
                    i = v.tensor_copy(dst, src)
                    self.tok_vde = sV.inc(i)
                b.v(v)
                pool.release(pi, self.tok_vde)

            def v2(v, tt=tt):
                i = v.memset(ap(vd[tt], 0, 128, [[65, 16], [1, 1]], off=64), 1.0)
                self.tok_vdo = sV.inc(i)
            b.v(v2)

        for h in range(NH):
            ro, rc = 64 * (h % 2), h // 2
            for sg in range(4):
                pi, ps, ptok = pool.acquire()

                def t(t, h=h, ro=ro, rc=rc, sg=sg, ps=ps, ptok=ptok):
                    wait(t, self.tok_qde, evd[-1], ptok)
                    i = t.matmul(ps[0:16, :], qd[rc][ro:ro + 64, :],
                                 kd[rc][ro:ro + 64, 512 * sg:512 * (sg + 1)],
                                 start=True, stop=True)
                    self.tok_qmj = sT.inc(i)
                b.t(t)

                def v(v, h=h, sg=sg, ps=ps):
                    wait(v, self.tok_qmj)
                    if sg > 0:
                        v.drain()
                    i = v.reduce_max(self.statP[:, 8 + sg:9 + sg], ps[0:16, :], axis=AXX)
                    self.tok_qmr = sV.inc(i)
                b.v(v)
                pool.release(pi, self.tok_qmr)

            def v3(v, h=h):
                wait(v, self.tok_qmr)
                v.drain()
                i = v.reduce_max(M_d[:, h:h + 1], self.statP[:, 8:12], axis=AXX)
                self.tok_md = sV.inc(i)
            b.v(v3)

        def v4(v):
            wait(v, self.tok_md)
            v.drain()
            i = v.tensor_copy(M_dbf[:, :], M_d[:, :])
            self.tok_mdb = sV.inc(i)
        b.v(v4)
        for h in range(NH):
            pi, ps, ptok = pool.acquire()
            orow = 64 * (h % 2)

            def t(t, h=h, ps=ps, ptok=ptok, orow=orow):
                wait(t, self.tok_mdb, ptok)
                i = t.matmul(ps[orow:orow + 1, 0:16], M_dbf[:, h:h + 1], self.C["identbf"][0:16, 0:16],
                             start=True, stop=True)
                self.tok_mtr = sT.inc(i)
            b.t(t)

            def s(v, h=h, ps=ps, orow=orow):
                wait(v, self.tok_mtr)
                i = v.tensor_scalar_mul(qd_aug[orow:orow + 1, 16 * h:16 * (h + 1)],
                                        ps[orow:orow + 1, 0:16], -1.0)
                self.tok_mrow = self.sV.inc(i)
            b.v(s)
            pool.release(pi, self.tok_mrow)



        zpi, psz, zptok = pool.acquire()
        pvd = []
        exps_d = []
        for ki in range(NTOK):
            pp = ki % 2
            pi, ps, ptok = pool.acquire()

            def t(t, ki=ki, ps=ps, ptok=ptok):
                wait(t, self.tok_mrow, ptok, self.tok_vdo)
                if ki == 0:
                    wait(t, zptok)
                i = None
                for par in range(2):
                    hs = [h for h in range(NH) if h % 2 == par]
                    ro = 64 * par
                    for n_, h in enumerate(hs):
                        rc = h // 2
                        i = t.matmul(ps[:, 16 * h:16 * (h + 1)],
                                     kd[rc][ro:ro + 64, 128 * ki:128 * (ki + 1)],
                                     qd[rc][ro:ro + 64, :], start=(n_ == 0), stop=False)
                        i = t.matmul(ps[:, 16 * h:16 * (h + 1)], self.C["ones128"][ro:ro + 1, 0:128],
                                     qd_aug[ro:ro + 1, 16 * h:16 * (h + 1)],
                                     start=False, stop=(n_ == len(hs) - 1))
                self.tok_std = sT.inc(i)
            b.t(t)

            def s(s, ki=ki, pp=pp, ps=ps):
                wait(s, self.tok_std)
                if len(pvd) >= 2:
                    wait(s, pvd[-2])
                i = s.activation(out=P_d[pp][:, :], in_=ps[:, 0:256], func=AF.Exp)
                exps_d.append(sS.inc(i))
            b.s(s)
            pool.release(pi, exps_d[-1])

            def t2(t, ki=ki, pp=pp, psz=psz):
                wait(t, exps_d[-1])
                i = None
                for h in range(NH):
                    p, odd = h // 2, h % 2
                    orow = 64 * odd
                    ocol = 512 * odd
                    i = t.matmul(self.ps_big[orow:orow + 64, ocol + 16 * p:ocol + 16 * (p + 1)],
                                 vd[ki][:, 65 * h:65 * h + 64],
                                 P_d[pp][:, 16 * h:16 * (h + 1)],
                                 start=(ki == 0 and h <= 1),
                                 stop=(ki == NTOK - 1 and h >= NH - 2))
                    i = t.matmul(psz[0:1, 16 * h:16 * (h + 1)], self.C["ones_colbf"][:, :],
                                 P_d[pp][:, 16 * h:16 * (h + 1)],
                                 start=(ki == 0 and h == 0), stop=(ki == NTOK - 1 and h == NH - 1))
                pvd.append(sT.inc(i))
            b.t(t2)

        def vz(v):
            wait(v, pvd[-1])
            i = v.reciprocal(self.r_scr[0:1, 0:256], psz[0:1, 0:256])
            self.tok_rzd = sV.inc(i)
        b.v(vz)
        pool.release(zpi, self.tok_rzd)
        bi, psb, btok = pool.acquire()

        def tb(t, psb=psb, btok=btok):
            wait(t, self.tok_rzd, btok)
            t.matmul(psb[0:64, 0:128], self.C["ones128"][0:1, 0:64],
                     ap(self.r_scr, 0, 1, [[32, 8], [1, 16]], off=0), start=True, stop=True)
            i = t.matmul(psb[64:128, 0:128], self.C["ones128"][0:1, 0:64],
                         ap(self.r_scr, 0, 1, [[32, 8], [1, 16]], off=16), start=True, stop=True)
            self.tok_rbd = sT.inc(i)
        b.t(tb)

        def sb_(v, psb=psb):
            wait(v, self.tok_rbd)
            i = v.tensor_copy(rb128[:, :], psb[:, 0:128])
            self.tok_rbcd = self.sV.inc(i)
        b.v(sb_)
        pool.release(bi, self.tok_rbcd)

        def vev(v):
            wait(v, self.tok_rbcd)
            v.tensor_tensor(attn_d[0:64, :], self.ps_big[0:64, 0:128], rb128[0:64, :], ALU.mult)
            i = v.tensor_tensor(attn_d[64:128, :], self.ps_big[64:128, 512:640], rb128[64:128, :], ALU.mult)
            self.tok_atd = sV.inc(i)
        b.v(vev)

        otr = []
        for dc in range(NCH):
            pi, ps, ptok = pool.acquire()

            def t(t, dc=dc, ps=ps, ptok=ptok):
                wait(t, self.tok_atd, ptok)
                i = None
                for p in range(NCH):
                    i = t.matmul(ps[:, 0:16], dwo[p][:, 128 * dc:128 * (dc + 1)],
                                 attn_d[:, 16 * p:16 * (p + 1)], start=(p == 0), stop=(p == NCH - 1))
                self.tok_od = sT.inc(i)
            b.t(t)

            def v(v, dc=dc, ps=ps):
                wait(v, self.tok_od, self.tok_dbc)
                i = v.scalar_tensor_tensor(o_fm[dc][:, :], ps[:, 0:16], dbc[:, 16 + dc:17 + dc],
                                           slotq[dc][:, :], ALU.add, ALU.add)
                self.tok_ofm = sV.inc(i)
            b.v(v)
            pool.release(pi, self.tok_ofm)

            p2i, ps2, p2tok = pool.acquire()

            def t2(t, dc=dc, ps2=ps2, p2tok=p2tok):
                wait(t, self.tok_ofm, p2tok)
                i = t.matmul(ps2[0:16, 0:128], o_fm[dc][:, :], self.C["identbf"][:, :],
                             start=True, stop=True)
                self.tok_otr = sT.inc(i)
            b.t(t2)

            def s2(v, dc=dc, ps2=ps2):
                wait(v, self.tok_otr)
                i = v.tensor_copy(o_tm[:, 128 * dc:128 * (dc + 1)], ps2[0:16, 0:128])
                otr.append(self.sV.inc(i))
            b.v(s2)
            pool.release(p2i, otr[-1])

        def yex(sy):
            wait(sy, otr[-1])
            self.tok_obo = self.sDy.inc(sy.dma_start(out=bo_in.ap()[:, :], in_=o_tm[:, :]), 16)
        b.y(yex)

        def gex(g):
            wait(g, self.tok_obo)
            i = g.collective_compute("AllGather", ALU.bypass, replica_groups=[list(range(8))],
                                     ins=[bo_in.ap().opt()], outs=[bo_out.ap().opt()])
            self.tok_ag1 = self.sG.inc(i)
        b.g(gex)

        self.mk.block("dmha_done")
        scw.close()
        if self.upto == 'dmha':
            self.mk.block("trunc")
            sc.close()
            return

        sce = ExitStack()
        og_sb = mk.sb("og_sb", [128, 1024], BF16, scope=sce)
        yg_sb = mk.sb("yg_sb", [128, 1024], BF16, scope=sce)
        h_all = [mk.sb("h_all", [128, 16], BF16, scope=sce) for _ in range(NCH)]
        h_raw = [mk.sb("h_raw", [128, 8], BF16, scope=sce) for _ in range(NCH)]
        sq8 = mk.sb("sq8", [128, 8], BF16, scope=sce)
        rbc8 = mk.sb("rbc8", [128, 16], BF16, scope=sce)
        w1b = [mk.sb("w1b", [128, 2048], BF16, scope=sce) for _ in range(2)]
        w2b = [mk.sb("w2b", [128, 1024], BF16, scope=sce) for _ in range(2)]
        usb = mk.sb("usb", [8, 4096], BF16, scope=sce)
        gsb = mk.sb("gsb", [8, 4096], BF16, scope=sce)
        ugsb = mk.sb("ugsb", [8, 4096], BF16, scope=sce)
        ugT = [mk.sb("ugT", [128, 8], BF16, scope=sce) for _ in range(32)]
        eb1bf = mk.sb("eb1bf", [1, 8192], BF16, scope=sce)
        ebgbf = mk.sb("ebgbf", [1, 8192], BF16, scope=sce)
        eb2bf = mk.sb("eb2bf", [1, 2048], BF16, scope=sce)
        elng = mk.sb("elng", [128, 16], F32, scope=sce)
        elnb = mk.sb("elnb", [128, 16], F32, scope=sce)
        rowe = mk.sb("rowe", [1, 64], F32, scope=sce)
        rowebf = mk.sb("rowebf", [1, 32], BF16, scope=sce)

        self.w1toks = []
        self.tok_sileva = None
        self.tok_w2mm = None
        self.tok_pgex = None
        self.tok_pgtr = None
        b = mk.block("expert")

        def gog(g):
            wait(g, self.tok_ag1)
            self.tok_og = self.sDg.inc(g.dma_start(out=og_sb[:, :], in_=bo_out.ap()[:, :]), 16)
            self.sDg.inc(g.dma_start(out=eb1bf[:, :], in_=self.eb1_in.ap()[:, :]), 16)
            self.sDg.inc(g.dma_start(out=ebgbf[:, :], in_=self.ebg_in.ap()[:, :]), 16)
            self.tok_ebb = self.sDg.inc(g.dma_start(out=eb2bf[:, :], in_=self.eb2_in.ap()[:, :]), 16)
        b.g(gog)

        def yg0(sy):
            self.sDy.inc(sy.dma_start(out=elng[:, :], in_=self.elng_in.ap()[:, :]), 16)
            self.tok_eln = self.sDy.inc(sy.dma_start(out=elnb[:, :], in_=self.elnb_in.ap()[:, :]), 16)
        b.y(yg0)

        ysb_done = []
        for e in range(2):
            seltok = []
            for ci in range(NCH):
                pi, ps, ptok = pool.acquire()

                def t(t, e=e, ci=ci, ps=ps, ptok=ptok):
                    wait(t, self.tok_og, self.tok_dld, ptok)
                    i = t.matmul(ps[:, 0:8], og_sb[:, 128 * ci:128 * (ci + 1)],
                                 selT[:, 8 * e:8 * (e + 1)], start=True, stop=True)
                    self.tok_sel = sT.inc(i)
                b.t(t)

                def s(v, e=e, ci=ci, ps=ps):
                    wait(v, self.tok_sel)
                    i = v.tensor_copy(h_raw[ci][:, :], ps[:, 0:8])
                    seltok.append(self.sV.inc(i))
                b.v(s)
                pool.release(pi, seltok[-1])

            z1, pz1, zt1 = pool.acquire()
            z2, pz2, zt2 = pool.acquire()
            sqtoks = []
            for ci in range(NCH):
                def s(v, e=e, ci=ci):
                    wait(v, seltok[-1])
                    if sqtoks:
                        wait(v, self.tok_esq)
                    i = v.tensor_tensor(sq8[:, :], h_raw[ci][:, :], h_raw[ci][:, :], ALU.mult)
                    sqtoks.append(self.sV.inc(i))
                b.v(s)

                def t(t, e=e, ci=ci, pz1=pz1, pz2=pz2, zt1=zt1, zt2=zt2):
                    wait(t, sqtoks[-1])
                    if ci == 0:
                        wait(t, zt1, zt2)
                    t.matmul(pz1[0:1, 0:8], self.C["ones_colbf"][:, :], h_raw[ci][:, :],
                             start=(ci == 0), stop=(ci == NCH - 1))
                    i = t.matmul(pz2[0:1, 0:8], self.C["ones_colbf"][:, :], sq8[:, :],
                                 start=(ci == 0), stop=(ci == NCH - 1))
                    self.tok_esq = sT.inc(i)
                b.t(t)

            def v(v, e=e, pz1=pz1, pz2=pz2):
                wait(v, self.tok_esq)
                v.tensor_scalar_mul(rowe[:, 0:8], pz1[0:1, 0:8], 1.0 / D)
                v.drain()
                v.tensor_tensor(rowe[:, 8:16], rowe[:, 0:8], rowe[:, 0:8], ALU.mult)
                v.drain()
                i = v.scalar_tensor_tensor(rowe[:, 16:24], pz2[0:1, 0:8], 1.0 / D,
                                           rowe[:, 8:16], ALU.mult, ALU.subtract)
                self.tok_evar = sV.inc(i)
            b.v(v)
            pool.release(z1, self.tok_evar)
            pool.release(z2, self.tok_evar)

            def s(s, e=e):
                wait(s, self.tok_evar)
                i = s.activation(out=rowe[:, 24:32], in_=rowe[:, 16:24], func=AF.Sqrt,
                                 bias=self.statP[0:1, 4:5], scale=1.0)
                self.tok_esd = sS.inc(i)
            b.s(s)

            def v2(v, e=e):
                wait(v, self.tok_esd)
                v.reciprocal(rowe[:, 32:40], rowe[:, 24:32])
                v.drain()
                v.scalar_tensor_tensor(rowe[:, 40:48], rowe[:, 0:8], -1.0,
                                       rowe[:, 32:40], ALU.mult, ALU.mult)
                v.drain()
                v.tensor_copy(rowebf[0:1, 0:8], rowe[:, 32:40])
                i = v.tensor_copy(rowebf[0:1, 8:16], rowe[:, 40:48])
                self.tok_erow = sV.inc(i)
            b.v(v2)

            b1, pb1, bt1 = pool.acquire()

            def t2(t, e=e, pb1=pb1, bt1=bt1):
                wait(t, self.tok_erow, bt1)
                t.matmul(pb1[:, 0:8], self.C["ones_row"][:, :], rowebf[0:1, 0:8],
                         start=True, stop=False)
                i = t.matmul(pb1[:, 8:16], self.C["ones_row"][:, :], rowebf[0:1, 8:16],
                             start=False, stop=True)
                self.tok_ebc = sT.inc(i)
            b.t(t2)

            def s2(v, e=e, pb1=pb1):
                wait(v, self.tok_ebc)
                i = v.tensor_copy(rbc8[:, :], pb1[:, 0:16])
                self.tok_ebcc = self.sV.inc(i)
            b.v(s2)
            pool.release(b1, self.tok_ebcc)

            lntoks = []
            for ci in range(NCH):
                def v3(v, e=e, ci=ci):
                    wait(v, self.tok_ebcc, self.tok_eln)
                    v.tensor_tensor(h_raw[ci][:, :], h_raw[ci][:, :], rbc8[:, 0:8], ALU.mult)
                    v.tensor_tensor(h_raw[ci][:, :], h_raw[ci][:, :], rbc8[:, 8:16], ALU.add)
                    i = v.tensor_scalar(h_all[ci][:, 8 * e:8 * (e + 1)], h_raw[ci][:, :],
                                        elng[:, 8 * e + ci:8 * e + ci + 1],
                                        elnb[:, 8 * e + ci:8 * e + ci + 1], ALU.mult, ALU.add)
                    lntoks.append(sV.inc(i))
                b.v(v3)

            for mat in range(2):
                w_in = self.ew1_in if mat == 0 else self.ewg_in
                brow = eb1bf if mat == 0 else ebgbf
                dst = usb if mat == 0 else gsb
                for Fh in range(2):
                    mmtoks = []

                    def t0(t, e=e, mat=mat, Fh=Fh, brow=brow):
                        wait(t, self.tok_ebb)
                        if self.tok_sileva is not None:
                            wait(t, self.tok_sileva)
                        if self.tok_w2mm is not None:
                            wait(t, self.tok_w2mm)
                        i = None
                        for fs in range(4):
                            i = t.matmul(self.ps_big[0:8, 512 * fs:512 * (fs + 1)],
                                         self.C["ones128"][0:1, 0:8],
                                         brow[0:1, 4096 * e + 2048 * Fh + 512 * fs:
                                              4096 * e + 2048 * Fh + 512 * (fs + 1)],
                                         start=True, stop=False)
                        mmtoks.append(sT.inc(i))
                    b.t(t0)

                    for dc in range(NCH):
                        def g(g, e=e, mat=mat, Fh=Fh, dc=dc, w_in=w_in):
                            if len(self.w1toks) >= 2:
                                wait(g, self.w1toks[-2][1])
                            i = g.dma_start(out=w1b[dc % 2][:, :],
                                            in_=w_in.ap()[1024 * e + 128 * dc:1024 * e + 128 * (dc + 1),
                                                          2048 * Fh:2048 * (Fh + 1)])
                            self.w1toks.append([self.sDg.inc(i, 16), None])
                        b.g(g)
                        ldtok = self.sDg.tok()

                        def t(t, e=e, dc=dc, ldtok=ldtok):
                            wait(t, ldtok, lntoks[-1], mmtoks[-1])
                            i = None
                            for fs in range(4):
                                i = t.matmul(self.ps_big[0:8, 512 * fs:512 * (fs + 1)],
                                             h_all[dc][:, 8 * e:8 * (e + 1)],
                                             w1b[dc % 2][:, 512 * fs:512 * (fs + 1)],
                                             start=False, stop=(dc == NCH - 1))
                            tk = sT.inc(i)
                            mmtoks.append(tk)
                            self.w1toks[-1][1] = tk
                        b.t(t)

                    for fs in range(4):
                        def s(s, e=e, Fh=Fh, fs=fs, dst=dst):
                            wait(s, mmtoks[-1])
                            i = s.activation(out=dst[:, 2048 * Fh + 512 * fs:2048 * Fh + 512 * (fs + 1)],
                                             in_=self.ps_big[0:8, 512 * fs:512 * (fs + 1)], func=AF.Silu)
                            self.tok_sileva = sS.inc(i)
                        b.s(s)

            ugtoks = []
            for Fh in range(2):
                def v4(v, e=e, Fh=Fh):
                    wait(v, self.tok_sileva)
                    i = v.tensor_tensor(ugsb[:, 2048 * Fh:2048 * (Fh + 1)],
                                        usb[:, 2048 * Fh:2048 * (Fh + 1)],
                                        gsb[:, 2048 * Fh:2048 * (Fh + 1)], ALU.mult)
                    ugtoks.append(sV.inc(i))
                b.v(v4)

            trtoks = []
            for fc in range(32):
                pi, ps, ptok = pool.acquire()

                def t3(t, e=e, fc=fc, ps=ps, ptok=ptok):
                    wait(t, ugtoks[-1], ptok)
                    i = t.matmul(ps[0:128, 0:8], ugsb[0:8, 128 * fc:128 * (fc + 1)],
                                 self.C["identbf"][0:8, 0:8], start=True, stop=True)
                    self.tok_ugt = sT.inc(i)
                b.t(t3)

                def s3(v, e=e, fc=fc, ps=ps):
                    wait(v, self.tok_ugt)
                    i = v.tensor_copy(ugT[fc][:, :], ps[0:128, 0:8])
                    trtoks.append(self.sV.inc(i))
                b.v(s3)
                pool.release(pi, trtoks[-1])

            y1, py1, yt1 = pool.acquire()
            y2, py2, yt2 = pool.acquire()
            w2toks = []

            def t4(t, e=e, py1=py1, py2=py2, yt1=yt1, yt2=yt2):
                wait(t, self.tok_ebb, yt1, yt2)
                t.matmul(py1[0:8, :], self.C["ones128"][0:1, 0:8], eb2bf[0:1, 1024 * e:1024 * e + 512],
                         start=True, stop=False)
                i = t.matmul(py2[0:8, :], self.C["ones128"][0:1, 0:8],
                             eb2bf[0:1, 1024 * e + 512:1024 * e + 1024], start=True, stop=False)
                self.tok_w2mm = sT.inc(i)
            b.t(t4)

            for fc in range(32):
                def g2(g, e=e, fc=fc):
                    if len(w2toks) >= 2:
                        wait(g, w2toks[-2][1])
                    i = g.dma_start(out=w2b[fc % 2][:, :],
                                    in_=self.ew2_in.ap()[4096 * e + 128 * fc:4096 * e + 128 * (fc + 1), :])
                    w2toks.append([self.sDg.inc(i, 16), None])
                b.g(g2)
                ldtok = self.sDg.tok()

                def t5(t, e=e, fc=fc, ldtok=ldtok, py1=py1, py2=py2):
                    wait(t, ldtok, trtoks[-1], self.tok_w2mm)
                    t.matmul(py1[0:8, :], ugT[fc][:, :], w2b[fc % 2][:, 0:512],
                             start=False, stop=(fc == 31))
                    i = t.matmul(py2[0:8, :], ugT[fc][:, :], w2b[fc % 2][:, 512:1024],
                                 start=False, stop=(fc == 31))
                    tk = sT.inc(i)
                    self.tok_w2mm = tk
                    w2toks[-1][1] = tk
                b.t(t5)

            def s4(v, e=e, py1=py1, py2=py2):
                wait(v, self.tok_w2mm)
                v.tensor_copy(y_sb[:, 0:512], py1[0:8, :])
                i = v.tensor_copy(y_sb[:, 512:1024], py2[0:8, :])
                self.tok_ysb = self.sV.inc(i)
            b.v(s4)
            pool.release(y1, self.tok_ysb)
            pool.release(y2, self.tok_ysb)

            def y5(sy, e=e):
                wait(sy, self.tok_ysb)
                i = sy.dma_start(out=by_in.ap()[8 * e:8 * (e + 1), :], in_=y_sb[0:8, :])
                ysb_done.append(self.sDy.inc(i, 16))
            b.y(y5)
            def s5(v, e=e):
                wait(v, ysb_done[-1])
            b.v(s5)

        def gex2(g):
            wait(g, ysb_done[-1])
            i = g.collective_compute("AllGather", ALU.bypass, replica_groups=[list(range(8))],
                                     ins=[by_in.ap().opt()], outs=[by_out.ap().opt()])
            self.tok_ag2 = self.sG.inc(i)
            wait(g, self.tok_ag2)
            self.tok_yg = self.sDg.inc(g.dma_start(out=yg_sb[:, :], in_=by_out.ap()[:, :]), 16)
        b.g(gex2)

        for dc in range(NCH):
            pi, ps, ptok = pool.acquire()

            def t6(t, dc=dc, ps=ps, ptok=ptok):
                wait(t, self.tok_yg, ptok)
                i = t.matmul(ps[:, 0:16], yg_sb[:, 128 * dc:128 * (dc + 1)], selT[:, 16:32],
                             start=True, stop=True)
                self.tok_ysel = sT.inc(i)
            b.t(t6)

            def v6(v, dc=dc, ps=ps):
                wait(v, self.tok_ysel)
                i = v.scalar_tensor_tensor(o_fm[dc][:, :], ps[:, 0:16], 1.0, o_fm[dc][:, :],
                                           ALU.mult, ALU.add)
                self.tok_onew = sV.inc(i)
            b.v(v6)
            pool.release(pi, self.tok_onew)

        self.mk.block("expert_done")
        sce.close()
        if self.upto == 'expert':
            self.mk.block("trunc")
            sc.close()
            return

        scc = ExitStack()
        qc = [mk.sb("qc", [128, N], BF16, scope=scc) for _ in range(NCH)]
        kc = [mk.sb("kc", [128, 16], BF16, scope=scc) for _ in range(NCH)]
        vc = mk.sb("vc", [16, 1024], BF16, scope=scc)
        attn_c = [mk.sb("attn_c", [128, N], BF16, scope=scc) for _ in range(NCH)]
        P_Th = [mk.sb("P_Th", [16, N], BF16, scope=scc) for _ in range(4)]
        Pg = [mk.sb("Pg", [128, 64], BF16, scope=scc) for _ in range(2)]
        mx4 = mk.sb("mx4", [128, 8], F32, scope=scc)
        cbc = mk.sb("cbc", [128, 24], F32, scope=scc)
        cbv_row = mk.sb("cbv_row", [1, 1024], BF16, scope=scc)
        cbo_row = mk.sb("cbo_row", [1, 1024], BF16, scope=scc)
        ones2k = mk.sb("ones2k", [1, 512], BF16, scope=scc)
        scwc = ExitStack()
        cwq = [mk.sb("cwq", [128, 1024], BF16, scope=scwc) for _ in range(NCH)]
        cwk = [mk.sb("cwk", [128, 1024], BF16, scope=scwc) for _ in range(NCH)]
        cwv = [mk.sb("cwv", [128, 1024], BF16, scope=scwc) for _ in range(NCH)]

        b = mk.block("cmha")

        def gld(g):
            for wsb, win in [(cwq, self.cwq_in), (cwk, self.cwk_in), (cwv, self.cwv_in)]:
                for c in range(NCH):
                    self.sDg.inc(g.dma_start(out=wsb[c][:, :], in_=win.ap()[128 * c:128 * (c + 1), :]), 16)
            self.sDg.inc(g.dma_start(out=cbv_row[:, :], in_=self.cbv_in.ap()[:, :]), 16)
            self.tok_cld = self.sDg.inc(g.dma_start(out=cbo_row[:, :], in_=self.cbo_in.ap()[:, :]), 16)
        b.g(gld)
        wtok = self.sDg.tok()

        def yld(sy):
            self.sDy.inc(sy.dma_start(out=cbc[:, 8:16], in_=self.cbq_in.ap()[:, :]), 16)
            self.tok_cbc = self.sDy.inc(sy.dma_start(out=cbc[:, 16:24], in_=self.cbk_in.ap()[:, :]), 16)
        b.y(yld)

        def v0(v):
            wait(v, self.tok_cbc)
            v.tensor_scalar_mul(cbc[:, 0:8], cbc[:, 8:16], 0.125)
            self.tok_cbc8 = sV.inc(v.memset(ones2k[:, :], 1.0))
        b.v(v0)

        evc = []
        for dc in range(NCH):
            for ts in range(4):
                pi, ps, ptok = pool.acquire()

                def t(t, dc=dc, ts=ts, ps=ps, ptok=ptok):
                    wait(t, wtok, ptok, self.x_ready[dc])
                    i = None
                    for c in range(NCH):
                        i = t.matmul(ps[:, :], cwq[c][:, 128 * dc:128 * (dc + 1)],
                                     self.X[c][:, 512 * ts:512 * (ts + 1)],
                                     start=(c == 0), stop=(c == NCH - 1))
                    self.tok_qc = sT.inc(i)
                b.t(t)

                def s(v, dc=dc, ts=ts, ps=ps):
                    wait(v, self.tok_qc, self.tok_cbc8)
                    i = v.tensor_scalar(qc[dc][:, 512 * ts:512 * (ts + 1)], ps[:, :],
                                        cbc[:, 8 + dc:9 + dc], 0.125, ALU.add, ALU.mult)
                    evc.append(self.sV.inc(i))
                b.v(s)
                pool.release(pi, evc[-1])

        for dc in range(NCH):
            pi, ps, ptok = pool.acquire()

            def t(t, dc=dc, ps=ps, ptok=ptok):
                wait(t, wtok, ptok, self.tok_onew)
                i = None
                for c in range(NCH):
                    i = t.matmul(ps[:, 0:16], cwk[c][:, 128 * dc:128 * (dc + 1)], o_fm[c][:, :],
                                 start=(c == 0), stop=(c == NCH - 1))
                self.tok_kc = sT.inc(i)
            b.t(t)

            def s(v, dc=dc, ps=ps):
                wait(v, self.tok_kc, self.tok_cbc)
                i = v.tensor_scalar(kc[dc][:, :], ps[:, 0:16], cbc[:, 16 + dc:17 + dc], 1.0,
                                    ALU.add, ALU.mult)
                self.tok_kce = self.sV.inc(i)
            b.v(s)
            pool.release(pi, self.tok_kce)

        for half in range(2):
            pi, ps, ptok = pool.acquire()

            def t(t, half=half, ps=ps, ptok=ptok):
                wait(t, wtok, ptok, self.tok_onew, self.tok_cld)
                i = None
                for c in range(NCH):
                    i = t.matmul(ps[0:16, :], o_fm[c][:, :], cwv[c][:, 512 * half:512 * (half + 1)],
                                 start=(c == 0), stop=False)
                i = t.matmul(ps[0:16, :], self.C["ones128"][0:1, 0:16],
                             cbv_row[0:1, 512 * half:512 * (half + 1)], start=False, stop=True)
                self.tok_vc = sT.inc(i)
            b.t(t)

            def s(v, half=half, ps=ps):
                wait(v, self.tok_vc)
                i = v.tensor_copy(vc[:, 512 * half:512 * (half + 1)], ps[0:16, :])
                self.tok_vce = self.sV.inc(i)
            b.v(s)
            pool.release(pi, self.tok_vce)

        if self.upto == 'cmha1':
            self.mk.block("trunc")
            scwc.close()
            scc.close()
            sc.close()
            return
        self.mk.block("cmha_w")
        scwc.close()
        scwo = ExitStack()
        cwo = [mk.sb("cwo", [128, 1024], BF16, scope=scwo) for _ in range(NCH)]
        b = mk.block("cmha2")

        def gldo(g):
            for c in range(NCH):
                self.tok_cwo = self.sDg.inc(g.dma_start(out=cwo[c][:, :], in_=self.cwo_in.ap()[128 * c:128 * (c + 1), :]), 16)
        b.g(gldo)

        self.tok_pgdone = None
        ngq = 4
        GQ_HEADS = [[0, 2, 4, 6], [8, 10, 12, 14], [1, 3, 5, 7], [9, 11, 13, 15]]
        for gq in range(ngq):
            gheads = GQ_HEADS[gq]
            cp_toks = []
            for tt in range(NTOK):
                pp = tt % 2
                pi, ps, ptok = pool.acquire()

                def t(t, gq=gq, tt=tt, ps=ps, ptok=ptok, gheads=gheads):
                    wait(t, evc[-1], self.tok_kce, ptok)
                    i = None
                    for j, h in enumerate(gheads):
                        ro, rc = 64 * (h % 2), h // 2
                        i = t.matmul(ps[:, 16 * j:16 * (j + 1)],
                                     qc[rc][ro:ro + 64, 128 * tt:128 * (tt + 1)],
                                     kc[rc][ro:ro + 64, :], start=(j == 0), stop=(j == 3))
                    self.tok_scc = sT.inc(i)
                b.t(t)

                def v(v, gq=gq, tt=tt, ps=ps):
                    wait(v, self.tok_scc, self.tok_pgex if tt >= 1 else None)
                    v.tensor_reduce(mx4[:, 0:4], ap(ps, 0, 128, [[16, 4], [1, 16]]),
                                    axis=AXX, op=ALU.max)
                    v.drain()
                    i = v.tensor_reduce(mx4[:, 4:5], mx4[:, 0:4], axis=AXX, op=ALU.max, negate=True)
                    self.tok_mx = sV.inc(i)
                b.v(v)


                def s(s, gq=gq, tt=tt, pp=pp, ps=ps):
                    wait(s, self.tok_mx)
                    if self.tok_pgtr is not None:
                        wait(s, self.tok_pgtr)
                    i = s.activation(out=Pg[pp][:, :], in_=ps[:, 0:64],
                                     func=AF.Exp, bias=mx4[:, 4:5], scale=1.0)
                    self.tok_pgex = sS.inc(i)
                b.s(s)
                pool.release(pi, self.tok_pgex)


                def v2(v, gq=gq, tt=tt, pp=pp):
                    wait(v, self.tok_pgex)
                    v.tensor_reduce(mx4[:, 0:4], ap(Pg[pp], 0, 128, [[16, 4], [1, 16]]),
                                    axis=AXX, op=ALU.add)
                    v.drain()
                    v.reciprocal(mx4[:, 4:8], mx4[:, 0:4])
                    v.drain()
                    i = None
                    for j in range(4):
                        i = v.tensor_scalar_mul(Pg[pp][:, 16 * j:16 * (j + 1)],
                                                Pg[pp][:, 16 * j:16 * (j + 1)], mx4[:, 4 + j:5 + j])
                    self.tok_pn = sV.inc(i)
                b.v(v2)


                def t2(t, gq=gq, tt=tt, pp=pp):
                    wait(t, self.tok_pn)
                    if tt % 4 == 0 and tt > 0:
                        wait(t, cp_toks[-1])
                    elif tt == 0:
                        wait(t, self.tok_pgdone)
                    i = None
                    for j in range(4):
                        i = t.matmul(self.ps_big[0:16, 512 * j + 128 * (tt % 4):512 * j + 128 * (tt % 4) + 128],
                                     Pg[pp][:, 16 * j:16 * (j + 1)], self.C["identbf"][:, :],
                                     start=(tt % 4 == 0), stop=(tt % 4 == 3))
                    self.tok_pgtr = sT.inc(i)
                b.t(t2)

                if tt % 4 == 3:
                    for j in range(4):
                        def s2(v, gq=gq, tt=tt, j=j):
                            wait(v, self.tok_pgtr, self.tok_pgdone)
                            i = v.tensor_copy(P_Th[j][:, 512 * (tt // 4):512 * (tt // 4 + 1)],
                                              self.ps_big[0:16, 512 * j:512 * (j + 1)])
                            cp_toks.append(self.sV.inc(i))
                        b.v(s2)

            pv_toks = []
            for j, h in enumerate(gheads):
                ro, p = 64 * (h % 2), h // 2
                for cs in range(4):
                    pi, ps, ptok = pool.acquire()

                    def t3(t, h=h, j=j, ro=ro, cs=cs, ps=ps, ptok=ptok):
                        wait(t, cp_toks[-1], self.tok_vce, ptok)
                        i = t.matmul(ps[ro:ro + 64, :], vc[:, 64 * h:64 * (h + 1)],
                                     P_Th[j][:, 512 * cs:512 * (cs + 1)], start=True, stop=True)
                        self.tok_pvc = sT.inc(i)
                    b.t(t3)

                    def s3(v, h=h, ro=ro, p=p, cs=cs, ps=ps):
                        wait(v, self.tok_pvc)
                        i = v.tensor_copy(attn_c[p][ro:ro + 64, 512 * cs:512 * (cs + 1)],
                                          ps[ro:ro + 64, :])
                        pv_toks.append(self.sV.inc(i))
                    b.v(s3)
                    pool.release(pi, pv_toks[-1])
            self.tok_pgdone = pv_toks[-1]
            self.tok_pgdone = pv_toks[-1]

        if self.upto == 'cmha2':
            self.mk.block("trunc")
            scwo.close()
            scc.close()
            sc.close()
            return
        for dc in range(NCH):
            for ts in range(4):
                pi, ps, ptok = pool.acquire()

                def t(t, dc=dc, ts=ts, ps=ps, ptok=ptok):
                    wait(t, pv_toks[-1] if pv_toks else None, ptok, self.tok_cbc8, self.tok_cwo)
                    t.matmul(ps[:, :], cbo_row[0:1, 128 * dc:128 * (dc + 1)], ones2k[0:1, :],
                             start=True, stop=False)
                    i = None
                    for r in range(NCH):
                        i = t.matmul(ps[:, :], cwo[r][:, 128 * dc:128 * (dc + 1)],
                                     attn_c[r][:, 512 * ts:512 * (ts + 1)],
                                     start=False, stop=(r == NCH - 1))
                    self.tok_co = sT.inc(i)
                b.t(t)

                def v(v, dc=dc, ts=ts, ps=ps):
                    wait(v, self.tok_co)
                    sl = self.X[dc][:, 512 * ts:512 * (ts + 1)]
                    i = v.scalar_tensor_tensor(sl, ps[:, :], 1.0, sl, ALU.mult, ALU.add)
                    tk = sV.inc(i)
                    self.x_ready[dc] = tk
                b.v(v)
                pool.release(pi, self.x_ready[dc])

        self.mk.block("cmha_done")
        scwo.close()
        scc.close()
        self.layernorm("ln4")
        self.mk.block("l1f_done")
        sc.close()

    def layernorm(self, name):
        mk = self.mk
        sT, sV, sS = self.sT, self.sV, self.sS
        pool = self.pool
        b = mk.block(name)
        st = self.stat
        ap_toks = []
        for ts in range(4):
            i1, ps1, tk1 = pool.acquire()
            i2, ps2, tk2 = pool.acquire()
            sq_toks, mmtk = [], []
            for c in range(NCH):
                def s(v, ts=ts, c=c):
                    wait(v, self.x_ready[c])
                    if len(sq_toks) >= 2:
                        wait(v, mmtk[-2])
                    sl = self.X[c][:, 512 * ts:512 * (ts + 1)]
                    i = v.tensor_tensor(self.xsq[c % 2][:, :], sl, sl, ALU.mult)
                    sq_toks.append(self.sV.inc(i))
                b.v(s)

                def t(t, ts=ts, c=c, ps1=ps1, ps2=ps2, tk1=tk1, tk2=tk2):
                    wait(t, sq_toks[-1], self.x_ready[c])
                    if c == 0:
                        wait(t, tk1, tk2)
                    t.matmul(ps1[0:1, :], self.C["ones_colbf"][:, :],
                             self.X[c][:, 512 * ts:512 * (ts + 1)], start=(c == 0), stop=(c == NCH - 1))
                    i = t.matmul(ps2[0:1, :], self.C["ones_colbf"][:, :], self.xsq[c % 2][:, :],
                                 start=(c == 0), stop=(c == NCH - 1))
                    mmtk.append(sT.inc(i))
                b.t(t)

            def v(v, ts=ts, ps1=ps1, ps2=ps2):
                wait(v, mmtk[-1])
                v.tensor_scalar_mul(st[:, 0:512], ps1[0:1, :], 1.0 / D)
                v.drain()
                v.tensor_tensor(st[:, 512:1024], st[:, 0:512], st[:, 0:512], ALU.mult)
                v.drain()
                i = v.scalar_tensor_tensor(st[:, 1024:1536], ps2[0:1, :], 1.0 / D,
                                           st[:, 512:1024], ALU.mult, ALU.subtract)
                self.tok_var = sV.inc(i)
            b.v(v)
            pool.release(i1, self.tok_var)
            pool.release(i2, self.tok_var)

            def s(s, ts=ts):
                wait(s, self.tok_var, self.tok_hyg)
                i = s.activation(out=st[:, 1536:2048], in_=st[:, 1024:1536], func=AF.Sqrt,
                                 bias=self.statP[0:1, 4:5], scale=1.0)
                self.tok_sd = sS.inc(i)
            b.s(s)

            def v2(v, ts=ts):
                wait(v, self.tok_sd)
                v.reciprocal(st[:, 2048:2560], st[:, 1536:2048])
                v.drain()
                v.scalar_tensor_tensor(st[:, 2560:3072], st[:, 0:512], -1.0,
                                       st[:, 2048:2560], ALU.mult, ALU.mult)
                v.drain()
                v.tensor_copy(self.r_scr[0:1, 0:512], st[:, 2048:2560])
                i = v.tensor_copy(self.r_scr[0:1, 512:1024], st[:, 2560:3072])
                self.tok_rows = sV.inc(i)
            b.v(v2)

            b1, psb1, tb1 = pool.acquire()
            b2, psb2, tb2 = pool.acquire()

            def t2(t, ts=ts, psb1=psb1, psb2=psb2, tb1=tb1, tb2=tb2):
                wait(t, self.tok_rows, tb1, tb2)
                t.matmul(psb1[:, :], self.C["ones_row"][:, :], self.r_scr[0:1, 0:512],
                         start=True, stop=True)
                i = t.matmul(psb2[:, :], self.C["ones_row"][:, :], self.r_scr[0:1, 512:1024],
                             start=True, stop=True)
                self.tok_lnb = sT.inc(i)
            b.t(t2)

            def s2(v, ts=ts, psb1=psb1, psb2=psb2):
                wait(v, self.tok_lnb)
                v.tensor_copy(self.rb[0][:, :], psb1[:, :])
                i = v.tensor_copy(self.rb[1][:, :], psb2[:, :])
                self.tok_lnrb = sV.inc(i)
            b.v(s2)

            for c in range(NCH):
                def v3(v, ts=ts, c=c):
                    wait(v, self.tok_lnrb)
                    sl = self.X[c][:, 512 * ts:512 * (ts + 1)]
                    v.tensor_tensor(sl, sl, self.rb[0][:, :], ALU.mult)
                    i = v.tensor_tensor(sl, sl, self.rb[1][:, :], ALU.add)
                    tk = sV.inc(i)
                    ap_toks.append(tk)
                    self.x_ready[c] = tk
                b.v(v3)
            pool.release(b1, ap_toks[-1])
            pool.release(b2, ap_toks[-1])

    def store_out(self):
        mk = self.mk
        self.ost = [mk.sb("ost", [128, 512], F32) for _ in range(2)]
        b = mk.block("outp")
        o_toks = []
        for c in range(NCH):
            for ts in range(4):
                def v(v, c=c, ts=ts):
                    wait(v, self.x_ready[c])
                    if len(o_toks) >= 2:
                        wait(v, o_toks[-2])
                    i = v.tensor_copy(self.ost[(c * 4 + ts) % 2][:, :], self.X[c][:, 512 * ts:512 * (ts + 1)])
                    self.tok_ostc = self.sV.inc(i)
                b.v(v)

                def y(sy, c=c, ts=ts):
                    wait(sy, self.tok_ostc)
                    i = sy.dma_start(out=self.out_t.ap()[128 * c:128 * (c + 1), 512 * ts:512 * (ts + 1)],
                                     in_=self.ost[(c * 4 + ts) % 2][:, :])
                    o_toks.append(self.sO.inc(i, 16))
                b.y(y)

        def y2(sy):
            wait(sy, o_toks[-1])
        b.y(y2)

    def build(self):
        self.declare()
        self.pv1_done = []
        self.pool = PsumPool(self.mk, 4)
        self.ps_big = self.mk.ps("ps_big", [128, N])
        with self.nc.allow_low_precision(reason="intentional bf16 activations/weights"):
            self.phase_load()
            if self.upto != 'none':
                self.l0_attn()
                if self.upto in ('ffn', 'l01', 'l1a', 'full2', 'dmha', 'expert', 'cmha1', 'cmha2'):
                    self.l0_ffn()
                if self.upto in ('l1a', 'full2', 'dmha', 'expert', 'cmha1', 'cmha2'):
                    self.l1_attn()
                if self.upto in ('full2', 'dmha', 'expert', 'cmha1', 'cmha2'):
                    self.l1_ffn()
            self.store_out()
            self.dump_dbg()
            self.mk.end_blocks()
        return self.nc


CONST_KEYS = ["ones_row", "ones_colbf", "tri", "C2", "S2", "E_pack", "erow",
              "ones128", "identbf"]


def stage_inputs_core(x_b, wqkv, wo, logxi, consts, wfc=None, wpr=None, **kw):
    d = {"x": np.ascontiguousarray(x_b.T),
         "wqkv": np.ascontiguousarray(wqkv),
         "wo": np.ascontiguousarray(wo),
         "logxi": np.asarray(logxi, dtype=np.float32).reshape(1, NH)}
    if wfc is not None:
        d["wfc"] = np.ascontiguousarray(wfc.T)
        d["wpr"] = np.ascontiguousarray(wpr.T)
    if kw.get("wq1") is not None:
        d["wq1"] = np.ascontiguousarray(kw["wq1"].T)
        d["wk1"] = np.ascontiguousarray(kw["wk1"].T)
        d["wv1"] = np.ascontiguousarray(kw["wv1"].T)
    if kw.get("g") is not None:
        gg = kw["g"]
        core = kw["core"]
        d["slotq"] = np.ascontiguousarray(gg["L1_ffn_slot_query"][0].T)
        for nm, key in [("dwq", "L1_ffn_d_Wq"), ("dwk", "L1_ffn_d_Wk"),
                        ("dwv", "L1_ffn_d_Wv"), ("dwo", "L1_ffn_d_Wo"),
                        ("cwq", "L1_ffn_c_Wq"), ("cwk", "L1_ffn_c_Wk"),
                        ("cwv", "L1_ffn_c_Wv"), ("cwo", "L1_ffn_c_Wo")]:
            d[nm] = np.ascontiguousarray(gg[key])
        for nm, key in [("dbq", "L1_ffn_d_bq"), ("dbk", "L1_ffn_d_bk"),
                        ("dbo", "L1_ffn_d_bo"), ("cbq", "L1_ffn_c_bq"),
                        ("cbk", "L1_ffn_c_bk")]:
            d[nm] = np.ascontiguousarray(gg[key].reshape(8, 128).T)
        for nm, key in [("dbv", "L1_ffn_d_bv"), ("cbv", "L1_ffn_c_bv"),
                        ("cbo", "L1_ffn_c_bo")]:
            d[nm] = np.ascontiguousarray(gg[key].reshape(1, D))
        es = [2 * core, 2 * core + 1]
        d["ew1"] = np.ascontiguousarray(gg["L1_ffn_e_W1"][es].reshape(2048, FF))
        d["ewg"] = np.ascontiguousarray(gg["L1_ffn_e_Wg"][es].reshape(2048, FF))
        d["ew2"] = np.ascontiguousarray(gg["L1_ffn_e_W2"][es].reshape(8192, D))
        d["eb1"] = np.ascontiguousarray(gg["L1_ffn_e_b1"][es].reshape(1, 8192))
        d["ebg"] = np.ascontiguousarray(gg["L1_ffn_e_bg"][es].reshape(1, 8192))
        d["eb2"] = np.ascontiguousarray(gg["L1_ffn_e_b2"][es].reshape(1, 2048))
        d["elng"] = np.ascontiguousarray(
            gg["L1_ffn_e_ln_g"][es].reshape(2, 8, 128).transpose(2, 0, 1).reshape(128, 16))
        d["elnb"] = np.ascontiguousarray(
            gg["L1_ffn_e_ln_b"][es].reshape(2, 8, 128).transpose(2, 0, 1).reshape(128, 16))
        sel = np.zeros((128, 32), dtype=np.float32)
        for e in range(2):
            for bb in range(8):
                sel[16 * bb + 2 * core + e, 8 * e + bb] = 1.0
        for s_ in range(16):
            sel[16 * (s_ // 2) + 8 * (s_ % 2) + core, 16 + s_] = 1.0
        d["selT"] = sel
    for k in CONST_KEYS:
        d[k] = consts[k]
    return d


_BUILD_CACHE = {}


def _get_nc():
    if "nc" not in _BUILD_CACHE:
        b = Build(upto="full2")
        _BUILD_CACHE["nc"] = b.build()
        _BUILD_CACHE["consts"] = host_constants()
    return _BUILD_CACHE["nc"], _BUILD_CACHE["consts"]


def kernel(**inputs):
    from concourse.bass_utils import run_bass_kernel_spmd

    g = {k: np.asarray(v) for k, v in inputs.items()}
    nc, consts = _get_nc()
    in_maps = []
    for c in range(8):
        d = stage_inputs_core(
            g["L0_attn_x"][c], g["L0_attn_w_qkv"], g["L0_attn_w_o"],
            g["L0_attn_log_xi"], consts, g["L0_ffn_W_fc"], g["L0_ffn_W_proj"],
            wq1=g["L1_attn_WQ"], wk1=g["L1_attn_WK"], wv1=g["L1_attn_WV"],
            g=g, core=c)
        in_maps.append(d)
    res = run_bass_kernel_spmd(nc, in_maps, core_ids=list(range(8)))
    out = np.stack([np.ascontiguousarray(res.results[c]["out"].T) for c in range(8)], axis=0)
    return out.astype(np.float32)


# version 05 (final)
# speedup vs baseline: 1.0811x; 1.0811x over previous
import numpy as np
import ml_dtypes
from contextlib import ExitStack

import concourse.bass as bass
import concourse.mybir as mybir

F32 = mybir.dt.float32
BF16 = mybir.dt.bfloat16
AF = mybir.ActivationFunctionType
ALU = mybir.AluOpType
AXX = mybir.AxisListType.X

D, N, NH, DH, NCH, NTOK, FF = 1024, 2048, 16, 64, 8, 16, 4096
HEAD_ROPE = [(h % 4) in (2, 3) for h in range(NH)]
HEAD_DECAY = [(h % 4) in (1, 3) for h in range(NH)]


class Sem:
    def __init__(self, handle):
        self.h = handle
        self.v = 0

    def inc(self, instr, n=1):
        instr.then_inc(self.h, n)
        self.v += n
        return (self, self.v)

    def tok(self):
        return (self, self.v)


WAITED = {}


def wait(eng, *toks):
    best = {}
    for tk in toks:
        if tk is None:
            continue
        s, v = tk
        if v > best.get(id(s), (s, -1))[1]:
            best[id(s)] = (s, v)
    kind = type(eng).__name__
    for s, v in best.values():
        if v > 0:
            key = (kind, s)
            if WAITED.get(key, 0) >= v:
                continue
            eng.wait_ge(s.h, v)
            WAITED[key] = v


def ap(t, poff, pcnt, fdims, off=0):
    prow = int(t.shape[1])
    return bass.AP(t, poff * prow + off, [[prow, pcnt]] + [list(d) for d in fdims])


class B:
    def __init__(self, bl):
        self.bl = bl

    def t(self, f):
        self.bl.tensor(f)

    def v(self, f):
        self.bl.vector(f)

    def s(self, f):
        self.bl.scalar(f)

    def g(self, f):
        self.bl.gpsimd(f)

    def y(self, f):
        self.bl.sync(f)


class MK:
    def __init__(self, nc):
        self.nc = nc
        self.es = ExitStack()
        self._names = {}

    def name(self, base):
        i = self._names.get(base, 0)
        self._names[base] = i + 1
        return f"{base}_{i}"

    def sem(self, name=None):
        return Sem(self.es.enter_context(self.nc.semaphore(self.name(name or "s"))))

    def sb(self, name, shape, dtype, scope=None):
        return (scope or self.es).enter_context(self.nc.sbuf_tensor(self.name(name), shape, dtype))

    def ps(self, name, shape, dtype=F32):
        return self.es.enter_context(self.nc.psum_tensor(self.name(name), shape, dtype))

    def block(self, name):
        if getattr(self, "_blk", None) is not None:
            self._blk.close()
        self._blk = ExitStack()
        return B(self._blk.enter_context(self.nc.Block(self.name(name), no_gpsimd_drain=True)))

    def end_blocks(self):
        if getattr(self, "_blk", None) is not None:
            self._blk.close()
            self._blk = None



class PsumPool:

    def __init__(self, mk, n=4):
        self.slots = [mk.ps("pool", [128, 512]) for _ in range(n)]
        self.freed = [(i, None) for i in range(n)]
        self.held = {}

    def acquire(self):
        if not self.freed:
            raise RuntimeError("psum pool exhausted")
        i, tok = self.freed.pop(0)
        return i, self.slots[i], tok

    def release(self, i, consumer_tok):
        self.freed.append((i, consumer_tok))


def host_constants():
    c = {}
    c["ones_row"] = np.ones((1, 128), dtype=ml_dtypes.bfloat16)
    c["onescol_f32"] = np.ones((1, 128), dtype=np.float32)
    c["ones_colbf"] = np.ones((128, 1), dtype=ml_dtypes.bfloat16)
    c["identf32"] = np.eye(128, dtype=np.float32)
    q = np.arange(128)
    c["tri"] = (q[None, :] >= q[:, None]).astype(ml_dtypes.bfloat16)
    pos = np.arange(N, dtype=np.float64)
    inv = 10000.0 ** (-(np.arange(0, DH, 2, dtype=np.float64) / DH))
    th = pos[None, :] * inv[:, None]
    cos, sin = np.cos(th), np.sin(th)
    C2 = np.empty((128, N), dtype=np.float64)
    S2 = np.empty((128, N), dtype=np.float64)
    for blk in (0, 64):
        C2[blk + 0:blk + 64:2] = cos
        C2[blk + 1:blk + 64:2] = cos
        S2[blk + 0:blk + 64:2] = -sin
        S2[blk + 1:blk + 64:2] = sin
    c["C2"] = C2.astype(ml_dtypes.bfloat16)
    c["S2"] = S2.astype(ml_dtypes.bfloat16)
    E = np.zeros((128, 128), dtype=np.float32)
    for ch in range(8):
        E[0:64, ch * 16 + 2 * ch] = 1.0
        E[64:128, ch * 16 + 2 * ch + 1] = 1.0
    c["E_pack"] = E.astype(ml_dtypes.bfloat16)
    c["erow"] = (np.arange(NH) % 2).astype(np.float32).reshape(1, NH)
    c["ones128"] = np.ones((128, 128), dtype=ml_dtypes.bfloat16)
    c["identbf"] = np.eye(128, dtype=ml_dtypes.bfloat16)
    return c


def seg_list(k0):
    segs = [(k0, k0 + 128, True)]
    b = k0 + 128
    while b < N:
        e = min(((b // 512) + 1) * 512, N)
        segs.append((b, e, False))
        b = e
    return segs


class Build:
    def __init__(self, upto="l0"):
        WAITED.clear()
        self.nc = bass.Bass("TRN2", target_bir_lowering=False,
                            detect_race_conditions=False)
        self.mk = MK(self.nc)
        self.upto = upto

    def inp(self, name, shape, dtype):
        return self.nc.dram_tensor(name, shape, dtype, kind="ExternalInput")

    def reg_dbg(self, name, t):
        if not getattr(self, "want_dbg", False):
            return
        if not hasattr(self, "dbg"):
            self.dbg = {}
        self.dbg[name] = t

    def dump_dbg(self):
        if not getattr(self, "dbg", None):
            return
        b = self.mk.block("dbg")

        def y(sy):
            wait(sy, self.sT.tok(), self.sV.tok(), self.sS.tok(), self.sG.tok())
            last = None
            for nm, t in self.dbg.items():
                shp = [int(s) for s in t.shape]
                dt = t.dtype
                o = self.nc.dram_tensor("dbg_" + nm, shp, dt, kind="ExternalOutput")
                last = self.sO.inc(sy.dma_start(out=o.ap()[:, :], in_=t[:, :]), 16)
            wait(sy, last)
        b.y(y)

    def declare(self):
        mk = self.mk
        self.x_in = self.inp("x", [D, N], F32)
        self.wqkv_in = self.inp("wqkv", [D, 3 * D], F32)
        self.wo_in = self.inp("wo", [D, D], F32)
        self.logxi_in = self.inp("logxi", [1, NH], F32)
        self.wfc_in = self.inp("wfc", [D, FF], F32)
        self.wpr_in = self.inp("wpr", [FF, D], F32)
        self.wq1_in = self.inp("wq1", [D, D], F32)
        self.wk1_in = self.inp("wk1", [D, D], F32)
        self.wv1_in = self.inp("wv1", [D, D], F32)
        self.slotq_in = self.inp("slotq", [D, 16], F32)
        self.dwq_in = self.inp("dwq", [D, D], F32)
        self.dwk_in = self.inp("dwk", [D, D], F32)
        self.dwv_in = self.inp("dwv", [D, D], F32)
        self.dwo_in = self.inp("dwo", [D, D], F32)
        self.dbq_in = self.inp("dbq", [128, 8], F32)
        self.dbk_in = self.inp("dbk", [128, 8], F32)
        self.dbo_in = self.inp("dbo", [128, 8], F32)
        self.dbv_in = self.inp("dbv", [1, D], F32)
        self.selT_in = self.inp("selT", [128, 32], F32)
        self.ew1_in = self.inp("ew1", [2048, FF], F32)
        self.ewg_in = self.inp("ewg", [2048, FF], F32)
        self.ew2_in = self.inp("ew2", [8192, D], F32)
        self.eb1_in = self.inp("eb1", [1, 8192], F32)
        self.ebg_in = self.inp("ebg", [1, 8192], F32)
        self.eb2_in = self.inp("eb2", [1, 2048], F32)
        self.elng_in = self.inp("elng", [128, 16], F32)
        self.elnb_in = self.inp("elnb", [128, 16], F32)
        self.cwq_in = self.inp("cwq", [D, D], F32)
        self.cwk_in = self.inp("cwk", [D, D], F32)
        self.cwv_in = self.inp("cwv", [D, D], F32)
        self.cwo_in = self.inp("cwo", [D, D], F32)
        self.cbq_in = self.inp("cbq", [128, 8], F32)
        self.cbk_in = self.inp("cbk", [128, 8], F32)
        self.cbv_in = self.inp("cbv", [1, D], F32)
        self.cbo_in = self.inp("cbo", [1, D], F32)
        self.cin = {}
        for nm, shp, dt in [("ones_row", [1, 128], BF16),
                            ("ones_colbf", [128, 1], BF16),
                            ("tri", [128, 128], BF16), ("C2", [128, N], BF16),
                            ("S2", [128, N], BF16), ("E_pack", [128, 128], BF16),
                            ("ones128", [128, 128], BF16), ("identbf", [128, 128], BF16),
                            ("erow", [1, NH], F32)]:
            self.cin[nm] = self.inp(nm, shp, dt)
        self.out_t = self.nc.dram_tensor("out", [D, N], F32, kind="ExternalOutput")

        self.X = [mk.sb("X", [128, N], BF16) for _ in range(NCH)]
        self.C = {}
        for nm in self.cin:
            if nm in ("C2", "S2"):
                continue
            dt = {"erow": F32}.get(nm, BF16)
            self.C[nm] = mk.sb("c_" + nm, list(self.cin[nm].shape), dt)
        self.r_scr = mk.sb("r_scr", [128, N], BF16)
        self.stat = mk.sb("stat", [1, 3072], F32)
        self.statP = mk.sb("statP", [16, 16], F32)
        self.statPbf = mk.sb("statPbf", [16, 4], BF16)
        self.rowbf = mk.sb("rowbf", [1, 64], BF16)
        self.rowf = mk.sb("rowf", [1, 128], F32)
        self.rb = [mk.sb("rb", [128, 512], BF16) for _ in range(2)]
        self.xsq = [mk.sb("xsq", [128, 512], BF16) for _ in range(2)]

        self.sDy = mk.sem("sDy")
        self.sDs = mk.sem("sDs")
        self.sDg = mk.sem("sDg")
        self.sT = mk.sem("sT")
        self.sV = mk.sem("sV")
        self.sS = mk.sem("sS")
        self.sG = mk.sem("sG")
        self.sO = mk.sem("sO")
        self.x_ready = [None] * NCH

    def phase_load(self):
        b = self.mk.block("ld")

        def v0(v):
            v.memset(self.stat[:, :], 0.0)
            v.memset(self.statP[:, :], 0.0)
            v.memset(self.rowbf[:, :], 0.0)
            v.memset(self.rowf[:, :], 0.0)
            v.memset(self.r_scr[:, :], 0.0)
            self.tok_hyg = self.sV.inc(v.memset(self.statP[0:1, 4:5], 1e-5))
        b.v(v0)

        def g(g):
            for c in range(NCH):
                self.tok_X = self.sDg.inc(g.dma_start(out=self.X[c][:, :], in_=self.x_in.ap()[128 * c:128 * (c + 1), :]), 16)
        b.g(g)

        def y(sy):
            for nm in self.cin:
                if nm in ("C2", "S2"):
                    continue
                self.sDy.inc(sy.dma_start(out=self.C[nm][:, :], in_=self.cin[nm].ap()[:, :]), 16)
            wait(sy, self.tok_hyg)
            self.tok_const = self.sDy.inc(sy.dma_start(out=self.rowf[:, 0:16], in_=self.logxi_in.ap()[:, :]), 16)
        b.y(y)
        for c in range(NCH):
            self.x_ready[c] = self.tok_X


    def bcast_row(self, b, row_f32, n, dst, deps, tag, final_scale=1.0):
        mk, pool = self.mk, self.pool
        sT, sV, sS = self.sT, self.sV, self.sS
        rb = self.rowbf

        def v(v):
            wait(v, *deps)
            v.tensor_copy(rb[0:1, 0:n], row_f32)
            v.drain()
            v.scalar_tensor_tensor(self.rowf[:, 80:80 + n], rb[0:1, 0:n], -1.0,
                                   row_f32, ALU.mult, ALU.add)
            v.drain()
            i = v.tensor_copy(rb[0:1, 32:32 + n], self.rowf[:, 80:80 + n])
            self._bc_tok = sV.inc(i)
        b.v(v)
        pi, ps, ptok = pool.acquire()

        def t(t):
            wait(t, self._bc_tok, ptok)
            t.matmul(ps[:, 0:n], self.C["ones_row"][:, :], rb[0:1, 0:n], start=True, stop=False)
            i = t.matmul(ps[:, 0:n], self.C["ones_row"][:, :], rb[0:1, 32:32 + n], start=False, stop=True)
            self._bc_mm = sT.inc(i)
        b.t(t)

        def s(v):
            wait(v, self._bc_mm)
            i = v.tensor_scalar_mul(dst[:, 0:n], ps[:, 0:n], final_scale)
            v.drain()
            self._bc_done = sV.inc(i)
        b.v(s)
        pool.release(pi, self._bc_done)
        return self._bc_done

    def l0_attn(self):
        mk = self.mk
        sT, sV, sS = self.sT, self.sV, self.sS
        pool = self.pool
        sc0 = self.l0_scope = ExitStack()
        _sb = mk.sb
        mk_sb = lambda nm, shp, dt: _sb(nm, shp, dt, scope=sc0)
        Dbig = mk_sb("Dbig", [128, N + 128], F32)
        self.C["C2"] = mk_sb("c_C2", [128, N], BF16)
        self.C["S2"] = mk_sb("c_S2", [128, N], BF16)
        c_all = mk_sb("c_all", [128, 16], F32)
        negB = mk_sb("negB", [128, 16], F32)
        v_tm = [mk_sb("v_tm", [128, 1040], BF16) for _ in range(NTOK)]
        wv = None

        wqk = [[mk_sb("wqk", [128, 512], BF16) for _ in range(NCH)] for _ in range(2)]
        q_sb = [mk_sb("q_sb", [128, N], BF16) for _ in range(2)]
        k_sb = [mk_sb("k_sb", [128, N], BF16) for _ in range(2)]
        swp = mk_sb("swp", [128, N], BF16)
        wv_scope = ExitStack()
        _sbwv = lambda nm, shp, dt: wv_scope.enter_context(self.nc.sbuf_tensor(mk.name(nm), shp, dt))
        Q2 = mk_sb("Q2", [16, N], BF16)
        K2 = mk_sb("K2", [16, N], BF16)
        P_T = [mk_sb("P_T", [128, N], BF16) for _ in range(2)]
        wv = [_sbwv("wv", [128, D], BF16) for _ in range(NCH)]

        st = self.stat

        b = mk.block("l0prep")

        def g(g):
            g.iota(Dbig[:, :], [[1, N + 128]], base=-128, channel_multiplier=-1,
                   allow_small_or_imprecise_dtypes=True)
            self.tok_iota = self.sG.inc(g.engine_nop())
            for c in range(NCH):
                self.tok_wv = self.sDg.inc(g.dma_start(out=wv[c][:, :], in_=self.wqkv_in.ap()[128 * c:128 * (c + 1), 2048:3072]), 16)
            for c in range(NCH):
                self.sDg.inc(g.dma_start(out=wqk[0][c][:, 0:256], in_=self.wqkv_in.ap()[128 * c:128 * (c + 1), 0:256]), 16)
                tk = self.sDg.inc(g.dma_start(out=wqk[0][c][:, 256:512], in_=self.wqkv_in.ap()[128 * c:128 * (c + 1), 1024:1280]), 16)
            self.tok_wqk = [tk, None, None, None]
        b.g(g)

        def yld(sy):
            self.sDy.inc(sy.dma_start(out=self.C["C2"][:, :], in_=self.cin["C2"].ap()[:, :]), 16)
            self.tok_cs2 = self.sDy.inc(sy.dma_start(out=self.C["S2"][:, :], in_=self.cin["S2"].ap()[:, :]), 16)
        b.y(yld)

        def s(s):
            wait(s, self.tok_const)
            self.tok_xi = sS.inc(s.activation(out=self.rowf[:, 16:32], in_=self.rowf[:, 0:16], func=AF.Exp))
        b.s(s)

        def v(v):
            wait(v, self.tok_xi)
            v.reciprocal(self.rowf[:, 32:48], self.rowf[:, 16:32])
            v.drain()
            i = v.scalar_tensor_tensor(self.rowf[:, 48:64], self.rowf[:, 32:48], -1.0,
                                       self.C["erow"][:, :], ALU.mult, ALU.mult)
            self.tok_crow = sV.inc(i)
            v.memset(Q2[:, :], 0.0)
            v.memset(P_T[0][:, :], 0.0)
            v.memset(P_T[1][:, :], 0.0)
            self.tok_q2z = sV.inc(v.memset(K2[:, :], 0.0))
        b.v(v)

        self.tok_call = self.bcast_row(b, self.rowf[:, 48:64], 16, c_all, [self.tok_crow, self.tok_const], "call")

        b = mk.block("l0v")
        ev_toks, mm_toks = [], []
        for tt in range(NTOK):
            for half in range(2):
                pi, psv, ptok = pool.acquire()

                def t(t, tt=tt, half=half, psv=psv, ptok=ptok):
                    wait(t, self.tok_wv, self.tok_X, ptok)
                    i = None
                    for c in range(NCH):
                        i = t.matmul(psv[:, :], self.X[c][:, 128 * tt:128 * (tt + 1)],
                                     wv[c][:, 512 * half:512 * (half + 1)],
                                     start=(c == 0), stop=(c == NCH - 1))
                    mm_toks.append(sT.inc(i))
                b.t(t)

                def v(v, tt=tt, half=half, psv=psv):
                    wait(v, mm_toks[-1])
                    src = ap(psv, 0, 128, [[64, 8], [1, 64]])
                    dst = ap(v_tm[tt], 0, 128, [[65, 8], [1, 64]], off=65 * 8 * half)
                    ev_toks.append(sV.inc(v.tensor_copy(dst, src)))
                b.v(v)
                pool.release(pi, ev_toks[-1])

            def v2(v, tt=tt):
                i = v.memset(ap(v_tm[tt], 0, 128, [[65, 16], [1, 1]], off=64), 1.0)
                self.tok_vones = sV.inc(i)
            b.v(v2)

        wv_scope.close()
        attn = [mk_sb("attn", [128, N], BF16) for _ in range(NCH)]
        wo_ch = [mk_sb("wo_ch", [128, D], BF16) for _ in range(NCH)]

        b = mk.block("l0at")
        ps_o = self.ps_big

        evac_q = []
        exp_cnt = [0]
        pv_done = []
        pv_bank_ev = [None] * 4
        last_score = [None]
        wqk_used = [None, None]
        self.tok_wo = None

        for grp in range(4):
            wbuf = grp % 2
            if grp < 3:
                def g(g, grp=grp):
                    if grp >= 1:
                        wait(g, wqk_used[(grp + 1) % 2])
                    for c in range(NCH):
                        self.sDg.inc(g.dma_start(out=wqk[(grp + 1) % 2][c][:, 0:256],
                                        in_=self.wqkv_in.ap()[128 * c:128 * (c + 1), 256 * (grp + 1):256 * (grp + 2)]), 16)
                        tk = self.sDg.inc(g.dma_start(out=wqk[(grp + 1) % 2][c][:, 256:512],
                                        in_=self.wqkv_in.ap()[128 * c:128 * (c + 1), 1024 + 256 * (grp + 1):1280 + 256 * (grp + 1)]), 16)
                    self.tok_wqk[grp + 1] = tk
                b.g(g)
            else:
                def g(g):
                    for c in range(NCH):
                        self.tok_wo = self.sDg.inc(g.dma_start(out=wo_ch[c][:, :], in_=self.wo_in.ap()[128 * c:128 * (c + 1), :]), 16)
                b.g(g)

            if self.upto == 'v':
                continue
            q2i = [None, None]
            for side in range(2):
                dst = q_sb if side == 0 else k_sb
                sq_dst = Q2 if side == 0 else K2
                for ts in range(4):
                    for qc in range(2):
                        pi, psq, ptok = pool.acquire()

                        def t(t, grp=grp, side=side, ts=ts, qc=qc, psq=psq, ptok=ptok, wbuf=wbuf):
                            wait(t, self.tok_wqk[grp], self.tok_X, ptok)
                            i = None
                            for c in range(NCH):
                                i = t.matmul(psq[:, :],
                                             wqk[wbuf][c][:, 256 * side + 128 * qc:256 * side + 128 * qc + 128],
                                             self.X[c][:, 512 * ts:512 * (ts + 1)],
                                             start=(c == 0), stop=(c == NCH - 1))
                            mm_toks.append(sT.inc(i))
                            wqk_used[wbuf] = mm_toks[-1]
                        b.t(t)

                        if qc == 0:
                            q2i[ts % 2] = pool.acquire()

                        def s(v, grp=grp, side=side, ts=ts, qc=qc, psq=psq):
                            wait(v, mm_toks[-1])
                            if grp >= 1:
                                wait(v, last_score[0])
                            scale = 0.125 if side == 0 else 1.0
                            sl = dst[qc][:, 512 * ts:512 * (ts + 1)]
                            v.tensor_scalar_mul(sl, psq[:, :], scale)
                            i = v.tensor_tensor(self.xsq[qc][:, :], sl, sl, ALU.mult)
                            evac_q.append(self.sV.inc(i))
                        b.v(s)

                        def t2(t, grp=grp, ts=ts, qc=qc):
                            wait(t, evac_q[-1])
                            if qc == 0:
                                wait(t, q2i[ts % 2][2])
                            gc = 2 * grp + qc
                            i = t.matmul(q2i[ts % 2][1][0:16, :], self.C["E_pack"][:, 16 * gc:16 * gc + 16],
                                         self.xsq[qc][:, :], start=(qc == 0), stop=(qc == 1))
                            mm_toks.append(sT.inc(i))
                        b.t(t2)
                        pool.release(pi, evac_q[-1])

                    def v(v, side=side, ts=ts, sq_dst=sq_dst):
                        wait(v, mm_toks[-1], self.tok_q2z)
                        i = v.tensor_tensor(sq_dst[:, 512 * ts:512 * (ts + 1)],
                                            sq_dst[:, 512 * ts:512 * (ts + 1)],
                                            q2i[ts % 2][1][0:16, :], ALU.add)
                        self.tok_q2a = sV.inc(i)
                    b.v(v)
                    pool.release(q2i[ts % 2][0], self.tok_q2a)

            def v(v, grp=grp):
                wait(v, evac_q[-1], self.tok_cs2)
                i = None
                for t_sb in (q_sb[1], k_sb[1]):
                    v.stream_shuffle(swp[:, :], t_sb[:, :], [i0 ^ 1 for i0 in range(32)])
                    v.drain()
                    v.tensor_tensor(t_sb[:, :], t_sb[:, :], self.C["C2"][:, :], ALU.mult)
                    v.tensor_tensor(swp[:, :], swp[:, :], self.C["S2"][:, :], ALU.mult)
                    i = v.tensor_tensor(t_sb[:, :], t_sb[:, :], swp[:, :], ALU.add)
                self.tok_rope = sV.inc(i)
            b.v(v)

            def v(v, grp=grp):
                wait(v, self.tok_q2a)
                v.drain()
                v.reduce_max(self.statP[:, 0:1], Q2[:, :], axis=AXX)
                v.reduce_max(self.statP[:, 1:2], K2[:, :], axis=AXX)
                v.drain()
                i = v.tensor_tensor(self.statP[:, 2:3], self.statP[:, 0:1], self.statP[:, 1:2], ALU.mult)
                self.tok_bmax = sV.inc(i)
            b.v(v)

            def s(s, grp=grp):
                wait(s, self.tok_bmax)
                if grp >= 1:
                    wait(s, self.tok_brow)
                self.tok_bsq = sS.inc(s.activation(out=self.statP[:, 3:4], in_=self.statP[:, 2:3], func=AF.Sqrt))
            b.s(s)

            def vbf(v, grp=grp):
                wait(v, self.tok_bsq)
                self.tok_bbf = sV.inc(v.tensor_copy(self.statPbf[:, 0:1], self.statP[:, 3:4]))
            b.v(vbf)
            bi, ps_bx, btok = pool.acquire()

            def tmt(t, grp=grp, ps_bx=ps_bx, btok=btok):
                wait(t, self.tok_bbf, btok)
                i = t.matmul(ps_bx[0:1, 0:16], self.statPbf[:, 0:1],
                             self.C["identbf"][0:16, 0:16], start=True, stop=True)
                self.tok_btr = sT.inc(i)
            b.t(tmt)

            def scp(s, grp=grp, ps_bx=ps_bx):
                wait(s, self.tok_btr)
                self.tok_brow = sS.inc(s.activation(out=self.rowf[:, 64:80], in_=ps_bx[0:1, 0:16],
                                                    func=AF.Copy))
            b.s(scp)
            pool.release(bi, self.tok_brow)
            self.tok_negB = self.bcast_row(b, self.rowf[:, 64:80], 16, negB,
                                           [self.tok_brow], "negB", final_scale=-1.0)

            for hl in (range(4) if self.upto not in ('v', 'qkv') else []):
                h = 4 * grp + hl
                qc, ro = hl // 2, 64 * (hl % 2)
                odd = h % 2 == 1
                rope_dep = self.tok_rope if HEAD_ROPE[h] else None
                orow = 64 if odd else 0
                zrow = 32 if odd else 64

                pend_pv = None
                for ki in range(NTOK):
                    k0 = 128 * ki
                    pp = ki % 2
                    if pend_pv is not None:
                        b.t(pend_pv)
                        pend_pv = None
                    m = len(pv_done)
                    seg_exp = []
                    segs = seg_list(k0)
                    first_batch = True
                    for bs in range(0, len(segs), 2):
                        batch = []
                        for (qs, qe, diag) in segs[bs:bs + 2]:
                            pi, pss, ptok = pool.acquire()

                            def t(t, h=h, qc=qc, ro=ro, k0=k0, qs=qs, qe=qe, pss=pss, ptok=ptok,
                                  rope_dep=rope_dep):
                                wait(t, evac_q[-1], rope_dep, self.tok_negB, self.tok_call, ptok)
                                i = t.matmul(pss[:, 0:qe - qs],
                                             k_sb[qc][ro:ro + 64, k0:k0 + 128],
                                             q_sb[qc][ro:ro + 64, qs:qe], start=True, stop=True)
                                mm_toks.append(sT.inc(i))
                                last_score[0] = mm_toks[-1]
                            b.t(t)
                            dec = [None]
                            if HEAD_DECAY[h]:
                                def v(v, h=h, k0=k0, qs=qs, qe=qe, pss=pss, dec=dec, mmtk=mm_toks[-1]):
                                    wait(v, mmtk, self.tok_iota, self.tok_call)
                                    i = v.scalar_tensor_tensor(pss[:, 0:qe - qs],
                                                               Dbig[:, qs - k0 + 128:qe - k0 + 128],
                                                               c_all[:, h:h + 1],
                                                               pss[:, 0:qe - qs], ALU.mult, ALU.add)
                                    dec[0] = sV.inc(i)
                                b.v(v)
                            batch.append((pi, pss, qs, qe, diag, mm_toks[-1], dec))

                        def s(s, h=h, pp=pp, m=m, first=first_batch, batch=batch):
                            wait(s, self.tok_negB,
                                 *[it[5] for it in batch], *[it[6][0] for it in batch])
                            if first and m >= 2:
                                wait(s, pv_done[m - 2])
                            i = None
                            for (pi, pss, qs, qe, diag, mmtk, dec) in batch:
                                i = s.activation(out=P_T[pp][:, qs:qe], in_=pss[:, 0:qe - qs],
                                                 func=AF.Exp, bias=negB[:, h:h + 1], scale=1.0)
                            seg_exp.append(sS.inc(i))
                        b.s(s)
                        for (pi, pss, qs, qe, diag, mmtk, dec) in batch:
                            pool.release(pi, seg_exp[-1])
                        first_batch = False
                        for (pi, pss, qs, qe, diag, mmtk, dec) in batch:
                            if diag:
                                def v(v, qs=qs, qe=qe, pp=pp, seg_exp=seg_exp):
                                    wait(v, seg_exp[-1])
                                    i = v.tensor_tensor(P_T[pp][:, qs:qe], P_T[pp][:, qs:qe],
                                                        self.C["tri"][:, 0:qe - qs], ALU.mult)
                                    seg_exp.append(sV.inc(i))
                                b.v(v)

                    def tpv(t, h=h, ki=ki, k0=k0, pp=pp, seg_exp=seg_exp, odd=odd, orow=orow, zrow=zrow):
                        wait(t, *seg_exp)
                        if ki == 0:
                            wait(t, self.tok_vones, *[tk for tk in pv_bank_ev if tk])
                            if odd:
                                wait(t, self.tok_recip)
                        i = None
                        for j in range(4):
                            qs2, qe2 = max(k0, 512 * j), 512 * (j + 1)
                            if qs2 >= qe2:
                                continue
                            last = min(NTOK - 1, 4 * j + 3) == ki
                            M = 64 if odd else 65
                            i = t.matmul(ps_o[orow:orow + M, qs2:qe2],
                                         v_tm[ki][:, 65 * h:65 * h + M],
                                         P_T[pp][:, qs2:qe2], start=(ki == 0), stop=last)
                            if odd:
                                i = t.matmul(ps_o[zrow:zrow + 1, qs2:qe2],
                                             self.C["ones_colbf"][:, :],
                                             P_T[pp][:, qs2:qe2], start=(ki == 0), stop=last)
                        pv_done.append(sT.inc(i))
                    pend_pv = tpv
                b.t(pend_pv)

                def v(v, h=h, zrow=zrow):
                    wait(v, pv_done[-1])
                    i = v.reciprocal(self.r_scr[zrow:zrow + 1, :], ps_o[zrow:zrow + 1, :])
                    self.tok_recip = sV.inc(i)
                b.v(v)
                for j in range(4):
                    ri, ps_rx, rtok = pool.acquire()

                    def t(t, h=h, j=j, ps_rx=ps_rx, rtok=rtok, orow=orow, zrow=zrow):
                        wait(t, self.tok_recip, rtok)
                        i = t.matmul(ps_rx[orow:orow + 64, :],
                                     self.C["ones128"][zrow:zrow + 1, 0:64],
                                     self.r_scr[zrow:zrow + 1, 512 * j:512 * (j + 1)],
                                     start=True, stop=True)
                        self.tok_rbmm = sT.inc(i)
                    b.t(t)

                    def s(v, h=h, j=j, ps_rx=ps_rx, orow=orow):
                        wait(v, self.tok_rbmm)
                        i = v.tensor_copy(self.rb[j % 2][orow:orow + 64, :],
                                          ps_rx[orow:orow + 64, :])
                        self.tok_rbcp = self.sV.inc(i)
                    b.v(s)

                    def v(v, h=h, j=j, orow=orow):
                        wait(v, self.tok_rbcp)
                        i = v.tensor_tensor(attn[h // 2][orow:orow + 64, 512 * j:512 * (j + 1)],
                                            ps_o[orow:orow + 64, 512 * j:512 * (j + 1)],
                                            self.rb[j % 2][orow:orow + 64, :], ALU.mult)
                        tk = sV.inc(i)
                        pv_bank_ev[j] = tk
                        self.tok_attn_ev = tk
                    b.v(v)
                    pool.release(ri, self.tok_attn_ev)

        for ii in range(NCH):
            self.reg_dbg(f"attn{ii}", attn[ii])
        self.reg_dbg("negB", negB)
        self.reg_dbg("c_all", c_all)
        self.reg_dbg("Q2", Q2)
        self.reg_dbg("K2", K2)
        self.reg_dbg("q0", q_sb[0]); self.reg_dbg("q1", q_sb[1])
        self.reg_dbg("k0", k_sb[0]); self.reg_dbg("k1", k_sb[1])
        self.reg_dbg("v0", v_tm[0]); self.reg_dbg("v5", v_tm[5])
        self.reg_dbg("P0", P_T[0]); self.reg_dbg("P1", P_T[1])
        self.reg_dbg("rscr", self.r_scr)
        self.reg_dbg("statP", self.statP)
        self.reg_dbg("rowbf", self.rowbf)
        self.reg_dbg("statrow", self.stat)
        self.reg_dbg("rowf", self.rowf)
        self.reg_dbg("erowc", self.C["erow"])
        self.reg_dbg("onesrowc", self.C["ones_row"])

        if self.upto in ('v', 'qkv', 'heads'):
            if self.upto == 'heads':
                pass
            self.layernorm_skip = True
            return
        b = mk.block("l0wo")
        for dc in range(NCH):
            for ts in range(4):
                pi, psy, ptok = pool.acquire()

                def t(t, dc=dc, ts=ts, psy=psy, ptok=ptok):
                    wait(t, self.tok_wo, self.tok_attn_ev, ptok)
                    i = None
                    for r in range(NCH):
                        i = t.matmul(psy[:, :],
                                     wo_ch[r][:, 128 * dc:128 * (dc + 1)],
                                     attn[r][:, 512 * ts:512 * (ts + 1)],
                                     start=(r == 0), stop=(r == NCH - 1))
                    mm_toks.append(sT.inc(i))
                b.t(t)

                def v(v, dc=dc, ts=ts, psy=psy):
                    wait(v, mm_toks[-1])
                    sl = self.X[dc][:, 512 * ts:512 * (ts + 1)]
                    i = v.scalar_tensor_tensor(sl, psy[:, :], 1.0, sl, ALU.mult, ALU.add)
                    tk = sV.inc(i)
                    self.x_ready[dc] = tk
                b.v(v)
                pool.release(pi, self.x_ready[dc])

        if self.upto != 'wo':
            self.layernorm("ln1")
        self.mk.block("l0done")
        self.l0_scope.close()


    def l0_ffn(self):
        mk = self.mk
        sT, sV, sS = self.sT, self.sV, self.sS
        pool = self.pool
        sc = ExitStack()
        wfcb = [[mk.sb("wfcb", [128, 256], BF16, scope=sc) for _ in range(NCH)] for _ in range(2)]
        wpr = [mk.sb("wpr", [128, 1024], BF16, scope=sc) for _ in range(32)]
        h_sb = [mk.sb("h_sb", [128, 1024], BF16, scope=sc) for _ in range(32)]
        ymb = mk.sb("ymb", [128, 1024], F32, scope=sc)
        mem = mk.sb("mem", [128, 1024], F32, scope=sc)
        tmb = mk.sb("tmb", [128, 1024], F32, scope=sc)
        acc = mk.sb("acc", [128, 1024], F32, scope=sc)

        b = mk.block("ffn")
        wfc_tok = []
        relu_toks = []
        spike_toks = []
        ybank_free = None

        def gld(g):
            for f in range(32):
                self.tok_wpr = self.sDg.inc(g.dma_start(out=wpr[f][:, :],
                                                        in_=self.wpr_in.ap()[128 * f:128 * (f + 1), :]), 16)
        b.g(gld)

        for half in range(2):
            t0 = 1024 * half
            for f in range(32):
                par = (f // 2) % 2
                fo = 128 * (f % 2)
                if f % 2 == 0:
                    def g(g, half=half, f=f, par=par):
                        if len(wfc_tok) >= 2:
                            wait(g, wfc_tok[-2][1])
                        for dc in range(NCH):
                            i = g.dma_start(out=wfcb[par][dc][:, :],
                                            in_=self.wfc_in.ap()[128 * dc:128 * (dc + 1),
                                                                 128 * f:128 * f + 256])
                            self.sDg.inc(i, 16)
                        wfc_tok.append((self.sDg.tok(), None))
                    b.g(g)
                ldtok = self.sDg.tok()
                p1, ps1, tk1 = pool.acquire()
                p2, ps2, tk2 = pool.acquire()

                def t(t, half=half, f=f, par=par, fo=fo, ps1=ps1, ps2=ps2, tk1=tk1, tk2=tk2, ldtok=ldtok):
                    wait(t, ldtok, tk1, tk2, self.x_ready[0], self.x_ready[7])
                    for cs in range(2):
                        ps = ps1 if cs == 0 else ps2
                        i = None
                        for dc in range(NCH):
                            i = t.matmul(ps[:, :], wfcb[par][dc][:, fo:fo + 128],
                                         self.X[dc][:, 1024 * half + 512 * cs:1024 * half + 512 * (cs + 1)],
                                         start=(dc == 0), stop=(dc == NCH - 1))
                        self.tok_fc = sT.inc(i)
                    if f % 2 == 1:
                        wfc_tok[-1] = (wfc_tok[-1][0], self.tok_fc)
                b.t(t)

                def s(s, f=f, ps1=ps1, ps2=ps2):
                    wait(s, self.tok_fc)
                    if relu_toks:
                        wait(s, relu_toks[-1])
                    s.activation(out=self.xsq[0][:, :], in_=ps1[:, :], func=AF.Relu)
                    i = s.activation(out=self.xsq[1][:, :], in_=ps2[:, :], func=AF.Relu)
                    self.tok_relu = sS.inc(i)
                b.s(s)

                def v(v, f=f, ps1=ps1, ps2=ps2):
                    wait(v, self.tok_relu)
                    v.tensor_tensor(h_sb[f][:, 0:512], self.xsq[0][:, :], ps1[:, :], ALU.mult)
                    i = v.tensor_tensor(h_sb[f][:, 512:1024], self.xsq[1][:, :], ps2[:, :], ALU.mult)
                    relu_toks.append(sV.inc(i))
                b.v(v)
                pool.release(p1, relu_toks[-1])
                pool.release(p2, relu_toks[-1])

            for d in range(NCH):
                def t(t, half=half, d=d):
                    wait(t, relu_toks[-1], self.tok_wpr)
                    if ybank_free is not None:
                        wait(t, ybank_free)
                    i = None
                    for f in range(32):
                        for cs in range(2):
                            i = t.matmul(self.ps_big[:, 512 * cs:512 * (cs + 1)],
                                         wpr[f][:, 128 * d:128 * (d + 1)],
                                         h_sb[f][:, 512 * cs:512 * (cs + 1)],
                                         start=(f == 0), stop=(f == 31))
                    self.tok_proj = sT.inc(i)
                b.t(t)

                def v(v, half=half, d=d):
                    wait(v, self.tok_proj)
                    y = self.ps_big[:, 0:1024]
                    v.tensor_scalar_add(ymb[:, :], y, -0.9)
                    v.tensor_scalar_add(mem[:, :], y, 1.0)
                    v.tensor_single_scalar(acc[:, :], y, 1.0, ALU.is_gt)
                    for step in range(4):
                        v.scalar_tensor_tensor(tmb[:, :], mem[:, :], 0.9, ymb[:, :], ALU.mult, ALU.add)
                        v.scalar_tensor_tensor(mem[:, :], mem[:, :], 2.0, tmb[:, :], ALU.is_le, ALU.add)
                        v.scalar_tensor_tensor(acc[:, :], mem[:, :], 2.0, acc[:, :], ALU.is_gt, ALU.add)
                    sl = self.X[d][:, 1024 * half:1024 * (half + 1)]
                    i = v.scalar_tensor_tensor(sl, acc[:, :], 0.2, sl, ALU.mult, ALU.add)
                    tk = sV.inc(i)
                    spike_toks.append(tk)
                    self.x_ready[d] = tk
                b.v(v)
                ybank_free = spike_toks[-1]

        self.layernorm("ln2")
        self.mk.block("ffndone")
        sc.close()


    def l1_attn(self):
        mk = self.mk
        sT, sV, sS = self.sT, self.sV, self.sS
        pool = self.pool
        sc = ExitStack()
        q1 = [mk.sb("q1", [128, N], BF16, scope=sc) for _ in range(NCH)]
        k1 = [mk.sb("k1", [128, N], BF16, scope=sc) for _ in range(NCH)]
        v1 = [mk.sb("v1", [128, 1024], BF16, scope=sc) for _ in range(NTOK)]
        q1aug = mk.sb("q1aug", [1, N], BF16, scope=sc)
        P1a = [mk.sb("P1a", [128, 256], BF16, scope=sc) for _ in range(NTOK)]
        rbq = mk.sb("rbq", [128, 256], BF16, scope=sc)
        tmpq = mk.sb("tmpq", [128, 256], BF16, scope=sc)

        scw = ExitStack()
        wq1 = [mk.sb("wq1", [128, 1024], BF16, scope=scw) for _ in range(NCH)]
        wk1 = [mk.sb("wk1", [128, 1024], BF16, scope=scw) for _ in range(NCH)]
        b = mk.block("l1a")

        def gld(g):
            for c in range(NCH):
                self.sDg.inc(g.dma_start(out=wq1[c][:, :], in_=self.wq1_in.ap()[128 * c:128 * (c + 1), :]), 16)
            for c in range(NCH):
                self.sDg.inc(g.dma_start(out=wk1[c][:, :], in_=self.wk1_in.ap()[128 * c:128 * (c + 1), :]), 16)
        b.g(gld)
        wtok = self.sDg.tok()

        evq = []
        mmq = []
        for side in range(2):
            w_sb = wq1 if side == 0 else wk1
            dstq = q1 if side == 0 else k1
            scl = 1.0 / 32.0 if side == 0 else 1.0
            for ts in range(4):
                zp, psz, ztk = pool.acquire()
                for dc in range(NCH):
                    pi, ps, ptok = pool.acquire()

                    def t(t, side=side, ts=ts, dc=dc, ps=ps, ptok=ptok, w_sb=w_sb):
                        wait(t, wtok, ptok, self.x_ready[dc])
                        i = None
                        for c in range(NCH):
                            i = t.matmul(ps[:, :], w_sb[c][:, 128 * dc:128 * (dc + 1)],
                                         self.X[c][:, 512 * ts:512 * (ts + 1)],
                                         start=(c == 0), stop=(c == NCH - 1))
                        mmq.append(sT.inc(i))
                    b.t(t)

                    def s(v, side=side, ts=ts, dc=dc, ps=ps, scl=scl, dstq=dstq):
                        wait(v, mmq[-1])
                        if evq:
                            wait(v, self.tok_sqc)
                        sl = dstq[dc][:, 512 * ts:512 * (ts + 1)]
                        v.tensor_scalar_mul(sl, ps[:, :], scl)
                        i = v.tensor_tensor(self.xsq[dc % 2][:, :], sl, sl, ALU.mult)
                        evq.append(self.sV.inc(i))
                    b.v(s)

                    def t2(t, side=side, ts=ts, dc=dc, psz=psz, ztk=ztk):
                        wait(t, evq[-1])
                        if dc == 0:
                            wait(t, ztk)
                        i = t.matmul(psz[0:1, :], self.C["ones_colbf"][:, :], self.xsq[dc % 2][:, :],
                                     start=(dc == 0), stop=(dc == NCH - 1))
                        self.tok_sqmm = sT.inc(i)
                        self.tok_sqc = self.tok_sqmm
                    b.t(t2)
                    pool.release(pi, evq[-1])

                if side == 0:
                    def v(v, ts=ts, psz=psz):
                        wait(v, self.tok_sqmm)
                        self.tok_q2s = sV.inc(v.tensor_copy(self.stat[:, 512 * ts:512 * (ts + 1)], psz[0:1, :]))
                    b.v(v)
                    pool.release(zp, self.tok_q2s)
                else:
                    def v(v, ts=ts, psz=psz):
                        wait(v, self.tok_sqmm)
                        self.tok_k2m = sV.inc(v.reduce_max(self.rowf[0:1, 100 + ts:101 + ts], psz[0:1, :], axis=AXX))
                    b.v(v)
                    pool.release(zp, self.tok_k2m)

        def v(v):
            wait(v, self.tok_k2m, self.tok_q2s)
            v.drain()
            i = v.reduce_max(self.rowf[0:1, 104:105], self.rowf[0:1, 100:104], axis=AXX)
            self.tok_kmax = sV.inc(i)
        b.v(v)
        for ts in range(4):
            def v(v, ts=ts):
                wait(v, self.tok_kmax)
                v.drain()
                i = v.tensor_scalar_mul(self.stat[:, 512 * ts:512 * (ts + 1)],
                                        self.stat[:, 512 * ts:512 * (ts + 1)], self.rowf[0:1, 104:105])
                self.tok_bm = sV.inc(i)
            b.v(v)

            def s(s, ts=ts):
                wait(s, self.tok_bm)
                i = s.activation(out=self.stat[:, 512 * ts:512 * (ts + 1)],
                                 in_=self.stat[:, 512 * ts:512 * (ts + 1)], func=AF.Sqrt)
                s.drain()
                i = s.activation(out=q1aug[0:1, 512 * ts:512 * (ts + 1)],
                                 in_=self.stat[:, 512 * ts:512 * (ts + 1)], func=AF.Copy, scale=-1.0)
                self.tok_baug = sS.inc(i)
            b.s(s)

        self.mk.block("l1av")
        scw.close()
        scv = ExitStack()
        wv1 = [mk.sb("wv1", [128, 1024], BF16, scope=scv) for _ in range(NCH)]
        b = mk.block("l1av2")

        def gldv(g):
            for c in range(NCH):
                self.tok_wv1 = self.sDg.inc(g.dma_start(out=wv1[c][:, :], in_=self.wv1_in.ap()[128 * c:128 * (c + 1), :]), 16)
        b.g(gldv)
        for tt in range(NTOK):
            for half in range(2):
                pi, ps, ptok = pool.acquire()

                def t(t, tt=tt, half=half, ps=ps, ptok=ptok):
                    wait(t, self.tok_wv1, ptok, self.x_ready[0], self.x_ready[7])
                    i = None
                    for c in range(NCH):
                        i = t.matmul(ps[:, :], self.X[c][:, 128 * tt:128 * (tt + 1)],
                                     wv1[c][:, 512 * half:512 * (half + 1)],
                                     start=(c == 0), stop=(c == NCH - 1))
                    self.tok_v1 = sT.inc(i)
                b.t(t)

                def s(v, tt=tt, half=half, ps=ps):
                    wait(v, self.tok_v1)
                    i = v.tensor_copy(v1[tt][:, 512 * half:512 * (half + 1)], ps[:, :])
                    self.tok_v1e = self.sV.inc(i)
                b.v(s)
                pool.release(pi, self.tok_v1e)

        res3 = []
        for Q in range(8):
            q0 = 256 * Q
            nsl = 2 * Q + 2
            zp, psz, ztk = pool.acquire()
            exp_list = []
            seg_all = []
            for ki in range(nsl):
                k0 = 128 * ki
                pi, ps, ptok = pool.acquire()

                def t(t, Q=Q, q0=q0, ki=ki, k0=k0, ps=ps, ptok=ptok):
                    wait(t, self.tok_baug, evq[-1], ptok)
                    if Q >= 1:
                        wait(t, self.pv1_done[-1] if self.pv1_done else None)
                    i = None
                    for c in range(NCH):
                        i = t.matmul(ps[:, 0:256], k1[c][:, k0:k0 + 128],
                                     q1[c][:, q0:q0 + 256], start=(c == 0), stop=False)
                    i = t.matmul(ps[:, 0:256], self.C["ones128"][0:1, 0:128], q1aug[0:1, q0:q0 + 256],
                                 start=False, stop=True)
                    self.tok_sc1 = sT.inc(i)
                b.t(t)

                def s(s, Q=Q, ki=ki, ps=ps):
                    wait(s, self.tok_sc1)
                    i = s.activation(out=P1a[ki][:, :], in_=ps[:, 0:256], func=AF.Exp)
                    tk = sS.inc(i)
                    exp_list.append(tk)
                    seg_all.append(tk)
                b.s(s)
                pool.release(pi, exp_list[-1])

                if ki >= 2 * Q:
                    off = 0 if ki == 2 * Q else 128

                    def v(v, ki=ki, off=off):
                        wait(v, exp_list[-1])
                        if off == 128:
                            v.memset(P1a[ki][:, 0:128], 0.0)
                        i = v.tensor_tensor(P1a[ki][:, off:off + 128], P1a[ki][:, off:off + 128],
                                            self.C["tri"][:, 0:128], ALU.mult)
                        self.tok_msk1 = sV.inc(i)
                        seg_all.append(self.tok_msk1)
                    b.v(v)

            pv_toks = []
            for half in range(2):
                def t2(t, Q=Q, nsl=nsl, half=half, psz=psz, ztk=ztk, seg_all=seg_all):
                    wait(t, *seg_all)
                    if half == 0:
                        wait(t, ztk, self.tok_v1e, *res3[-8:])
                    else:
                        wait(t, self.pv1_done[-1])
                    i = None
                    for ki in range(nsl):
                        for oc in range(4):
                            i = t.matmul(self.ps_big[:, 512 * oc:512 * oc + 256],
                                         v1[ki][:, 128 * (4 * half + oc):128 * (4 * half + oc + 1)],
                                         P1a[ki][:, :], start=(ki == 0), stop=(ki == nsl - 1))
                        if half == 0:
                            i = t.matmul(psz[0:1, 0:256], self.C["ones_colbf"][:, :], P1a[ki][:, :],
                                         start=(ki == 0), stop=(ki == nsl - 1))
                    self.pv1_done.append(sT.inc(i))
                b.t(t2)

                if half == 0:
                    def v(v, Q=Q, psz=psz):
                        wait(v, self.pv1_done[-1])
                        i = v.reciprocal(self.r_scr[0:1, 0:256], psz[0:1, 0:256])
                        self.tok_r1 = sV.inc(i)
                    b.v(v)
                    pool.release(zp, self.tok_r1)
                    bi, psb, btok = pool.acquire()

                    def t3(t, Q=Q, psb=psb, btok=btok):
                        wait(t, self.tok_r1, btok)
                        i = t.matmul(psb[:, 0:256], self.C["ones_row"][:, :], self.r_scr[0:1, 0:256],
                                     start=True, stop=True)
                        self.tok_rb1 = sT.inc(i)
                    b.t(t3)

                    def s3(v, Q=Q, psb=psb):
                        wait(v, self.tok_rb1)
                        i = v.tensor_copy(rbq[:, :], psb[:, 0:256])
                        self.tok_rbc1 = self.sV.inc(i)
                    b.v(s3)
                    pool.release(bi, self.tok_rbc1)

                evs = []
                for oc4 in range(4):
                    oc = 4 * half + oc4

                    def v2(v, Q=Q, q0=q0, oc=oc, oc4=oc4):
                        wait(v, self.tok_rbc1, self.pv1_done[-1])
                        v.tensor_tensor(tmpq[:, :], self.ps_big[:, 512 * oc4:512 * oc4 + 256],
                                        rbq[:, :], ALU.mult)
                        sl = self.X[oc][:, q0:q0 + 256]
                        i = v.tensor_tensor(sl, sl, tmpq[:, :], ALU.add)
                        tk = sV.inc(i)
                        res3.append(tk)
                        evs.append(tk)
                        self.x_ready[oc] = tk
                    b.v(v2)
                self.pv1_done.append(evs[-1])

        self.mk.block("l1avdone")
        scv.close()
        self.layernorm("ln3")
        self.mk.block("l1adone")
        sc.close()


    def l1_ffn(self):
        mk = self.mk
        sT, sV, sS = self.sT, self.sV, self.sS
        pool = self.pool
        sc = ExitStack()

        o_fm = [mk.sb("o_fm", [128, 16], BF16, scope=sc) for _ in range(NCH)]
        o_tm = mk.sb("o_tm", [16, 1024], F32, scope=sc)
        y_sb = mk.sb("y_sb", [8, 1024], F32, scope=sc)
        dbc = mk.sb("dbc", [128, 48], F32, scope=sc)
        selT = mk.sb("selT", [128, 32], BF16, scope=sc)

        bo_in = self.nc.dram_tensor("bo_in", [16, 1024], F32)
        bo_out = self.nc.dram_tensor("bo_out", [128, 1024], F32)
        by_in = self.nc.dram_tensor("by_in", [16, 1024], F32)
        by_out = self.nc.dram_tensor("by_out", [128, 1024], F32)
        self.bo_in, self.bo_out, self.by_in, self.by_out = bo_in, bo_out, by_in, by_out
        self.o_fm, self.o_tm, self.y_sb, self.dbc, self.selT = o_fm, o_tm, y_sb, dbc, selT
        self.l1f_scope = sc

        scw = ExitStack()
        dwq = [mk.sb("dwq", [128, 1024], BF16, scope=scw) for _ in range(NCH)]
        dwk = [mk.sb("dwk", [128, 1024], BF16, scope=scw) for _ in range(NCH)]
        dwv = [mk.sb("dwv", [128, 1024], BF16, scope=scw) for _ in range(NCH)]
        dwo = [mk.sb("dwo", [128, 1024], BF16, scope=scw) for _ in range(NCH)]
        slotq = [mk.sb("slotq", [128, 16], BF16, scope=scw) for _ in range(NCH)]
        kd = [mk.sb("kd", [128, N], BF16, scope=scw) for _ in range(NCH)]
        vd = [mk.sb("vd", [128, 1040], BF16, scope=scw) for _ in range(NTOK)]
        qd = [mk.sb("qd", [128, 16], BF16, scope=scw) for _ in range(NCH)]
        dbv_row = mk.sb("dbv_row", [1, 1024], BF16, scope=scw)
        P_d = [mk.sb("P_d", [128, 256], BF16, scope=scw) for _ in range(2)]
        M_d = mk.sb("M_d", [16, 16], F32, scope=scw)
        M_dbf = mk.sb("M_dbf", [16, 16], BF16, scope=scw)
        qd_aug = mk.sb("qd_aug", [128, 256], BF16, scope=scw)
        attn_d = mk.sb("attn_d", [128, 128], BF16, scope=scw)
        rb128 = mk.sb("rb128", [128, 128], BF16, scope=scw)

        b = mk.block("dmha")

        def gld(g):
            for wsb, win in [(dwq, self.dwq_in), (dwk, self.dwk_in), (dwv, self.dwv_in),
                             (dwo, self.dwo_in), (slotq, self.slotq_in)]:
                for c in range(NCH):
                    self.sDg.inc(g.dma_start(out=wsb[c][:, :], in_=win.ap()[128 * c:128 * (c + 1), :]), 16)
            self.sDg.inc(g.dma_start(out=dbv_row[:, :], in_=self.dbv_in.ap()[:, :]), 16)
            self.tok_dld = self.sDg.inc(g.dma_start(out=selT[:, :], in_=self.selT_in.ap()[:, :]), 16)
        b.g(gld)
        wtok = self.sDg.tok()

        def yld(sy):
            self.sDy.inc(sy.dma_start(out=dbc[:, 24:32], in_=self.dbq_in.ap()[:, :]), 16)
            self.sDy.inc(sy.dma_start(out=dbc[:, 8:16], in_=self.dbk_in.ap()[:, :]), 16)
            self.tok_dbc = self.sDy.inc(sy.dma_start(out=dbc[:, 16:24], in_=self.dbo_in.ap()[:, :]), 16)
        b.y(yld)

        def v0(v):
            wait(v, self.tok_dbc)
            i = v.tensor_scalar_mul(dbc[:, 0:8], dbc[:, 24:32], 0.125)
            self.tok_dbc8 = sV.inc(i)
        b.v(v0)

        for dc in range(NCH):
            pi, ps, ptok = pool.acquire()

            def t(t, dc=dc, ps=ps, ptok=ptok):
                wait(t, wtok, ptok)
                i = None
                for c in range(NCH):
                    i = t.matmul(ps[:, 0:16], dwq[c][:, 128 * dc:128 * (dc + 1)], slotq[c][:, :],
                                 start=(c == 0), stop=(c == NCH - 1))
                self.tok_qd = sT.inc(i)
            b.t(t)

            def s(v, dc=dc, ps=ps):
                wait(v, self.tok_qd, self.tok_dbc8)
                i = v.tensor_scalar(qd[dc][:, :], ps[:, 0:16], dbc[:, 24 + dc:25 + dc], 0.125,
                                    ALU.add, ALU.mult)
                self.tok_qde = self.sV.inc(i)
            b.v(s)
            pool.release(pi, self.tok_qde)

        evd = []
        for dc in range(NCH):
            for ts in range(4):
                pi, ps, ptok = pool.acquire()

                def t(t, dc=dc, ts=ts, ps=ps, ptok=ptok):
                    wait(t, wtok, ptok, self.x_ready[dc])
                    i = None
                    for c in range(NCH):
                        i = t.matmul(ps[:, :], dwk[c][:, 128 * dc:128 * (dc + 1)],
                                     self.X[c][:, 512 * ts:512 * (ts + 1)],
                                     start=(c == 0), stop=(c == NCH - 1))
                    self.tok_kd = sT.inc(i)
                b.t(t)

                def s(v, dc=dc, ts=ts, ps=ps):
                    wait(v, self.tok_kd, self.tok_dbc)
                    i = v.tensor_scalar(kd[dc][:, 512 * ts:512 * (ts + 1)], ps[:, :],
                                        dbc[:, 8 + dc:9 + dc], 1.0, ALU.add, ALU.mult)
                    evd.append(self.sV.inc(i))
                b.v(s)
                pool.release(pi, evd[-1])

        for tt in range(NTOK):
            for half in range(2):
                pi, ps, ptok = pool.acquire()

                def t(t, tt=tt, half=half, ps=ps, ptok=ptok):
                    wait(t, ptok, self.x_ready[0], self.x_ready[7], wtok)
                    i = None
                    for c in range(NCH):
                        i = t.matmul(ps[:, :], self.X[c][:, 128 * tt:128 * (tt + 1)],
                                     dwv[c][:, 512 * half:512 * (half + 1)],
                                     start=(c == 0), stop=False)
                    i = t.matmul(ps[:, :], self.C["ones128"][0:1, 0:128],
                                 dbv_row[0:1, 512 * half:512 * (half + 1)], start=False, stop=True)
                    self.tok_vd = sT.inc(i)
                b.t(t)

                def v(v, tt=tt, half=half, ps=ps):
                    wait(v, self.tok_vd)
                    src = ap(ps, 0, 128, [[64, 8], [1, 64]])
                    dst = ap(vd[tt], 0, 128, [[65, 8], [1, 64]], off=65 * 8 * half)
                    i = v.tensor_copy(dst, src)
                    self.tok_vde = sV.inc(i)
                b.v(v)
                pool.release(pi, self.tok_vde)

            def v2(v, tt=tt):
                i = v.memset(ap(vd[tt], 0, 128, [[65, 16], [1, 1]], off=64), 1.0)
                self.tok_vdo = sV.inc(i)
            b.v(v2)

        for h in range(NH):
            ro, rc = 64 * (h % 2), h // 2
            for sg in range(4):
                pi, ps, ptok = pool.acquire()

                def t(t, h=h, ro=ro, rc=rc, sg=sg, ps=ps, ptok=ptok):
                    wait(t, self.tok_qde, evd[-1], ptok)
                    i = t.matmul(ps[0:16, :], qd[rc][ro:ro + 64, :],
                                 kd[rc][ro:ro + 64, 512 * sg:512 * (sg + 1)],
                                 start=True, stop=True)
                    self.tok_qmj = sT.inc(i)
                b.t(t)

                def v(v, h=h, sg=sg, ps=ps):
                    wait(v, self.tok_qmj)
                    if sg > 0:
                        v.drain()
                    i = v.reduce_max(self.statP[:, 8 + sg:9 + sg], ps[0:16, :], axis=AXX)
                    self.tok_qmr = sV.inc(i)
                b.v(v)
                pool.release(pi, self.tok_qmr)

            def v3(v, h=h):
                wait(v, self.tok_qmr)
                v.drain()
                i = v.reduce_max(M_d[:, h:h + 1], self.statP[:, 8:12], axis=AXX)
                self.tok_md = sV.inc(i)
            b.v(v3)

        def v4(v):
            wait(v, self.tok_md)
            v.drain()
            i = v.tensor_copy(M_dbf[:, :], M_d[:, :])
            self.tok_mdb = sV.inc(i)
        b.v(v4)
        for h in range(NH):
            pi, ps, ptok = pool.acquire()
            orow = 64 * (h % 2)

            def t(t, h=h, ps=ps, ptok=ptok, orow=orow):
                wait(t, self.tok_mdb, ptok)
                i = t.matmul(ps[orow:orow + 1, 0:16], M_dbf[:, h:h + 1], self.C["identbf"][0:16, 0:16],
                             start=True, stop=True)
                self.tok_mtr = sT.inc(i)
            b.t(t)

            def s(v, h=h, ps=ps, orow=orow):
                wait(v, self.tok_mtr)
                i = v.tensor_scalar_mul(qd_aug[orow:orow + 1, 16 * h:16 * (h + 1)],
                                        ps[orow:orow + 1, 0:16], -1.0)
                self.tok_mrow = self.sV.inc(i)
            b.v(s)
            pool.release(pi, self.tok_mrow)



        zpi, psz, zptok = pool.acquire()
        pvd = []
        exps_d = []
        for ki in range(NTOK):
            pp = ki % 2
            pi, ps, ptok = pool.acquire()

            def t(t, ki=ki, ps=ps, ptok=ptok):
                wait(t, self.tok_mrow, ptok, self.tok_vdo)
                if ki == 0:
                    wait(t, zptok)
                i = None
                for par in range(2):
                    hs = [h for h in range(NH) if h % 2 == par]
                    ro = 64 * par
                    for n_, h in enumerate(hs):
                        rc = h // 2
                        i = t.matmul(ps[:, 16 * h:16 * (h + 1)],
                                     kd[rc][ro:ro + 64, 128 * ki:128 * (ki + 1)],
                                     qd[rc][ro:ro + 64, :], start=(n_ == 0), stop=False)
                        i = t.matmul(ps[:, 16 * h:16 * (h + 1)], self.C["ones128"][ro:ro + 1, 0:128],
                                     qd_aug[ro:ro + 1, 16 * h:16 * (h + 1)],
                                     start=False, stop=(n_ == len(hs) - 1))
                self.tok_std = sT.inc(i)
            b.t(t)

            def s(s, ki=ki, pp=pp, ps=ps):
                wait(s, self.tok_std)
                if len(pvd) >= 2:
                    wait(s, pvd[-2])
                i = s.activation(out=P_d[pp][:, :], in_=ps[:, 0:256], func=AF.Exp)
                exps_d.append(sS.inc(i))
            b.s(s)
            pool.release(pi, exps_d[-1])

            def t2(t, ki=ki, pp=pp, psz=psz):
                wait(t, exps_d[-1])
                i = None
                for h in range(NH):
                    p, odd = h // 2, h % 2
                    orow = 64 * odd
                    ocol = 512 * odd
                    i = t.matmul(self.ps_big[orow:orow + 64, ocol + 16 * p:ocol + 16 * (p + 1)],
                                 vd[ki][:, 65 * h:65 * h + 64],
                                 P_d[pp][:, 16 * h:16 * (h + 1)],
                                 start=(ki == 0 and h <= 1),
                                 stop=(ki == NTOK - 1 and h >= NH - 2))
                    i = t.matmul(psz[0:1, 16 * h:16 * (h + 1)], self.C["ones_colbf"][:, :],
                                 P_d[pp][:, 16 * h:16 * (h + 1)],
                                 start=(ki == 0 and h == 0), stop=(ki == NTOK - 1 and h == NH - 1))
                pvd.append(sT.inc(i))
            b.t(t2)

        def vz(v):
            wait(v, pvd[-1])
            i = v.reciprocal(self.r_scr[0:1, 0:256], psz[0:1, 0:256])
            self.tok_rzd = sV.inc(i)
        b.v(vz)
        pool.release(zpi, self.tok_rzd)
        bi, psb, btok = pool.acquire()

        def tb(t, psb=psb, btok=btok):
            wait(t, self.tok_rzd, btok)
            t.matmul(psb[0:64, 0:128], self.C["ones128"][0:1, 0:64],
                     ap(self.r_scr, 0, 1, [[32, 8], [1, 16]], off=0), start=True, stop=True)
            i = t.matmul(psb[64:128, 0:128], self.C["ones128"][0:1, 0:64],
                         ap(self.r_scr, 0, 1, [[32, 8], [1, 16]], off=16), start=True, stop=True)
            self.tok_rbd = sT.inc(i)
        b.t(tb)

        def sb_(v, psb=psb):
            wait(v, self.tok_rbd)
            i = v.tensor_copy(rb128[:, :], psb[:, 0:128])
            self.tok_rbcd = self.sV.inc(i)
        b.v(sb_)
        pool.release(bi, self.tok_rbcd)

        def vev(v):
            wait(v, self.tok_rbcd)
            v.tensor_tensor(attn_d[0:64, :], self.ps_big[0:64, 0:128], rb128[0:64, :], ALU.mult)
            i = v.tensor_tensor(attn_d[64:128, :], self.ps_big[64:128, 512:640], rb128[64:128, :], ALU.mult)
            self.tok_atd = sV.inc(i)
        b.v(vev)

        otr = []
        for dc in range(NCH):
            pi, ps, ptok = pool.acquire()

            def t(t, dc=dc, ps=ps, ptok=ptok):
                wait(t, self.tok_atd, ptok)
                i = None
                for p in range(NCH):
                    i = t.matmul(ps[:, 0:16], dwo[p][:, 128 * dc:128 * (dc + 1)],
                                 attn_d[:, 16 * p:16 * (p + 1)], start=(p == 0), stop=(p == NCH - 1))
                self.tok_od = sT.inc(i)
            b.t(t)

            def v(v, dc=dc, ps=ps):
                wait(v, self.tok_od, self.tok_dbc)
                i = v.scalar_tensor_tensor(o_fm[dc][:, :], ps[:, 0:16], dbc[:, 16 + dc:17 + dc],
                                           slotq[dc][:, :], ALU.add, ALU.add)
                self.tok_ofm = sV.inc(i)
            b.v(v)
            pool.release(pi, self.tok_ofm)

            p2i, ps2, p2tok = pool.acquire()

            def t2(t, dc=dc, ps2=ps2, p2tok=p2tok):
                wait(t, self.tok_ofm, p2tok)
                i = t.matmul(ps2[0:16, 0:128], o_fm[dc][:, :], self.C["identbf"][:, :],
                             start=True, stop=True)
                self.tok_otr = sT.inc(i)
            b.t(t2)

            def s2(v, dc=dc, ps2=ps2):
                wait(v, self.tok_otr)
                i = v.tensor_copy(o_tm[:, 128 * dc:128 * (dc + 1)], ps2[0:16, 0:128])
                otr.append(self.sV.inc(i))
            b.v(s2)
            pool.release(p2i, otr[-1])

        def yex(sy):
            wait(sy, otr[-1])
            self.tok_obo = self.sDy.inc(sy.dma_start(out=bo_in.ap()[:, :], in_=o_tm[:, :]), 16)
        b.y(yex)

        def gex(g):
            wait(g, self.tok_obo)
            i = g.collective_compute("AllGather", ALU.bypass, replica_groups=[list(range(8))],
                                     ins=[bo_in.ap().opt()], outs=[bo_out.ap().opt()])
            self.tok_ag1 = self.sG.inc(i)
        b.g(gex)

        self.mk.block("dmha_done")
        scw.close()
        if self.upto == 'dmha':
            self.mk.block("trunc")
            sc.close()
            return

        sce = ExitStack()
        og_sb = mk.sb("og_sb", [128, 1024], BF16, scope=sce)
        yg_sb = mk.sb("yg_sb", [128, 1024], BF16, scope=sce)
        h_all = [mk.sb("h_all", [128, 16], BF16, scope=sce) for _ in range(NCH)]
        h_raw = [mk.sb("h_raw", [128, 8], BF16, scope=sce) for _ in range(NCH)]
        sq8 = mk.sb("sq8", [128, 8], BF16, scope=sce)
        rbc8 = mk.sb("rbc8", [128, 16], BF16, scope=sce)
        w1b = [mk.sb("w1b", [128, 2048], BF16, scope=sce) for _ in range(2)]
        w2b = [mk.sb("w2b", [128, 1024], BF16, scope=sce) for _ in range(2)]
        usb = mk.sb("usb", [8, 4096], BF16, scope=sce)
        gsb = mk.sb("gsb", [8, 4096], BF16, scope=sce)
        ugsb = mk.sb("ugsb", [8, 4096], BF16, scope=sce)
        ugT = [mk.sb("ugT", [128, 8], BF16, scope=sce) for _ in range(32)]
        eb1bf = mk.sb("eb1bf", [1, 8192], BF16, scope=sce)
        ebgbf = mk.sb("ebgbf", [1, 8192], BF16, scope=sce)
        eb2bf = mk.sb("eb2bf", [1, 2048], BF16, scope=sce)
        elng = mk.sb("elng", [128, 16], F32, scope=sce)
        elnb = mk.sb("elnb", [128, 16], F32, scope=sce)
        rowe = mk.sb("rowe", [1, 64], F32, scope=sce)
        rowebf = mk.sb("rowebf", [1, 32], BF16, scope=sce)

        self.w1toks = []
        self.tok_sileva = None
        self.tok_w2mm = None
        self.tok_pgex = None
        self.tok_pgtr = None
        b = mk.block("expert")

        def gog(g):
            wait(g, self.tok_ag1)
            self.tok_og = self.sDg.inc(g.dma_start(out=og_sb[:, :], in_=bo_out.ap()[:, :]), 16)
            self.sDg.inc(g.dma_start(out=eb1bf[:, :], in_=self.eb1_in.ap()[:, :]), 16)
            self.sDg.inc(g.dma_start(out=ebgbf[:, :], in_=self.ebg_in.ap()[:, :]), 16)
            self.tok_ebb = self.sDg.inc(g.dma_start(out=eb2bf[:, :], in_=self.eb2_in.ap()[:, :]), 16)
        b.g(gog)

        def yg0(sy):
            self.sDy.inc(sy.dma_start(out=elng[:, :], in_=self.elng_in.ap()[:, :]), 16)
            self.tok_eln = self.sDy.inc(sy.dma_start(out=elnb[:, :], in_=self.elnb_in.ap()[:, :]), 16)
        b.y(yg0)

        ysb_done = []
        for e in range(2):
            seltok = []
            for ci in range(NCH):
                pi, ps, ptok = pool.acquire()

                def t(t, e=e, ci=ci, ps=ps, ptok=ptok):
                    wait(t, self.tok_og, self.tok_dld, ptok)
                    i = t.matmul(ps[:, 0:8], og_sb[:, 128 * ci:128 * (ci + 1)],
                                 selT[:, 8 * e:8 * (e + 1)], start=True, stop=True)
                    self.tok_sel = sT.inc(i)
                b.t(t)

                def s(v, e=e, ci=ci, ps=ps):
                    wait(v, self.tok_sel)
                    i = v.tensor_copy(h_raw[ci][:, :], ps[:, 0:8])
                    seltok.append(self.sV.inc(i))
                b.v(s)
                pool.release(pi, seltok[-1])

            z1, pz1, zt1 = pool.acquire()
            z2, pz2, zt2 = pool.acquire()
            sqtoks = []
            for ci in range(NCH):
                def s(v, e=e, ci=ci):
                    wait(v, seltok[-1])
                    if sqtoks:
                        wait(v, self.tok_esq)
                    i = v.tensor_tensor(sq8[:, :], h_raw[ci][:, :], h_raw[ci][:, :], ALU.mult)
                    sqtoks.append(self.sV.inc(i))
                b.v(s)

                def t(t, e=e, ci=ci, pz1=pz1, pz2=pz2, zt1=zt1, zt2=zt2):
                    wait(t, sqtoks[-1])
                    if ci == 0:
                        wait(t, zt1, zt2)
                    t.matmul(pz1[0:1, 0:8], self.C["ones_colbf"][:, :], h_raw[ci][:, :],
                             start=(ci == 0), stop=(ci == NCH - 1))
                    i = t.matmul(pz2[0:1, 0:8], self.C["ones_colbf"][:, :], sq8[:, :],
                                 start=(ci == 0), stop=(ci == NCH - 1))
                    self.tok_esq = sT.inc(i)
                b.t(t)

            def v(v, e=e, pz1=pz1, pz2=pz2):
                wait(v, self.tok_esq)
                v.tensor_scalar_mul(rowe[:, 0:8], pz1[0:1, 0:8], 1.0 / D)
                v.drain()
                v.tensor_tensor(rowe[:, 8:16], rowe[:, 0:8], rowe[:, 0:8], ALU.mult)
                v.drain()
                i = v.scalar_tensor_tensor(rowe[:, 16:24], pz2[0:1, 0:8], 1.0 / D,
                                           rowe[:, 8:16], ALU.mult, ALU.subtract)
                self.tok_evar = sV.inc(i)
            b.v(v)
            pool.release(z1, self.tok_evar)
            pool.release(z2, self.tok_evar)

            def s(s, e=e):
                wait(s, self.tok_evar)
                i = s.activation(out=rowe[:, 24:32], in_=rowe[:, 16:24], func=AF.Sqrt,
                                 bias=self.statP[0:1, 4:5], scale=1.0)
                self.tok_esd = sS.inc(i)
            b.s(s)

            def v2(v, e=e):
                wait(v, self.tok_esd)
                v.reciprocal(rowe[:, 32:40], rowe[:, 24:32])
                v.drain()
                v.scalar_tensor_tensor(rowe[:, 40:48], rowe[:, 0:8], -1.0,
                                       rowe[:, 32:40], ALU.mult, ALU.mult)
                v.drain()
                v.tensor_copy(rowebf[0:1, 0:8], rowe[:, 32:40])
                i = v.tensor_copy(rowebf[0:1, 8:16], rowe[:, 40:48])
                self.tok_erow = sV.inc(i)
            b.v(v2)

            b1, pb1, bt1 = pool.acquire()

            def t2(t, e=e, pb1=pb1, bt1=bt1):
                wait(t, self.tok_erow, bt1)
                t.matmul(pb1[:, 0:8], self.C["ones_row"][:, :], rowebf[0:1, 0:8],
                         start=True, stop=False)
                i = t.matmul(pb1[:, 8:16], self.C["ones_row"][:, :], rowebf[0:1, 8:16],
                             start=False, stop=True)
                self.tok_ebc = sT.inc(i)
            b.t(t2)

            def s2(v, e=e, pb1=pb1):
                wait(v, self.tok_ebc)
                i = v.tensor_copy(rbc8[:, :], pb1[:, 0:16])
                self.tok_ebcc = self.sV.inc(i)
            b.v(s2)
            pool.release(b1, self.tok_ebcc)

            lntoks = []
            for ci in range(NCH):
                def v3(v, e=e, ci=ci):
                    wait(v, self.tok_ebcc, self.tok_eln)
                    v.tensor_tensor(h_raw[ci][:, :], h_raw[ci][:, :], rbc8[:, 0:8], ALU.mult)
                    v.tensor_tensor(h_raw[ci][:, :], h_raw[ci][:, :], rbc8[:, 8:16], ALU.add)
                    i = v.tensor_scalar(h_all[ci][:, 8 * e:8 * (e + 1)], h_raw[ci][:, :],
                                        elng[:, 8 * e + ci:8 * e + ci + 1],
                                        elnb[:, 8 * e + ci:8 * e + ci + 1], ALU.mult, ALU.add)
                    lntoks.append(sV.inc(i))
                b.v(v3)

            for mat in range(2):
                w_in = self.ew1_in if mat == 0 else self.ewg_in
                brow = eb1bf if mat == 0 else ebgbf
                dst = usb if mat == 0 else gsb
                for Fh in range(2):
                    mmtoks = []

                    def t0(t, e=e, mat=mat, Fh=Fh, brow=brow):
                        wait(t, self.tok_ebb)
                        if self.tok_sileva is not None:
                            wait(t, self.tok_sileva)
                        if self.tok_w2mm is not None:
                            wait(t, self.tok_w2mm)
                        i = None
                        for fs in range(4):
                            i = t.matmul(self.ps_big[0:8, 512 * fs:512 * (fs + 1)],
                                         self.C["ones128"][0:1, 0:8],
                                         brow[0:1, 4096 * e + 2048 * Fh + 512 * fs:
                                              4096 * e + 2048 * Fh + 512 * (fs + 1)],
                                         start=True, stop=False)
                        mmtoks.append(sT.inc(i))
                    b.t(t0)

                    for dc in range(NCH):
                        def g(g, e=e, mat=mat, Fh=Fh, dc=dc, w_in=w_in):
                            if len(self.w1toks) >= 2:
                                wait(g, self.w1toks[-2][1])
                            i = g.dma_start(out=w1b[dc % 2][:, :],
                                            in_=w_in.ap()[1024 * e + 128 * dc:1024 * e + 128 * (dc + 1),
                                                          2048 * Fh:2048 * (Fh + 1)])
                            self.w1toks.append([self.sDg.inc(i, 16), None])
                        b.g(g)
                        ldtok = self.sDg.tok()

                        def t(t, e=e, dc=dc, ldtok=ldtok):
                            wait(t, ldtok, lntoks[-1], mmtoks[-1])
                            i = None
                            for fs in range(4):
                                i = t.matmul(self.ps_big[0:8, 512 * fs:512 * (fs + 1)],
                                             h_all[dc][:, 8 * e:8 * (e + 1)],
                                             w1b[dc % 2][:, 512 * fs:512 * (fs + 1)],
                                             start=False, stop=(dc == NCH - 1))
                            tk = sT.inc(i)
                            mmtoks.append(tk)
                            self.w1toks[-1][1] = tk
                        b.t(t)

                    for fs in range(4):
                        def s(s, e=e, Fh=Fh, fs=fs, dst=dst):
                            wait(s, mmtoks[-1])
                            i = s.activation(out=dst[:, 2048 * Fh + 512 * fs:2048 * Fh + 512 * (fs + 1)],
                                             in_=self.ps_big[0:8, 512 * fs:512 * (fs + 1)], func=AF.Silu)
                            self.tok_sileva = sS.inc(i)
                        b.s(s)

            ugtoks = []
            for Fh in range(2):
                def v4(v, e=e, Fh=Fh):
                    wait(v, self.tok_sileva)
                    i = v.tensor_tensor(ugsb[:, 2048 * Fh:2048 * (Fh + 1)],
                                        usb[:, 2048 * Fh:2048 * (Fh + 1)],
                                        gsb[:, 2048 * Fh:2048 * (Fh + 1)], ALU.mult)
                    ugtoks.append(sV.inc(i))
                b.v(v4)

            trtoks = []
            for fc in range(32):
                pi, ps, ptok = pool.acquire()

                def t3(t, e=e, fc=fc, ps=ps, ptok=ptok):
                    wait(t, ugtoks[-1], ptok)
                    i = t.matmul(ps[0:128, 0:8], ugsb[0:8, 128 * fc:128 * (fc + 1)],
                                 self.C["identbf"][0:8, 0:8], start=True, stop=True)
                    self.tok_ugt = sT.inc(i)
                b.t(t3)

                def s3(v, e=e, fc=fc, ps=ps):
                    wait(v, self.tok_ugt)
                    i = v.tensor_copy(ugT[fc][:, :], ps[0:128, 0:8])
                    trtoks.append(self.sV.inc(i))
                b.v(s3)
                pool.release(pi, trtoks[-1])

            y1, py1, yt1 = pool.acquire()
            y2, py2, yt2 = pool.acquire()
            w2toks = []

            def t4(t, e=e, py1=py1, py2=py2, yt1=yt1, yt2=yt2):
                wait(t, self.tok_ebb, yt1, yt2)
                t.matmul(py1[0:8, :], self.C["ones128"][0:1, 0:8], eb2bf[0:1, 1024 * e:1024 * e + 512],
                         start=True, stop=False)
                i = t.matmul(py2[0:8, :], self.C["ones128"][0:1, 0:8],
                             eb2bf[0:1, 1024 * e + 512:1024 * e + 1024], start=True, stop=False)
                self.tok_w2mm = sT.inc(i)
            b.t(t4)

            for fc in range(32):
                def g2(g, e=e, fc=fc):
                    if len(w2toks) >= 2:
                        wait(g, w2toks[-2][1])
                    i = g.dma_start(out=w2b[fc % 2][:, :],
                                    in_=self.ew2_in.ap()[4096 * e + 128 * fc:4096 * e + 128 * (fc + 1), :])
                    w2toks.append([self.sDg.inc(i, 16), None])
                b.g(g2)
                ldtok = self.sDg.tok()

                def t5(t, e=e, fc=fc, ldtok=ldtok, py1=py1, py2=py2):
                    wait(t, ldtok, trtoks[-1], self.tok_w2mm)
                    t.matmul(py1[0:8, :], ugT[fc][:, :], w2b[fc % 2][:, 0:512],
                             start=False, stop=(fc == 31))
                    i = t.matmul(py2[0:8, :], ugT[fc][:, :], w2b[fc % 2][:, 512:1024],
                                 start=False, stop=(fc == 31))
                    tk = sT.inc(i)
                    self.tok_w2mm = tk
                    w2toks[-1][1] = tk
                b.t(t5)

            def s4(v, e=e, py1=py1, py2=py2):
                wait(v, self.tok_w2mm)
                v.tensor_copy(y_sb[:, 0:512], py1[0:8, :])
                i = v.tensor_copy(y_sb[:, 512:1024], py2[0:8, :])
                self.tok_ysb = self.sV.inc(i)
            b.v(s4)
            pool.release(y1, self.tok_ysb)
            pool.release(y2, self.tok_ysb)

            def y5(sy, e=e):
                wait(sy, self.tok_ysb)
                i = sy.dma_start(out=by_in.ap()[8 * e:8 * (e + 1), :], in_=y_sb[0:8, :])
                ysb_done.append(self.sDy.inc(i, 16))
            b.y(y5)
            def s5(v, e=e):
                wait(v, ysb_done[-1])
            b.v(s5)

        def gex2(g):
            wait(g, ysb_done[-1])
            i = g.collective_compute("AllGather", ALU.bypass, replica_groups=[list(range(8))],
                                     ins=[by_in.ap().opt()], outs=[by_out.ap().opt()])
            self.tok_ag2 = self.sG.inc(i)
            wait(g, self.tok_ag2)
            self.tok_yg = self.sDg.inc(g.dma_start(out=yg_sb[:, :], in_=by_out.ap()[:, :]), 16)
        b.g(gex2)

        for dc in range(NCH):
            pi, ps, ptok = pool.acquire()

            def t6(t, dc=dc, ps=ps, ptok=ptok):
                wait(t, self.tok_yg, ptok)
                i = t.matmul(ps[:, 0:16], yg_sb[:, 128 * dc:128 * (dc + 1)], selT[:, 16:32],
                             start=True, stop=True)
                self.tok_ysel = sT.inc(i)
            b.t(t6)

            def v6(v, dc=dc, ps=ps):
                wait(v, self.tok_ysel)
                i = v.scalar_tensor_tensor(o_fm[dc][:, :], ps[:, 0:16], 1.0, o_fm[dc][:, :],
                                           ALU.mult, ALU.add)
                self.tok_onew = sV.inc(i)
            b.v(v6)
            pool.release(pi, self.tok_onew)

        self.mk.block("expert_done")
        sce.close()
        if self.upto == 'expert':
            self.mk.block("trunc")
            sc.close()
            return

        scc = ExitStack()
        qc = [mk.sb("qc", [128, N], BF16, scope=scc) for _ in range(NCH)]
        kc = [mk.sb("kc", [128, 16], BF16, scope=scc) for _ in range(NCH)]
        vc = mk.sb("vc", [16, 1024], BF16, scope=scc)
        attn_c = [mk.sb("attn_c", [128, N], BF16, scope=scc) for _ in range(NCH)]
        P_Th = [mk.sb("P_Th", [16, N], BF16, scope=scc) for _ in range(4)]
        Pg = [mk.sb("Pg", [128, 64], BF16, scope=scc) for _ in range(2)]
        mx4 = mk.sb("mx4", [128, 8], F32, scope=scc)
        cbc = mk.sb("cbc", [128, 24], F32, scope=scc)
        cbv_row = mk.sb("cbv_row", [1, 1024], BF16, scope=scc)
        cbo_row = mk.sb("cbo_row", [1, 1024], BF16, scope=scc)
        ones2k = mk.sb("ones2k", [1, 512], BF16, scope=scc)
        scwc = ExitStack()
        cwq = [mk.sb("cwq", [128, 1024], BF16, scope=scwc) for _ in range(NCH)]
        cwk = [mk.sb("cwk", [128, 1024], BF16, scope=scwc) for _ in range(NCH)]
        cwv = [mk.sb("cwv", [128, 1024], BF16, scope=scwc) for _ in range(NCH)]

        b = mk.block("cmha")

        def gld(g):
            for wsb, win in [(cwq, self.cwq_in), (cwk, self.cwk_in), (cwv, self.cwv_in)]:
                for c in range(NCH):
                    self.sDg.inc(g.dma_start(out=wsb[c][:, :], in_=win.ap()[128 * c:128 * (c + 1), :]), 16)
            self.sDg.inc(g.dma_start(out=cbv_row[:, :], in_=self.cbv_in.ap()[:, :]), 16)
            self.tok_cld = self.sDg.inc(g.dma_start(out=cbo_row[:, :], in_=self.cbo_in.ap()[:, :]), 16)
        b.g(gld)
        wtok = self.sDg.tok()

        def yld(sy):
            self.sDy.inc(sy.dma_start(out=cbc[:, 8:16], in_=self.cbq_in.ap()[:, :]), 16)
            self.tok_cbc = self.sDy.inc(sy.dma_start(out=cbc[:, 16:24], in_=self.cbk_in.ap()[:, :]), 16)
        b.y(yld)

        def v0(v):
            wait(v, self.tok_cbc)
            v.tensor_scalar_mul(cbc[:, 0:8], cbc[:, 8:16], 0.125)
            self.tok_cbc8 = sV.inc(v.memset(ones2k[:, :], 1.0))
        b.v(v0)

        evc = []
        for dc in range(NCH):
            for ts in range(4):
                pi, ps, ptok = pool.acquire()

                def t(t, dc=dc, ts=ts, ps=ps, ptok=ptok):
                    wait(t, wtok, ptok, self.x_ready[dc])
                    i = None
                    for c in range(NCH):
                        i = t.matmul(ps[:, :], cwq[c][:, 128 * dc:128 * (dc + 1)],
                                     self.X[c][:, 512 * ts:512 * (ts + 1)],
                                     start=(c == 0), stop=(c == NCH - 1))
                    self.tok_qc = sT.inc(i)
                b.t(t)

                def s(v, dc=dc, ts=ts, ps=ps):
                    wait(v, self.tok_qc, self.tok_cbc8)
                    i = v.tensor_scalar(qc[dc][:, 512 * ts:512 * (ts + 1)], ps[:, :],
                                        cbc[:, 8 + dc:9 + dc], 0.125, ALU.add, ALU.mult)
                    evc.append(self.sV.inc(i))
                b.v(s)
                pool.release(pi, evc[-1])

        for dc in range(NCH):
            pi, ps, ptok = pool.acquire()

            def t(t, dc=dc, ps=ps, ptok=ptok):
                wait(t, wtok, ptok, self.tok_onew)
                i = None
                for c in range(NCH):
                    i = t.matmul(ps[:, 0:16], cwk[c][:, 128 * dc:128 * (dc + 1)], o_fm[c][:, :],
                                 start=(c == 0), stop=(c == NCH - 1))
                self.tok_kc = sT.inc(i)
            b.t(t)

            def s(v, dc=dc, ps=ps):
                wait(v, self.tok_kc, self.tok_cbc)
                i = v.tensor_scalar(kc[dc][:, :], ps[:, 0:16], cbc[:, 16 + dc:17 + dc], 1.0,
                                    ALU.add, ALU.mult)
                self.tok_kce = self.sV.inc(i)
            b.v(s)
            pool.release(pi, self.tok_kce)

        for half in range(2):
            pi, ps, ptok = pool.acquire()

            def t(t, half=half, ps=ps, ptok=ptok):
                wait(t, wtok, ptok, self.tok_onew, self.tok_cld)
                i = None
                for c in range(NCH):
                    i = t.matmul(ps[0:16, :], o_fm[c][:, :], cwv[c][:, 512 * half:512 * (half + 1)],
                                 start=(c == 0), stop=False)
                i = t.matmul(ps[0:16, :], self.C["ones128"][0:1, 0:16],
                             cbv_row[0:1, 512 * half:512 * (half + 1)], start=False, stop=True)
                self.tok_vc = sT.inc(i)
            b.t(t)

            def s(v, half=half, ps=ps):
                wait(v, self.tok_vc)
                i = v.tensor_copy(vc[:, 512 * half:512 * (half + 1)], ps[0:16, :])
                self.tok_vce = self.sV.inc(i)
            b.v(s)
            pool.release(pi, self.tok_vce)

        if self.upto == 'cmha1':
            self.mk.block("trunc")
            scwc.close()
            scc.close()
            sc.close()
            return
        self.mk.block("cmha_w")
        scwc.close()
        scwo = ExitStack()
        cwo = [mk.sb("cwo", [128, 1024], BF16, scope=scwo) for _ in range(NCH)]
        b = mk.block("cmha2")

        def gldo(g):
            for c in range(NCH):
                self.tok_cwo = self.sDg.inc(g.dma_start(out=cwo[c][:, :], in_=self.cwo_in.ap()[128 * c:128 * (c + 1), :]), 16)
        b.g(gldo)

        self.tok_pgdone = None
        ngq = 4
        GQ_HEADS = [[0, 2, 4, 6], [8, 10, 12, 14], [1, 3, 5, 7], [9, 11, 13, 15]]
        for gq in range(ngq):
            gheads = GQ_HEADS[gq]
            cp_toks = []
            for tt in range(NTOK):
                pp = tt % 2
                pi, ps, ptok = pool.acquire()

                def t(t, gq=gq, tt=tt, ps=ps, ptok=ptok, gheads=gheads):
                    wait(t, evc[-1], self.tok_kce, ptok)
                    i = None
                    for j, h in enumerate(gheads):
                        ro, rc = 64 * (h % 2), h // 2
                        i = t.matmul(ps[:, 16 * j:16 * (j + 1)],
                                     qc[rc][ro:ro + 64, 128 * tt:128 * (tt + 1)],
                                     kc[rc][ro:ro + 64, :], start=(j == 0), stop=(j == 3))
                    self.tok_scc = sT.inc(i)
                b.t(t)

                def v(v, gq=gq, tt=tt, ps=ps):
                    wait(v, self.tok_scc, self.tok_pgex if tt >= 1 else None)
                    v.tensor_reduce(mx4[:, 0:4], ap(ps, 0, 128, [[16, 4], [1, 16]]),
                                    axis=AXX, op=ALU.max)
                    v.drain()
                    i = v.tensor_reduce(mx4[:, 4:5], mx4[:, 0:4], axis=AXX, op=ALU.max, negate=True)
                    self.tok_mx = sV.inc(i)
                b.v(v)


                def s(s, gq=gq, tt=tt, pp=pp, ps=ps):
                    wait(s, self.tok_mx)
                    if self.tok_pgtr is not None:
                        wait(s, self.tok_pgtr)
                    i = s.activation(out=Pg[pp][:, :], in_=ps[:, 0:64],
                                     func=AF.Exp, bias=mx4[:, 4:5], scale=1.0)
                    self.tok_pgex = sS.inc(i)
                b.s(s)
                pool.release(pi, self.tok_pgex)


                def v2(v, gq=gq, tt=tt, pp=pp):
                    wait(v, self.tok_pgex)
                    v.tensor_reduce(mx4[:, 0:4], ap(Pg[pp], 0, 128, [[16, 4], [1, 16]]),
                                    axis=AXX, op=ALU.add)
                    v.drain()
                    v.reciprocal(mx4[:, 4:8], mx4[:, 0:4])
                    v.drain()
                    i = None
                    for j in range(4):
                        i = v.tensor_scalar_mul(Pg[pp][:, 16 * j:16 * (j + 1)],
                                                Pg[pp][:, 16 * j:16 * (j + 1)], mx4[:, 4 + j:5 + j])
                    self.tok_pn = sV.inc(i)
                b.v(v2)


                def t2(t, gq=gq, tt=tt, pp=pp):
                    wait(t, self.tok_pn)
                    if tt % 4 == 0 and tt > 0:
                        wait(t, cp_toks[-1])
                    elif tt == 0:
                        wait(t, self.tok_pgdone)
                    i = None
                    for j in range(4):
                        i = t.matmul(self.ps_big[0:16, 512 * j + 128 * (tt % 4):512 * j + 128 * (tt % 4) + 128],
                                     Pg[pp][:, 16 * j:16 * (j + 1)], self.C["identbf"][:, :],
                                     start=(tt % 4 == 0), stop=(tt % 4 == 3))
                    self.tok_pgtr = sT.inc(i)
                b.t(t2)

                if tt % 4 == 3:
                    for j in range(4):
                        def s2(v, gq=gq, tt=tt, j=j):
                            wait(v, self.tok_pgtr, self.tok_pgdone)
                            i = v.tensor_copy(P_Th[j][:, 512 * (tt // 4):512 * (tt // 4 + 1)],
                                              self.ps_big[0:16, 512 * j:512 * (j + 1)])
                            cp_toks.append(self.sV.inc(i))
                        b.v(s2)

            pv_toks = []
            for j, h in enumerate(gheads):
                ro, p = 64 * (h % 2), h // 2
                for cs in range(4):
                    pi, ps, ptok = pool.acquire()

                    def t3(t, h=h, j=j, ro=ro, cs=cs, ps=ps, ptok=ptok):
                        wait(t, cp_toks[-1], self.tok_vce, ptok)
                        i = t.matmul(ps[ro:ro + 64, :], vc[:, 64 * h:64 * (h + 1)],
                                     P_Th[j][:, 512 * cs:512 * (cs + 1)], start=True, stop=True)
                        self.tok_pvc = sT.inc(i)
                    b.t(t3)

                    def s3(v, h=h, ro=ro, p=p, cs=cs, ps=ps):
                        wait(v, self.tok_pvc)
                        i = v.tensor_copy(attn_c[p][ro:ro + 64, 512 * cs:512 * (cs + 1)],
                                          ps[ro:ro + 64, :])
                        pv_toks.append(self.sV.inc(i))
                    b.v(s3)
                    pool.release(pi, pv_toks[-1])
            self.tok_pgdone = pv_toks[-1]
            self.tok_pgdone = pv_toks[-1]

        if self.upto == 'cmha2':
            self.mk.block("trunc")
            scwo.close()
            scc.close()
            sc.close()
            return
        for dc in range(NCH):
            for ts in range(4):
                pi, ps, ptok = pool.acquire()

                def t(t, dc=dc, ts=ts, ps=ps, ptok=ptok):
                    wait(t, pv_toks[-1] if pv_toks else None, ptok, self.tok_cbc8, self.tok_cwo)
                    t.matmul(ps[:, :], cbo_row[0:1, 128 * dc:128 * (dc + 1)], ones2k[0:1, :],
                             start=True, stop=False)
                    i = None
                    for r in range(NCH):
                        i = t.matmul(ps[:, :], cwo[r][:, 128 * dc:128 * (dc + 1)],
                                     attn_c[r][:, 512 * ts:512 * (ts + 1)],
                                     start=False, stop=(r == NCH - 1))
                    self.tok_co = sT.inc(i)
                b.t(t)

                def v(v, dc=dc, ts=ts, ps=ps):
                    wait(v, self.tok_co)
                    sl = self.X[dc][:, 512 * ts:512 * (ts + 1)]
                    i = v.scalar_tensor_tensor(sl, ps[:, :], 1.0, sl, ALU.mult, ALU.add)
                    tk = sV.inc(i)
                    self.x_ready[dc] = tk
                b.v(v)
                pool.release(pi, self.x_ready[dc])

        self.mk.block("cmha_done")
        scwo.close()
        scc.close()
        self.layernorm("ln4")
        self.mk.block("l1f_done")
        sc.close()

    def layernorm(self, name):
        mk = self.mk
        sT, sV, sS = self.sT, self.sV, self.sS
        pool = self.pool
        b = mk.block(name)
        st = self.stat
        ap_toks = []
        for ts in range(4):
            i1, ps1, tk1 = pool.acquire()
            i2, ps2, tk2 = pool.acquire()
            sq_toks, mmtk = [], []
            for c in range(NCH):
                def s(v, ts=ts, c=c):
                    wait(v, self.x_ready[c])
                    if len(sq_toks) >= 2:
                        wait(v, mmtk[-2])
                    sl = self.X[c][:, 512 * ts:512 * (ts + 1)]
                    i = v.tensor_tensor(self.xsq[c % 2][:, :], sl, sl, ALU.mult)
                    sq_toks.append(self.sV.inc(i))
                b.v(s)

                def t(t, ts=ts, c=c, ps1=ps1, ps2=ps2, tk1=tk1, tk2=tk2):
                    wait(t, sq_toks[-1], self.x_ready[c])
                    if c == 0:
                        wait(t, tk1, tk2)
                    t.matmul(ps1[0:1, :], self.C["ones_colbf"][:, :],
                             self.X[c][:, 512 * ts:512 * (ts + 1)], start=(c == 0), stop=(c == NCH - 1))
                    i = t.matmul(ps2[0:1, :], self.C["ones_colbf"][:, :], self.xsq[c % 2][:, :],
                                 start=(c == 0), stop=(c == NCH - 1))
                    mmtk.append(sT.inc(i))
                b.t(t)

            def v(v, ts=ts, ps1=ps1, ps2=ps2):
                wait(v, mmtk[-1])
                v.tensor_scalar_mul(st[:, 0:512], ps1[0:1, :], 1.0 / D)
                v.drain()
                v.tensor_tensor(st[:, 512:1024], st[:, 0:512], st[:, 0:512], ALU.mult)
                v.drain()
                i = v.scalar_tensor_tensor(st[:, 1024:1536], ps2[0:1, :], 1.0 / D,
                                           st[:, 512:1024], ALU.mult, ALU.subtract)
                self.tok_var = sV.inc(i)
            b.v(v)
            pool.release(i1, self.tok_var)
            pool.release(i2, self.tok_var)

            def s(s, ts=ts):
                wait(s, self.tok_var, self.tok_hyg)
                i = s.activation(out=st[:, 1536:2048], in_=st[:, 1024:1536], func=AF.Sqrt,
                                 bias=self.statP[0:1, 4:5], scale=1.0)
                self.tok_sd = sS.inc(i)
            b.s(s)

            def v2(v, ts=ts):
                wait(v, self.tok_sd)
                v.reciprocal(st[:, 2048:2560], st[:, 1536:2048])
                v.drain()
                v.scalar_tensor_tensor(st[:, 2560:3072], st[:, 0:512], -1.0,
                                       st[:, 2048:2560], ALU.mult, ALU.mult)
                v.drain()
                v.tensor_copy(self.r_scr[0:1, 0:512], st[:, 2048:2560])
                i = v.tensor_copy(self.r_scr[0:1, 512:1024], st[:, 2560:3072])
                self.tok_rows = sV.inc(i)
            b.v(v2)

            b1, psb1, tb1 = pool.acquire()
            b2, psb2, tb2 = pool.acquire()

            def t2(t, ts=ts, psb1=psb1, psb2=psb2, tb1=tb1, tb2=tb2):
                wait(t, self.tok_rows, tb1, tb2)
                t.matmul(psb1[:, :], self.C["ones_row"][:, :], self.r_scr[0:1, 0:512],
                         start=True, stop=True)
                i = t.matmul(psb2[:, :], self.C["ones_row"][:, :], self.r_scr[0:1, 512:1024],
                             start=True, stop=True)
                self.tok_lnb = sT.inc(i)
            b.t(t2)

            def s2(v, ts=ts, psb1=psb1, psb2=psb2):
                wait(v, self.tok_lnb)
                v.tensor_copy(self.rb[0][:, :], psb1[:, :])
                i = v.tensor_copy(self.rb[1][:, :], psb2[:, :])
                self.tok_lnrb = sV.inc(i)
            b.v(s2)

            for c in range(NCH):
                def v3(v, ts=ts, c=c):
                    wait(v, self.tok_lnrb)
                    sl = self.X[c][:, 512 * ts:512 * (ts + 1)]
                    v.tensor_tensor(sl, sl, self.rb[0][:, :], ALU.mult)
                    i = v.tensor_tensor(sl, sl, self.rb[1][:, :], ALU.add)
                    tk = sV.inc(i)
                    ap_toks.append(tk)
                    self.x_ready[c] = tk
                b.v(v3)
            pool.release(b1, ap_toks[-1])
            pool.release(b2, ap_toks[-1])

    def store_out(self):
        mk = self.mk
        self.ost = [mk.sb("ost", [128, 512], F32) for _ in range(2)]
        b = mk.block("outp")
        o_toks = []
        for c in range(NCH):
            for ts in range(4):
                def v(v, c=c, ts=ts):
                    wait(v, self.x_ready[c])
                    if len(o_toks) >= 2:
                        wait(v, o_toks[-2])
                    i = v.tensor_copy(self.ost[(c * 4 + ts) % 2][:, :], self.X[c][:, 512 * ts:512 * (ts + 1)])
                    self.tok_ostc = self.sV.inc(i)
                b.v(v)

                def y(sy, c=c, ts=ts):
                    wait(sy, self.tok_ostc)
                    i = sy.dma_start(out=self.out_t.ap()[128 * c:128 * (c + 1), 512 * ts:512 * (ts + 1)],
                                     in_=self.ost[(c * 4 + ts) % 2][:, :])
                    o_toks.append(self.sO.inc(i, 16))
                b.y(y)

        def y2(sy):
            wait(sy, o_toks[-1])
        b.y(y2)

    def build(self):
        self.declare()
        self.pv1_done = []
        self.pool = PsumPool(self.mk, 4)
        self.ps_big = self.mk.ps("ps_big", [128, N])
        with self.nc.allow_low_precision(reason="intentional bf16 activations/weights"):
            self.phase_load()
            if self.upto != 'none':
                self.l0_attn()
                if self.upto in ('ffn', 'l01', 'l1a', 'full2', 'dmha', 'expert', 'cmha1', 'cmha2'):
                    self.l0_ffn()
                if self.upto in ('l1a', 'full2', 'dmha', 'expert', 'cmha1', 'cmha2'):
                    self.l1_attn()
                if self.upto in ('full2', 'dmha', 'expert', 'cmha1', 'cmha2'):
                    self.l1_ffn()
            self.store_out()
            self.dump_dbg()
            self.mk.end_blocks()
        return self.nc


CONST_KEYS = ["ones_row", "ones_colbf", "tri", "C2", "S2", "E_pack", "erow",
              "ones128", "identbf"]


def stage_inputs_core(x_b, wqkv, wo, logxi, consts, wfc=None, wpr=None, **kw):
    d = {"x": np.ascontiguousarray(x_b.T),
         "wqkv": np.ascontiguousarray(wqkv),
         "wo": np.ascontiguousarray(wo),
         "logxi": np.asarray(logxi, dtype=np.float32).reshape(1, NH)}
    if wfc is not None:
        d["wfc"] = np.ascontiguousarray(wfc.T)
        d["wpr"] = np.ascontiguousarray(wpr.T)
    if kw.get("wq1") is not None:
        d["wq1"] = np.ascontiguousarray(kw["wq1"].T)
        d["wk1"] = np.ascontiguousarray(kw["wk1"].T)
        d["wv1"] = np.ascontiguousarray(kw["wv1"].T)
    if kw.get("g") is not None:
        gg = kw["g"]
        core = kw["core"]
        d["slotq"] = np.ascontiguousarray(gg["L1_ffn_slot_query"][0].T)
        for nm, key in [("dwq", "L1_ffn_d_Wq"), ("dwk", "L1_ffn_d_Wk"),
                        ("dwv", "L1_ffn_d_Wv"), ("dwo", "L1_ffn_d_Wo"),
                        ("cwq", "L1_ffn_c_Wq"), ("cwk", "L1_ffn_c_Wk"),
                        ("cwv", "L1_ffn_c_Wv"), ("cwo", "L1_ffn_c_Wo")]:
            d[nm] = np.ascontiguousarray(gg[key])
        for nm, key in [("dbq", "L1_ffn_d_bq"), ("dbk", "L1_ffn_d_bk"),
                        ("dbo", "L1_ffn_d_bo"), ("cbq", "L1_ffn_c_bq"),
                        ("cbk", "L1_ffn_c_bk")]:
            d[nm] = np.ascontiguousarray(gg[key].reshape(8, 128).T)
        for nm, key in [("dbv", "L1_ffn_d_bv"), ("cbv", "L1_ffn_c_bv"),
                        ("cbo", "L1_ffn_c_bo")]:
            d[nm] = np.ascontiguousarray(gg[key].reshape(1, D))
        es = [2 * core, 2 * core + 1]
        d["ew1"] = np.ascontiguousarray(gg["L1_ffn_e_W1"][es].reshape(2048, FF))
        d["ewg"] = np.ascontiguousarray(gg["L1_ffn_e_Wg"][es].reshape(2048, FF))
        d["ew2"] = np.ascontiguousarray(gg["L1_ffn_e_W2"][es].reshape(8192, D))
        d["eb1"] = np.ascontiguousarray(gg["L1_ffn_e_b1"][es].reshape(1, 8192))
        d["ebg"] = np.ascontiguousarray(gg["L1_ffn_e_bg"][es].reshape(1, 8192))
        d["eb2"] = np.ascontiguousarray(gg["L1_ffn_e_b2"][es].reshape(1, 2048))
        d["elng"] = np.ascontiguousarray(
            gg["L1_ffn_e_ln_g"][es].reshape(2, 8, 128).transpose(2, 0, 1).reshape(128, 16))
        d["elnb"] = np.ascontiguousarray(
            gg["L1_ffn_e_ln_b"][es].reshape(2, 8, 128).transpose(2, 0, 1).reshape(128, 16))
        sel = np.zeros((128, 32), dtype=np.float32)
        for e in range(2):
            for bb in range(8):
                sel[16 * bb + 2 * core + e, 8 * e + bb] = 1.0
        for s_ in range(16):
            sel[16 * (s_ // 2) + 8 * (s_ % 2) + core, 16 + s_] = 1.0
        d["selT"] = sel
    for k in CONST_KEYS:
        d[k] = consts[k]
    return d


_BUILD_CACHE = {}


def _get_nc():
    if "nc" not in _BUILD_CACHE:
        b = Build(upto="full2")
        _BUILD_CACHE["nc"] = b.build()
        _BUILD_CACHE["consts"] = host_constants()
    return _BUILD_CACHE["nc"], _BUILD_CACHE["consts"]


def kernel(**inputs):
    from concourse.bass_utils import run_bass_kernel_spmd

    g = {k: np.asarray(v) for k, v in inputs.items()}
    nc, consts = _get_nc()
    in_maps = []
    for c in range(8):
        d = stage_inputs_core(
            g["L0_attn_x"][c], g["L0_attn_w_qkv"], g["L0_attn_w_o"],
            g["L0_attn_log_xi"], consts, g["L0_ffn_W_fc"], g["L0_ffn_W_proj"],
            wq1=g["L1_attn_WQ"], wk1=g["L1_attn_WK"], wv1=g["L1_attn_WV"],
            g=g, core=c)
        in_maps.append(d)
    res = run_bass_kernel_spmd(nc, in_maps, core_ids=list(range(8)))
    out = np.stack([np.ascontiguousarray(res.results[c]["out"].T) for c in range(8)], axis=0)
    return out.astype(np.float32)
